# version 03
# baseline (speedup 1.0000x reference)
import numpy as np
import ml_dtypes
from contextlib import ExitStack

import concourse.bass as bass
import concourse.mybir as mybir
from concourse.bass_utils import run_bass_kernel_spmd

F32 = mybir.dt.float32
BF16 = mybir.dt.bfloat16
U32 = mybir.dt.uint32
I16 = mybir.dt.int16
AX = mybir.AxisListType
ALU = mybir.AluOpType
AF = mybir.ActivationFunctionType

E = 1024
T = 4096
B = 4
NH = 8
DH = 128
NB = 64
G = 32
NHEADS_CORE = 4
ROWL = 400
VOFF = 256
ONEOFF = 384
IDHI = 385
IDLO = 386
SBROWS = 7 * T + 4224
RSQD = float(DH) ** -0.5
P1B_TILES = G
P1B_HASH = True
P1B_PARTS = 9
ATT_HASHES = NH
ATT_PARTS = 9
NHEADS_RUN = NHEADS_CORE
P3_PARTS = 9


class Ctx:

    ENGS = ("sync", "scalar", "vector", "tensor", "gpsimd")

    def __init__(self, nc, st):
        self.nc = nc
        self.st = st
        self.steps = {e: [] for e in self.ENGS}
        self._semcnt = {}
        self._sems = {}
        self._roles = {}
        self.last_act_psum = 0
        self.last_dve_psum = 0
        self._parity_reg = {}

    def setup_parity(self, eng):
        cell = {}
        self._parity_reg[eng] = cell

        def go(e, cell=cell):
            pid = e.partition_id()
            pr = e.alloc_register()
            e.reg_mod(pr, pid, 2)
            cell["reg"] = pr
        self.emit(eng, go)

    def parity_reg(self, eng):
        return self._parity_reg[eng]

    def sem(self, name):
        if name not in self._sems:
            self._sems[name] = self.st.enter_context(self.nc.semaphore(name))
            self._semcnt[name] = 0
        return self._sems[name]

    def cnt(self, name):
        return self._semcnt[name]

    def bump(self, name, n):
        self._semcnt[name] += n
        return self._semcnt[name]

    def emit(self, eng, fn):
        self.steps[eng].append(fn)

    def wait(self, eng, sem_name, val):
        h = self.sem(sem_name)
        self.emit(eng, lambda e, h=h, v=val: e.wait_ge(h, v))

    def op(self, eng, fn, inc=None, n=1, psum=False):
        if psum:
            assert inc is not None
            if eng == "scalar" and self.last_dve_psum:
                self.wait(eng, "ve", self.last_dve_psum)
            if eng == "vector" and self.last_act_psum:
                self.wait(eng, "ac", self.last_act_psum)
        if inc is None:
            self.emit(eng, fn)
            return None
        h = self.sem(inc)
        v = self.bump(inc, n)
        self.emit(eng, lambda e, fn=fn, h=h, n=n: fn(e).then_inc(h, n))
        if psum:
            if eng == "scalar":
                self.last_act_psum = v
            elif eng == "vector":
                self.last_dve_psum = v
        return v

    NROT = 6

    def dma(self, eng, out_ap, in_ap, role, **kw):
        idx = self._roles.get(role, 0)
        self._roles[role] = (idx + 1) % self.NROT
        name = "%s_%d" % (role, idx)
        h = self.sem(name)
        prev = self._semcnt[name]
        if prev:
            self.emit(eng, lambda e, h=h, v=prev: e.wait_ge(h, v))
        v = self.bump(name, 16)
        self.emit(eng, lambda e, o=out_ap, i=in_ap, kw=kw, h=h: e.dma_start(out=o, in_=i, **kw).then_inc(h, 16))
        return (name, v)

    def waitt(self, eng, token):
        name, v = token
        self.wait(eng, name, v)

    def wait_role_all(self, eng, role):
        for idx in range(self.NROT):
            name = "%s_%d" % (role, idx)
            if name in self._semcnt and self._semcnt[name]:
                self.wait(eng, name, self._semcnt[name])

    def finalize(self, block):
        for name in self.ENGS:
            steps = self.steps[name]
            deco = getattr(block, name)

            def make(steps):
                def body(e):
                    for f in steps:
                        f(e)
                return body
            deco(make(steps))



def host_consts():
    c = {}
    c["slt128"] = (np.arange(128)[:, None] < np.arange(128)[None, :]).astype(ml_dtypes.bfloat16)
    c["slt32"] = (np.arange(32)[:, None] < np.arange(32)[None, :]).astype(np.float32)
    c["iota64"] = np.tile(np.arange(NB, dtype=np.float32)[None, :], (128, G))
    c["ones128b"] = np.ones((128, 1), dtype=ml_dtypes.bfloat16)
    c["ones128f"] = np.ones((128, 1), dtype=np.float32)
    c["ones32f"] = np.ones((32, 1), dtype=np.float32)
    c["ones1x128f"] = np.ones((1, 128), dtype=np.float32)
    c["ones1x128b"] = np.ones((1, 128), dtype=ml_dtypes.bfloat16)
    c["zeros1x64"] = np.zeros((1, 64), dtype=np.float32)
    c["epscol"] = np.full((128, 1), 1e-5, dtype=np.float32)
    c["ident128"] = np.eye(128, dtype=ml_dtypes.bfloat16)
    c["offs8"] = np.tile((np.arange(NH, dtype=np.float32) * 32)[None, :], (128, 1))
    pm = np.ones((128, 8, 64), dtype=ml_dtypes.bfloat16)
    for j in range(8):
        for i in range(64):
            pm[64 + i, j, i] = 0
    c["pmask8"] = pm.reshape(128, 512)
    tt = np.arange(T).reshape(G, 128).T
    c["idhi"] = (tt // 64).astype(np.float32)
    c["idlo"] = (tt % 64).astype(np.float32)
    return c


CONST_DTYPES = {
    "slt128": BF16, "slt32": F32, "iota64": F32, "ones128b": BF16,
    "ones128f": F32, "ones32f": F32, "ones1x128f": F32, "ones1x128b": BF16,
    "zeros1x64": F32, "epscol": F32, "ident128": BF16, "offs8": F32, "pmask8": BF16, "idhi": F32, "idlo": F32,
}




def build_program(debug=(), stage=5):
    nc = bass.Bass("TRN2", target_bir_lowering=False)

    ap = {}
    ap["xT"] = nc.dram_tensor("xT", [E, T], F32, kind="ExternalInput").ap()
    ap["wffnT"] = nc.dram_tensor("wffnT", [E, E], F32, kind="ExternalInput").ap()
    ap["wqkvT"] = nc.dram_tensor("wqkvT", [E, E], F32, kind="ExternalInput").ap()
    ap["woutT_own"] = nc.dram_tensor("woutT_own", [512, E], F32, kind="ExternalInput").ap()
    ap["woutT_sibA"] = nc.dram_tensor("woutT_sibA", [512, E], F32, kind="ExternalInput").ap()
    ap["woutT_sibB"] = nc.dram_tensor("woutT_sibB", [512, E], F32, kind="ExternalInput").ap()
    ap["boutRows"] = nc.dram_tensor("boutRows", [128, 8], F32, kind="ExternalInput").ap()
    ap["fbiasRows"] = nc.dram_tensor("fbiasRows", [128, 8], F32, kind="ExternalInput").ap()
    ap["alpha"] = nc.dram_tensor("alpha", [1, 1], F32, kind="ExternalInput").ap()
    ap["rotT"] = nc.dram_tensor("rotT", [128, 256], F32, kind="ExternalInput").ap()
    shapes = {"slt128": [128, 128], "slt32": [32, 32], "iota64": [128, G * NB],
              "ones128b": [128, 1], "ones128f": [128, 1], "ones32f": [32, 1],
              "ones1x128f": [1, 128], "ones1x128b": [1, 128], "zeros1x64": [1, 64],
              "epscol": [128, 1], "ident128": [128, 128],
              "offs8": [128, 8], "pmask8": [128, 512], "idhi": [128, G], "idlo": [128, G]}
    for k, dt in CONST_DTYPES.items():
        ap[k] = nc.dram_tensor(k, shapes[k], dt, kind="ExternalInput").ap()

    outT = nc.dram_tensor("outT", [E, T // 2], F32, kind="ExternalOutput").ap()
    dbg = {}
    if "x1" in debug:
        dbg["dbg_x1"] = nc.dram_tensor("dbg_x1", [E, T], F32, kind="ExternalOutput").ap()
    if "xa" in debug:
        dbg["dbg_xa"] = nc.dram_tensor("dbg_xa", [E, T], BF16, kind="ExternalOutput").ap()
    if "payload" in debug:
        dbg["dbg_payload"] = nc.dram_tensor("dbg_payload", [128, G * ROWL], BF16, kind="ExternalOutput").ap()
    if "buckets" in debug:
        dbg["dbg_buckets"] = nc.dram_tensor("dbg_buckets", [128, G * NH], F32, kind="ExternalOutput").ap()
    if "pos" in debug:
        dbg["dbg_pos"] = nc.dram_tensor("dbg_pos", [128, G * NH], U32, kind="ExternalOutput").ap()
    if "attn" in debug:
        dbg["dbg_attn"] = nc.dram_tensor("dbg_attn", [128, G * DH], BF16, kind="ExternalOutput").ap()
    if "p3" in debug:
        dbg["dbg_x1r"] = nc.dram_tensor("dbg_x1r", [128, 8 * 512], F32, kind="ExternalOutput").ap()
        dbg["dbg_x2"] = nc.dram_tensor("dbg_x2", [128, 8 * 512], F32, kind="ExternalOutput").ap()
        dbg["dbg_attT"] = nc.dram_tensor("dbg_attT", [128, 8 * 512], BF16, kind="ExternalOutput").ap()
        dbg["dbg_x1h"] = nc.dram_tensor("dbg_x1h", [E, 512], F32, kind="ExternalOutput").ap()

    x1T_h = nc.dram_tensor("x1T_h", [E, T], F32, kind="Internal").ap()
    xaT_h = nc.dram_tensor("xaT_h", [E, T], BF16, kind="Internal").ap()
    sbuf_h = nc.dram_tensor("sbuf_h", [SBROWS, ROWL], BF16, kind="Internal").ap()
    bo_h = nc.dram_tensor("bo_h", [T, 132], F32, kind="Internal").ap()
    ccin_h = nc.dram_tensor("ccin_h", [T // 2, 512], BF16, kind="Internal").ap()
    ccout_h = nc.dram_tensor("ccout_h", [T, 512], BF16, kind="Internal").ap()
    attn_h = nc.dram_tensor("attn_h", [NHEADS_CORE, T, DH], BF16, kind="Internal").ap()
    sort_scr1 = nc.dram_tensor("sort_scr1", [G * NB], F32, kind="Internal").ap()
    sort_scr2 = nc.dram_tensor("sort_scr2", [G * NB], F32, kind="Internal").ap()

    st = ExitStack()
    ctx = Ctx(nc, st)

    def sb(name, shape, dt):
        return st.enter_context(nc.sbuf_tensor(name, shape, dt))

    def psum(name, shape, dt=F32):
        return st.enter_context(nc.psum_tensor(name, shape, dt))

    with st:
        build_body(nc, ctx, ap, outT, dbg, x1T_h, xaT_h, sbuf_h, bo_h, ccin_h, ccout_h,
                   attn_h, sort_scr1, sort_scr2, sb, psum, debug, stage)
        block = st.enter_context(nc.Block())
        ctx.finalize(block)
    return nc


def build_body(nc, ctx, ap, outT, dbg, x1T_h, xaT_h, sbuf_h, bo_h, ccin_h, ccout_h, attn_h,
               sort_scr1, sort_scr2, sb, psum, debug, stage):
    from contextlib import ExitStack

    tern = sb("tern", [128, 8, 1024], BF16)
    wqkv = sb("wqkv", [128, 8, 1024], BF16)
    rotb = sb("rotb", [128, 256], BF16)
    cst = {}
    for k in CONST_DTYPES:
        cst[k] = sb("c_" + k, list(ap[k].shape), CONST_DTYPES[k])
    halfbias = sb("halfbias", [128, 8], F32)
    boutc = sb("boutc", [128, 8], F32)
    halfalpha = sb("halfalpha", [128, 1], F32)
    deltas = sb("deltas", [128, 8], F32)

    qL, qA, qG, pe, ve, ac, qS = "qL", "qA", "qG", "pe", "ve", "ac", "qS"
    ctx.setup_parity("gpsimd")

    def barrier(engines=("sync", "scalar", "vector", "tensor", "gpsimd")):
        for sem in (qG, pe, ve, ac):
            v = ctx.cnt(sem) if sem in ctx._semcnt else 0
            if v:
                for e in engines:
                    ctx.wait(e, sem, v)
        for role in (qL, qA, qS, "qDg"):
            for e in engines:
                ctx.wait_role_all(e, role)

    with ExitStack() as g1:
        def sb1(name, shape, dt):
            return g1.enter_context(nc.sbuf_tensor(name, shape, dt))

        def ps1(name, shape, dt=F32):
            return g1.enter_context(nc.psum_tensor(name, shape, dt))

        wf32 = sb1("wf32", [128, 8, 1024], F32)
        tpos = sb1("tpos", [128, 8, 1024], BF16)
        rotf = sb1("rotf", [128, 256], F32)
        xslab = sb1("xslab", [128, 8, 512], F32)
        sq = sb1("sq", [128, 8, 512], F32)
        xln = sb1("xln", [128, 8, 512], BF16)
        x1slab = sb1("x1slab", [128, 8, 512], F32)
        rowstat = sb1("rowstat", [1, 8 * 512], F32)
        evac = sb1("evac", [128, 8, 512], F32)
        PS_STAT = ps1("ps_stat", [1, 1024])
        PS_REP = ps1("ps_rep", [128, 512])
        PS_REP2 = ps1("ps_rep2", [128, 512])
        PS_F = ps1("ps_f", [128, 1024])
        PS_M = ps1("ps_m", [128, 512])

        for k in CONST_DTYPES:
            ctx.dma("sync", cst[k][:], ap[k][:], qL)
        ctx.dma("sync", rotf[:], ap["rotT"][:], qL)
        ctx.dma("sync", halfbias[:], ap["fbiasRows"][:], qL)
        ctx.dma("sync", boutc[:], ap["boutRows"][:], qL)
        ctx.dma("sync", wf32[:], ap["wffnT"].rearrange("(c p) o -> p c o", p=128), qL)

        ctx.wait_role_all("vector", qL)
        ctx.op("vector", lambda e: e.tensor_copy(rotb[:], rotf[:]), inc=ve)
        ctx.op("vector", lambda e: e.tensor_scalar(halfbias[:], halfbias[:], 0.5, None, op0=ALU.mult), inc=ve)

        v_abs = ctx.op("vector", lambda e: e.tensor_reduce(
            out=deltas[:, 0:1].rearrange("p (a o) -> p a o", a=1),
            in_=wf32[:].rearrange("p c o -> p (c o)").rearrange("p (a n) -> p a n", a=1),
            axis=AX.X, op=ALU.add, apply_absolute_value=True), inc=ve)
        ctx.wait("tensor", ve, v_abs)
        p_sum = ctx.op("tensor", lambda e: e.matmul(PS_M[0:1, 0:1], deltas[:, 0:1], cst["ones128f"][:],
                                                    start=True, stop=True), inc=pe)
        ctx.wait("vector", pe, p_sum)
        v_del = ctx.op("vector", lambda e: e.tensor_scalar(deltas[0:1, 1:2], PS_M[0:1, 0:1],
                                                           0.7 / (E * E), None, op0=ALU.mult), inc=ve, psum=True)
        ctx.wait("tensor", ve, v_del)
        p_drep = ctx.op("tensor", lambda e: e.matmul(PS_M[:, 1:2], cst["ones1x128f"][:], deltas[0:1, 1:2],
                                                     start=True, stop=True), inc=pe)
        ctx.wait("vector", pe, p_drep)
        v_dc = ctx.op("vector", lambda e: e.tensor_copy(deltas[:, 0:1], PS_M[:, 1:2]), inc=ve, psum=True)
        ctx.wait("vector", ve, v_dc)
        v_dn = ctx.op("vector", lambda e: e.tensor_scalar(deltas[:, 1:2], deltas[:, 0:1], -1.0, None, op0=ALU.mult), inc=ve)
        ctx.wait("vector", ve, v_dn)
        v_t1 = ctx.op("vector", lambda e: e.tensor_scalar(tpos[:], wf32[:], deltas[:, 0:1], None, op0=ALU.is_gt), inc=ve)
        v_t2 = ctx.op("vector", lambda e: e.tensor_scalar(tern[:], wf32[:], deltas[:, 1:2], None, op0=ALU.is_lt), inc=ve)
        ctx.wait("vector", ve, v_t2)
        v_t3 = ctx.op("vector", lambda e: e.tensor_tensor(tern[:], tpos[:], tern[:], ALU.subtract), inc=ve)

        ctx.wait("sync", ve, v_t3)
        l_wq = ctx.dma("sync", wf32[:], ap["wqkvT"].rearrange("(c p) o -> p c o", p=128), qL)
        ctx.waitt("vector", l_wq)
        v_wq = ctx.op("vector", lambda e: e.tensor_copy(wqkv[:], wf32[:]), inc=ve)
        l_al = ctx.dma("sync", deltas[0:1, 3:4], ap["alpha"][:], qL)
        ctx.waitt("tensor", l_al)
        ctx.wait("tensor", ve, v_wq)
        p_arep = ctx.op("tensor", lambda e: e.matmul(PS_M[:, 2:3], cst["ones1x128f"][:], deltas[0:1, 3:4],
                                                     start=True, stop=True), inc=pe)
        ctx.wait("vector", pe, p_arep)
        ctx.op("vector", lambda e: e.tensor_scalar(halfalpha[:], PS_M[:, 2:3], 0.5, None, op0=ALU.mult), inc=ve, psum=True)

        def ln_rows(src_slab, slab_cols):
            c0 = ctx.cnt(ve)
            ctx.wait("scalar", ve, c0)
            ctx.wait("tensor", ve, c0)
            a_sq = ctx.op("scalar", lambda e: e.activation(out=sq[:, :, 0:slab_cols], in_=src_slab,
                                                           func=AF.Square), inc=ac)
            for fc in range(8):
                ctx.op("tensor", lambda e, fc=fc: e.matmul(
                    PS_STAT[0:1, 0:slab_cols], cst["ones128f"][:], src_slab[:, fc, 0:slab_cols],
                    start=(fc == 0), stop=(fc == 7)), inc=pe if fc == 7 else None)
            ctx.wait("tensor", ac, a_sq)
            for fc in range(8):
                ctx.op("tensor", lambda e, fc=fc: e.matmul(
                    PS_STAT[0:1, 512:512 + slab_cols], cst["ones128f"][:], sq[:, fc, 0:slab_cols],
                    start=(fc == 0), stop=(fc == 7)), inc=pe if fc == 7 else None)
            p_done = ctx.cnt(pe)
            ctx.wait("vector", pe, p_done)
            R = rowstat[:].rearrange("a (k n) -> a k n", n=512)
            v1 = ctx.op("vector", lambda e: e.tensor_scalar(R[:, 0, 0:slab_cols], PS_STAT[0:1, 0:slab_cols],
                                                            1.0 / E, None, op0=ALU.mult), inc=ve, psum=True)
            v2 = ctx.op("vector", lambda e: e.tensor_scalar(R[:, 1, 0:slab_cols], PS_STAT[0:1, 512:512 + slab_cols],
                                                            1.0 / E, None, op0=ALU.mult), inc=ve, psum=True)
            ctx.wait("vector", ve, v2)
            v3 = ctx.op("vector", lambda e: e.tensor_tensor(R[:, 2, 0:slab_cols], R[:, 0, 0:slab_cols],
                                                            R[:, 0, 0:slab_cols], ALU.mult), inc=ve)
            ctx.wait("vector", ve, v3)
            v4 = ctx.op("vector", lambda e: e.tensor_tensor(R[:, 2, 0:slab_cols], R[:, 1, 0:slab_cols],
                                                            R[:, 2, 0:slab_cols], ALU.subtract), inc=ve)
            ctx.wait("scalar", ve, v4)
            a1 = ctx.op("scalar", lambda e: e.activation(out=R[:, 3, 0:slab_cols], in_=R[:, 2, 0:slab_cols],
                                                         func=AF.Ln, bias=cst["epscol"][0:1, 0:1]), inc=ac)
            ctx.wait("scalar", ac, a1)
            a2 = ctx.op("scalar", lambda e: e.activation(out=R[:, 1, 0:slab_cols], in_=R[:, 3, 0:slab_cols],
                                                         func=AF.Exp, scale=-0.5), inc=ac)
            ctx.wait("vector", ac, a2)
            v5 = ctx.op("vector", lambda e: e.tensor_tensor(R[:, 0, 0:slab_cols], R[:, 0, 0:slab_cols],
                                                            R[:, 1, 0:slab_cols], ALU.mult), inc=ve)
            ctx.wait("tensor", ve, v5)
            pa = ctx.op("tensor", lambda e: e.matmul(PS_REP[:, 0:slab_cols], cst["ones1x128f"][:],
                                                     R[0:1, 1, 0:slab_cols], start=True, stop=True), inc=pe)
            pb = ctx.op("tensor", lambda e: e.matmul(PS_REP2[:, 0:slab_cols], cst["ones1x128f"][:],
                                                     R[0:1, 0, 0:slab_cols], start=True, stop=True), inc=pe)
            return pb

        SLAB = 512
        prev_w = []
        for s in range(T // SLAB):
            for tok in prev_w:
                ctx.waitt("vector", tok)
            prev_w = []
            ld = ctx.dma("sync", xslab[:],
                         ap["xT"].rearrange("(c p) t -> p c t", p=128)[:, :, s * SLAB:(s + 1) * SLAB], qL)
            ctx.waitt("scalar", ld)
            ctx.waitt("vector", ld)
            ctx.waitt("tensor", ld)
            pb = ln_rows(xslab[:], SLAB)
            ctx.wait("vector", pe, pb)
            vA = ctx.op("vector", lambda e: e.tensor_tensor(
                sq[:], xslab[:],
                PS_REP[:, 0:SLAB].rearrange("p (c n) -> p c n", c=1).to_broadcast([128, 8, SLAB]),
                ALU.mult), inc=ve, psum=True)
            ctx.wait("vector", ve, vA)
            vB = ctx.op("vector", lambda e: e.tensor_tensor(
                xln[:], sq[:],
                PS_REP2[:, 0:SLAB].rearrange("p (c n) -> p c n", c=1).to_broadcast([128, 8, SLAB]),
                ALU.subtract), inc=ve, psum=True)
            ctx.wait("tensor", ve, vB)
            ev_by_oc = {}
            for oc in range(8):
                slot = oc % 2
                if oc >= 2:
                    ctx.wait("tensor", ve, ev_by_oc[oc - 2])
                for fc in range(8):
                    ctx.op("tensor", lambda e, fc=fc, oc=oc, slot=slot: e.matmul(
                        PS_F[:, slot * 512:slot * 512 + SLAB],
                        tern[:, fc, oc * 128:(oc + 1) * 128], xln[:, fc, :],
                        start=(fc == 0), stop=(fc == 7)), inc=pe if fc == 7 else None)
                pdone = ctx.cnt(pe)
                ctx.wait("vector", pe, pdone)
                ev = ctx.op("vector", lambda e, oc=oc, slot=slot: e.tensor_scalar(
                    evac[:, oc, :], PS_F[:, slot * 512:slot * 512 + SLAB],
                    halfalpha[:, 0:1], halfbias[:, oc:oc + 1], op0=ALU.mult, op1=ALU.add), inc=ve, psum=True)
                ev_by_oc[oc] = ev
                ctx.wait("vector", ve, ev)
                ctx.op("vector", lambda e, oc=oc: e.tensor_tensor(
                    x1slab[:, oc, :], xslab[:, oc, :], evac[:, oc, :], ALU.add), inc=ve)
            x1done = ctx.cnt(ve)
            ctx.wait("sync", ve, x1done)
            prev_w.append(ctx.dma("sync", x1T_h.rearrange("(c p) t -> p c t", p=128)[:, :, s * SLAB:(s + 1) * SLAB], x1slab[:], qS))
            if "dbg_x1" in dbg:
                prev_w.append(ctx.dma("sync", dbg["dbg_x1"].rearrange("(c p) t -> p c t", p=128)[:, :, s * SLAB:(s + 1) * SLAB], x1slab[:], qS))
            pb2 = ln_rows(x1slab[:], SLAB)
            ctx.wait("vector", pe, pb2)
            vA2 = ctx.op("vector", lambda e: e.tensor_tensor(
                sq[:], x1slab[:],
                PS_REP[:, 0:SLAB].rearrange("p (c n) -> p c n", c=1).to_broadcast([128, 8, SLAB]),
                ALU.mult), inc=ve, psum=True)
            ctx.wait("vector", ve, vA2)
            vB2 = ctx.op("vector", lambda e: e.tensor_tensor(
                xln[:], sq[:],
                PS_REP2[:, 0:SLAB].rearrange("p (c n) -> p c n", c=1).to_broadcast([128, 8, SLAB]),
                ALU.subtract), inc=ve, psum=True)
            ctx.wait("sync", ve, vB2)
            prev_w.append(ctx.dma("sync", xaT_h.rearrange("(c p) t -> p c t", p=128)[:, :, s * SLAB:(s + 1) * SLAB], xln[:], qS))
            if "dbg_xa" in dbg:
                prev_w.append(ctx.dma("sync", dbg["dbg_xa"].rearrange("(c p) t -> p c t", p=128)[:, :, s * SLAB:(s + 1) * SLAB], xln[:], qS))
            ctx.wait("sync", ve, ctx.cnt(ve))
        barrier()

    if stage < 2:
        finish_zero(nc, ctx, outT, sb)
        return

    with ExitStack() as g2:
        def sb2(name, shape, dt):
            return g2.enter_context(nc.sbuf_tensor(name, shape, dt))

        def ps2(name, shape, dt=F32):
            return g2.enter_context(nc.psum_tensor(name, shape, dt))

        payload = sb2("payload", [128, G, ROWL], BF16)
        buckets = sb2("buckets", [128, G, NH], F32)
        xat = sb2("xat", [128, 8, 2 * 128], BF16)
        qkTt = sb2("qkTt", [128, 2 * 128], BF16)
        rnorm = sb2("rnorm", [128, 8], F32)
        amax = sb2("amax", [128, 48], F32)
        aidx = sb2("aidx", [128, 16], U32)
        acond = sb2("acond", [128, 8], U32)
        onehot = sb2("onehot", [128, G * NB], BF16)
        ttrs = sb2("ttrs", [128, G * NB], F32)
        crow = sb2("crow", [1, G * NB], F32)
        ctT = sb2("ctT", [32, NB], F32)
        srow = sb2("srow", [1, 4 * NB], F32)
        ccs_row = sb2("ccs_row", [1, G * NB], F32)
        posf = sb2("posf", [128, G], F32)
        posfw = sb2("posfw", [128, G, NH], U32)
        posbo = sb2("posbo", [128, G, NH], U32)
        qkT_s = sb2("qkT_s", [128, 4224], BF16)
        knT_s = sb2("knT_s", [128, 4224], BF16)
        vt = sb2("vt", [128, 33, 132], BF16)
        vt2 = sb2("vt2", [128, 32, 132], BF16)
        PT = sb2("PT", [128, 512], BF16)
        PT2 = sb2("PT2", [128, 512], BF16)
        qidr = sb2("qidr", [1, 128], BF16)
        bmask = sb2("bmask", [128, 64], BF16)
        bm1 = sb2("bm1", [128, 64], F32)
        bm2 = sb2("bm2", [128, 64], F32)
        staging = sb2("staging", [128, G, 132], F32)
        gathered = sb2("gathered", [128, G, 132], BF16)
        acc = sb2("acc", [128, G, 132], F32)
        attnb = sb2("attnb", [128, G, DH], BF16)
        rcp = sb2("rcp", [128, G], F32)
        PA = ps2("pa", [128, 2048])
        PB = ps2("pb", [128, 1024])
        PC = ps2("pc", [128, 512])
        PD = ps2("pd", [128, 256], BF16)

        vz = ctx.op("vector", lambda e: e.memset(staging[:, :, 129:132], 0.0), inc=ve)
        ctx.op("vector", lambda e: e.memset(payload[:, :, IDLO + 1:ROWL], 0.0), inc=ve)
        ctx.op("vector", lambda e: e.memset(payload[:, :, ONEOFF:ONEOFF + 1], 1.0), inc=ve)
        ctx.op("vector", lambda e: e.tensor_copy(payload[:, :, IDHI:IDHI + 1],
                                                 cst["idhi"][:].rearrange("p (g o) -> p g o", o=1)), inc=ve)
        vlast = ctx.op("vector", lambda e: e.tensor_copy(payload[:, :, IDLO:IDLO + 1],
                                                         cst["idlo"][:].rearrange("p (g o) -> p g o", o=1)), inc=ve)

        def build_head_payload(hd):
            xaT_t = xaT_h.rearrange("(c p) t -> p c t", p=128)
            war_ve, war_ac, war_tr, war_rot = {}, {}, {}, {}
            for g in range(P1B_TILES):
                dbuf = g % 2
                if g >= 2:
                    ctx.wait("sync", ve, war_rot[g - 2])
                ld = ctx.dma("sync", xat[:, :, dbuf * 128:(dbuf + 1) * 128],
                             xaT_t[:, :, g * 128:(g + 1) * 128], qA)
                ctx.waitt("tensor", ld)
                slot = g % 2
                if g >= 2:
                    ctx.wait("tensor", ve, war_ve[g - 2])
                    ctx.wait("tensor", ac, war_ac[g - 2])
                for fc in range(8):
                    ctx.op("tensor", lambda e, fc=fc, slot=slot, dbuf=dbuf, hd=hd: e.matmul(
                        PB[:, slot * 256:slot * 256 + 128],
                        xat[:, fc, dbuf * 128:(dbuf + 1) * 128],
                        wqkv[:, fc, hd * 128:(hd + 1) * 128],
                        start=(fc == 0), stop=(fc == 7)), inc=None)
                for fc in range(8):
                    ctx.op("tensor", lambda e, fc=fc, slot=slot, dbuf=dbuf, hd=hd: e.matmul(
                        PB[:, slot * 256 + 128:slot * 256 + 256],
                        xat[:, fc, dbuf * 128:(dbuf + 1) * 128],
                        wqkv[:, fc, 512 + hd * 128:512 + (hd + 1) * 128],
                        start=(fc == 0), stop=(fc == 7)), inc=pe if fc == 7 else None)
                pq = ctx.cnt(pe)
                if P1B_PARTS < 2:
                    war_ve[g] = ctx.cnt(ve); war_ac[g] = ctx.cnt(ac)
                    war_tr[g] = war_ve[g]; war_rot[g] = war_ve[g]
                    continue
                ctx.wait("scalar", pe, pq)
                aq = ctx.op("scalar", lambda e, g=g, slot=slot: e.activation(
                    out=payload[:, g, 0:128], in_=PB[:, slot * 256:slot * 256 + 128],
                    func=AF.Copy), inc=ac, psum=True)
                if P1B_PARTS == 2:
                    war_ve[g] = ctx.cnt(ve); war_ac[g] = aq
                    war_tr[g] = war_ve[g]; war_rot[g] = war_ve[g]
                    continue
                ctx.wait("vector", pe, pq)
                vv = ctx.op("vector", lambda e, g=g, slot=slot: e.tensor_copy(
                    payload[:, g, VOFF:VOFF + 128], PB[:, slot * 256 + 128:slot * 256 + 256]), inc=ve, psum=True)
                if P1B_PARTS < 4:
                    war_ve[g] = vv; war_ac[g] = aq
                    war_tr[g] = vv; war_rot[g] = vv
                    continue
                ctx.wait("vector", ac, aq)
                v_sq = ctx.op("vector", lambda e, g=g: e.tensor_tensor(
                    ttrs[:, 0:128], payload[:, g, 0:128], payload[:, g, 0:128], ALU.mult), inc=ve)
                ctx.wait("vector", ve, v_sq)
                v_ss = ctx.op("vector", lambda e: e.tensor_reduce(
                    out=rnorm[:, 0:1].rearrange("p (a o) -> p a o", a=1),
                    in_=ttrs[:, 0:128].rearrange("p (a n) -> p a n", a=1),
                    axis=AX.X, op=ALU.add), inc=ve)
                ctx.wait("vector", ve, v_ss)
                v_cl = ctx.op("vector", lambda e: e.tensor_scalar(
                    rnorm[:, 1:2], rnorm[:, 0:1], 1e-24, None, op0=ALU.max), inc=ve)
                ctx.wait("scalar", ve, v_cl)
                a_ln = ctx.op("scalar", lambda e: e.activation(out=rnorm[:, 2:3], in_=rnorm[:, 1:2],
                                                               func=AF.Ln), inc=ac)
                ctx.wait("scalar", ac, a_ln)
                a_ex = ctx.op("scalar", lambda e: e.activation(out=rnorm[:, 3:4], in_=rnorm[:, 2:3],
                                                               func=AF.Exp, scale=-0.5), inc=ac)
                ctx.wait("vector", ac, a_ex)
                v_n1 = ctx.op("vector", lambda e: e.tensor_tensor(rnorm[:, 4:5], rnorm[:, 3:4], rnorm[:, 3:4], ALU.mult), inc=ve)
                ctx.wait("vector", ve, v_n1)
                v_n2 = ctx.op("vector", lambda e: e.tensor_tensor(rnorm[:, 4:5], rnorm[:, 4:5], rnorm[:, 1:2], ALU.mult), inc=ve)
                ctx.wait("vector", ve, v_n2)
                v_n3 = ctx.op("vector", lambda e: e.tensor_scalar(rnorm[:, 4:5], rnorm[:, 4:5], -0.5, 1.5, op0=ALU.mult, op1=ALU.add), inc=ve)
                ctx.wait("vector", ve, v_n3)
                v_n4 = ctx.op("vector", lambda e: e.tensor_tensor(rnorm[:, 5:6], rnorm[:, 3:4], rnorm[:, 4:5], ALU.mult), inc=ve)
                ctx.wait("vector", ve, v_n4)
                if P1B_PARTS < 5:
                    war_ve[g] = v_n4; war_ac[g] = aq
                    war_tr[g] = v_n4; war_rot[g] = v_n4
                    continue
                v_kn = ctx.op("vector", lambda e, g=g: e.tensor_scalar(
                    payload[:, g, 128:256], payload[:, g, 0:128], rnorm[:, 5:6], None, op0=ALU.mult), inc=ve)
                if not P1B_HASH:
                    war_ve[g] = v_kn
                    war_ac[g] = aq
                    war_tr[g] = v_kn
                    war_rot[g] = v_kn
                    continue
                tslot = g % 2
                if g >= 2:
                    ctx.wait("tensor", ve, war_tr[g - 2])
                ctx.wait("tensor", ac, aq)
                p_tr = ctx.op("tensor", lambda e, g=g, tslot=tslot: e.transpose(
                    PD[:, tslot * 128:(tslot + 1) * 128], payload[:, g, 0:128], cst["ident128"][:]), inc=pe)
                ctx.wait("vector", pe, p_tr)
                v_tr = ctx.op("vector", lambda e, tslot=tslot: e.tensor_copy(
                    qkTt[:, tslot * 128:(tslot + 1) * 128], PD[:, tslot * 128:(tslot + 1) * 128]), inc=ve, psum=True)
                war_tr[g] = v_tr
                ctx.wait("tensor", ve, v_tr)
                if g >= 2:
                    ctx.wait("tensor", ve, war_rot[g - 2])
                p_rot = ctx.op("tensor", lambda e, tslot=tslot: e.matmul(
                    PA[:, tslot * 256:(tslot + 1) * 256], qkTt[:, tslot * 128:(tslot + 1) * 128],
                    rotb[:], start=True, stop=True), inc=pe)
                ctx.wait("vector", pe, p_rot)
                R3 = PA[:, tslot * 256:(tslot + 1) * 256].rearrange("p (h r) -> p h r", r=32)
                v1 = ctx.op("vector", lambda e, R3=R3: e.tensor_reduce(
                    out=amax[:, 0:8].rearrange("p (h o) -> p h o", o=1), in_=R3, axis=AX.X, op=ALU.max), inc=ve, psum=True)
                v2 = ctx.op("vector", lambda e, R3=R3: e.tensor_reduce(
                    out=amax[:, 8:16].rearrange("p (h o) -> p h o", o=1), in_=R3, axis=AX.X, op=ALU.min), inc=ve, psum=True)
                ctx.wait("vector", ve, v2)
                v3 = ctx.op("vector", lambda e, tslot=tslot: e.max_index(
                    aidx[:, 0:8], amax[:, 0:8], PA[:, tslot * 256:(tslot + 1) * 256]), inc=ve, psum=True)
                v4 = ctx.op("vector", lambda e, tslot=tslot: e.max_index(
                    aidx[:, 8:16], amax[:, 8:16], PA[:, tslot * 256:(tslot + 1) * 256]), inc=ve, psum=True)
                war_rot[g] = v4
                ctx.wait("vector", ve, v4)
                ctx.op("vector", lambda e: e.tensor_copy(amax[:, 16:24], aidx[:, 0:8]), inc=ve)
                ctx.op("vector", lambda e: e.tensor_copy(amax[:, 24:32], aidx[:, 8:16]), inc=ve)
                v7 = ctx.op("vector", lambda e: e.tensor_tensor(amax[:, 32:40], amax[:, 0:8], amax[:, 8:16], ALU.add), inc=ve)
                ctx.wait("vector", ve, v7)
                v8 = ctx.op("vector", lambda e: e.tensor_scalar(acond[:, 0:8], amax[:, 32:40], 0.0, None, op0=ALU.is_ge), inc=ve)
                v9 = ctx.op("vector", lambda e: e.tensor_scalar(amax[:, 24:32], amax[:, 24:32], 32.0, None, op0=ALU.add), inc=ve)
                ctx.wait("vector", ve, v9)
                v10 = ctx.op("vector", lambda e, g=g: e.tensor_copy(buckets[:, g, :], amax[:, 24:32]), inc=ve)
                ctx.wait("vector", ve, v10)
                v11 = ctx.op("vector", lambda e, g=g: e.copy_predicated(buckets[:, g, :], acond[:, 0:8], amax[:, 16:24]), inc=ve)
                ctx.wait("vector", ve, v11)
                v12 = ctx.op("vector", lambda e, g=g: e.tensor_tensor(buckets[:, g, :], buckets[:, g, :], cst["offs8"][:], ALU.subtract), inc=ve)
                ctx.wait("vector", ve, v12)
                v13 = ctx.op("vector", lambda e, g=g: e.tensor_scalar(buckets[:, g, :], buckets[:, g, :], 0.0, 63.0, op0=ALU.max, op1=ALU.min), inc=ve)
                war_ve[g] = v13
                war_ac[g] = aq

        def sort_hash(hd, h):
            ctx.wait("vector", pe, ctx.cnt(pe))
            ctx.wait("vector", ve, ctx.cnt(ve))
            ctx.wait("vector", ac, ctx.cnt(ac))
            buck = buckets[:, :, h:h + 1].rearrange("p g o -> p (g o)")
            v_oh = ctx.op("vector", lambda e: e.tensor_tensor(
                onehot[:].rearrange("p (g k) -> p g k", k=NB), cst["iota64"][:].rearrange("p (g k) -> p g k", k=NB),
                buckets[:, :, h:h + 1].to_broadcast([128, G, NB]), ALU.is_equal), inc=ve)
            ctx.wait("tensor", ve, v_oh)
            for i in range(4):
                mm = ctx.op("tensor", lambda e, i=i: e.matmul(
                    PA[0:1, i * 512:(i + 1) * 512], cst["ones128b"][:], onehot[:, i * 512:(i + 1) * 512],
                    start=True, stop=True), inc=pe if i == 3 else None)
            ctx.wait("vector", pe, ctx.cnt(pe))
            v_cr = ctx.op("vector", lambda e: e.tensor_copy(crow[:], PA[0:1, :]), inc=ve, psum=True)
            ctx.wait("gpsimd", ve, v_cr)
            d1 = ctx.dma("gpsimd", sort_scr1.rearrange("(o n) -> o n", o=1), crow[:], "qDg")
            ctx.waitt("gpsimd", d1)
            d2 = ctx.dma("gpsimd", ctT[:], sort_scr1.rearrange("(g k) -> g k", g=G), "qDg")
            ctx.waitt("tensor", d2)
            p_cc = ctx.op("tensor", lambda e: e.matmul(PB[0:32, 0:NB], cst["slt32"][:], ctT[:],
                                                       start=True, stop=False), inc=pe)
            p_tot = ctx.op("tensor", lambda e: e.matmul(PB[0:1, 512:512 + NB], cst["ones32f"][:], ctT[:],
                                                        start=True, stop=True), inc=pe)
            ctx.wait("vector", pe, p_tot)
            v_tr = ctx.op("vector", lambda e: e.tensor_copy(srow[:, 0:NB], PB[0:1, 512:512 + NB]), inc=ve, psum=True)
            ctx.wait("vector", ve, v_tr)
            v_sc = ctx.op("vector", lambda e: e.tensor_tensor_scan(
                srow[:, NB:2 * NB], srow[:, 0:NB], cst["zeros1x64"][:], 0.0, ALU.add, ALU.add), inc=ve)
            ctx.wait("vector", ve, v_sc)
            v_st = ctx.op("vector", lambda e: e.tensor_tensor(
                srow[:, 2 * NB:3 * NB], srow[:, NB:2 * NB], srow[:, 0:NB], ALU.subtract), inc=ve)
            ctx.wait("tensor", ve, v_st)
            p_add = ctx.op("tensor", lambda e: e.matmul(PB[0:32, 0:NB], cst["ones1x128f"][:, 0:32],
                                                        srow[0:1, 2 * NB:3 * NB], start=False, stop=True), inc=pe)
            ctx.wait("vector", pe, p_add)
            v_ccs = ctx.op("vector", lambda e: e.tensor_copy(ctT[:], PB[0:32, 0:NB]), inc=ve, psum=True)
            ctx.wait("gpsimd", ve, v_ccs)
            d3 = ctx.dma("gpsimd", sort_scr2.rearrange("(g k) -> g k", g=G), ctT[:], "qDg")
            ctx.waitt("gpsimd", d3)
            d4 = ctx.dma("gpsimd", ccs_row[:], sort_scr2.rearrange("(o n) -> o n", o=1), "qDg")
            ctx.waitt("tensor", d4)
            for i in range(4):
                ctx.op("tensor", lambda e, i=i: e.matmul(
                    PA[:, i * 512:(i + 1) * 512], cst["ones1x128f"][:], ccs_row[:, i * 512:(i + 1) * 512],
                    start=True, stop=False), inc=None)
            for i in range(4):
                ctx.op("tensor", lambda e, i=i: e.matmul(
                    PA[:, i * 512:(i + 1) * 512], cst["slt128"][:], onehot[:, i * 512:(i + 1) * 512],
                    start=False, stop=True), inc=pe if i == 3 else None)
            p_r = ctx.cnt(pe)
            ctx.wait("vector", pe, p_r)
            v_mu = ctx.op("vector", lambda e: e.tensor_tensor(ttrs[:], onehot[:], PA[:], ALU.mult), inc=ve, psum=True)
            ctx.wait("vector", ve, v_mu)
            v_ps = ctx.op("vector", lambda e: e.tensor_reduce(
                out=posf[:].rearrange("p (g o) -> p g o", o=1),
                in_=ttrs[:].rearrange("p (g k) -> p g k", k=NB), axis=AX.X, op=ALU.add), inc=ve)
            ctx.wait("vector", ve, v_ps)
            v_pf = ctx.op("vector", lambda e, h=h: e.tensor_scalar(
                posfw[:, :, h], posf[:], float(64 + h * T), None, op0=ALU.add), inc=ve)
            v_pb = ctx.op("vector", lambda e, h=h: e.tensor_scalar(
                posbo[:, :, h], posf[:], 0.0, None, op0=ALU.add), inc=ve)
            return ctx.cnt(ve)

        def scatter_hash(hd, h):
            for g in range(G):
                ctx.op("gpsimd", lambda e, g=g, h=h: e.indirect_dma_start(
                    out=sbuf_h[:],
                    out_offset=bass.IndirectOffsetOnAxis(posfw[:, g, h:h + 1], 0),
                    in_=payload[:, g, :], in_offset=None), inc=qG, n=16)

        beq = sb2("beq", [128, 128], F32)
        kid = sb2("kid", [128, 2], F32)
        attend_state = {"acc_ve": 0, "bo_q": None, "bo_q1": None, "wrap_q": None, "scat_q": 0}

        def emit_attend(hd, h):
            base = h * T
            ctx.wait("sync", pe, ctx.cnt(pe))
            ctx.wait("scalar", pe, ctx.cnt(pe))
            ctx.wait("scalar", qG, attend_state["scat_q"])
            if h == 0:
                ctx.waitt("sync", attend_state["wrap_q"])
                ctx.waitt("scalar", attend_state["wrap_q"])
            r1 = ctx.dma("sync", qkT_s[:], sbuf_h[base:base + 4224, 0:128], qL, transpose=True)
            r2 = ctx.dma("sync", knT_s[:], sbuf_h[base:base + 4224, 128:256], qL, transpose=True)
            r3 = ctx.dma("scalar", vt[:],
                         sbuf_h[base:base + 4224, :].rearrange("(b p) r -> p b r", p=128)[:, :, VOFF:VOFF + 132], qA)
            r3b = ctx.dma("scalar", vt2[:],
                          sbuf_h[base + 64:base + 64 + 4096, :].rearrange("(b p) r -> p b r", p=128)[:, :, VOFF:VOFF + 132], qA)
            r4 = ctx.dma("sync", qidr[0:1, 0:64],
                         sbuf_h[base + 64:base + 128, IDHI:IDHI + 1].rearrange("t c -> c t"), qL,
                         allow_slow_non_contiguous=True)
            r5 = ctx.dma("sync", qidr[0:1, 64:128],
                         sbuf_h[base + 64:base + 128, IDLO:IDLO + 1].rearrange("t c -> c t"), qL,
                         allow_slow_non_contiguous=True)
            if ATT_PARTS < 2:
                return
            ctx.waitt("tensor", r4)
            ctx.waitt("tensor", r5)
            p_qh = ctx.op("tensor", lambda e: e.matmul(PB[:, 896:960], cst["ones1x128b"][:],
                                                       qidr[0:1, 0:64], start=True, stop=True), inc=pe)
            p_ql = ctx.op("tensor", lambda e: e.matmul(PB[:, 960:1024], cst["ones1x128b"][:],
                                                       qidr[0:1, 64:128], start=True, stop=True), inc=pe)
            ctx.waitt("vector", r3)
            vk1 = ctx.op("vector", lambda e: e.tensor_copy(kid[:, 0:1], vt[:, 0, 129:130]), inc=ve)
            vk2 = ctx.op("vector", lambda e: e.tensor_copy(kid[:, 1:2], vt[:, 0, 130:131]), inc=ve)
            ctx.wait("vector", pe, p_ql)
            ctx.wait("vector", ve, vk2)
            b1 = ctx.op("vector", lambda e: e.tensor_scalar(beq[:, 0:64], PB[:, 896:960],
                                                            kid[:, 0:1], None, op0=ALU.is_equal), inc=ve, psum=True)
            b2 = ctx.op("vector", lambda e: e.tensor_scalar(beq[:, 64:128], PB[:, 960:1024],
                                                            kid[:, 1:2], None, op0=ALU.is_equal), inc=ve, psum=True)
            ctx.wait("vector", ve, b2)
            b3 = ctx.op("vector", lambda e: e.tensor_tensor(beq[:, 0:64], beq[:, 0:64],
                                                            beq[:, 64:128], ALU.mult), inc=ve)
            ctx.wait("vector", ve, b3)
            b4 = ctx.op("vector", lambda e: e.tensor_scalar(bmask[:], beq[:, 0:64],
                                                            -1.0, 1.0, op0=ALU.mult, op1=ALU.add), inc=ve)
            if ATT_PARTS < 3:
                return
            evac_of = {}
            exp_of = {}
            mask_of = {}

            def dots_batch(j8):
                bank = PC if j8 % 2 == 0 else PB
                if j8 >= 2:
                    ctx.wait("tensor", ac, exp_of[j8 - 2])
                for cc in range(8):
                    c = 8 * j8 + cc
                    ctx.op("tensor", lambda e, c=c, cc=cc, bank=bank: e.matmul(
                        bank[:, cc * 64:(cc + 1) * 64],
                        knT_s[:, 64 * c:64 * c + 128],
                        qkT_s[:, 64 * c + 64:64 * c + 128],
                        start=True, stop=True), inc=pe if cc == 7 else None)
                return ctx.cnt(pe)

            def exp_mask(j8, last_dots):
                bank = PC if j8 % 2 == 0 else PB
                pt = PT if j8 % 2 == 0 else PT2
                ctx.wait("scalar", pe, last_dots)
                if j8 >= 2:
                    ctx.wait("scalar", ve, mask_of[j8 - 2])
                a_exp = ctx.op("scalar", lambda e, bank=bank, pt=pt: e.activation(
                    out=pt[:], in_=bank[:, 0:512], func=AF.Exp, scale=RSQD), inc=ac, psum=True)
                exp_of[j8] = a_exp
                ctx.wait("vector", ac, a_exp)
                v_m = ctx.op("vector", lambda e, pt=pt: e.tensor_tensor(pt[:], pt[:], cst["pmask8"][:], ALU.mult), inc=ve)
                if j8 == 0:
                    ctx.wait("vector", ve, max(v_m, b4))
                    v_m = ctx.op("vector", lambda e, pt=pt: e.tensor_tensor(pt[:, 0:64], pt[:, 0:64],
                                                                            bmask[:], ALU.mult), inc=ve)
                mask_of[j8] = v_m
                return v_m

            def pv_batch(j8):
                pt = PT if j8 % 2 == 0 else PT2
                ctx.wait("tensor", ve, mask_of[j8])
                if j8 == 0:
                    ctx.waitt("tensor", r3)
                    ctx.waitt("tensor", r3b)
                    if attend_state["bo_q"] is not None:
                        ctx.waitt("vector", attend_state["bo_q"])
                        ctx.waitt("vector", attend_state["bo_q1"])
                for cc in range(8):
                    c = 8 * j8 + cc
                    m = c // 2
                    slot = [0, 512, 1024, 1536][m % 4]
                    if c % 2 == 0:
                        if m >= 4 and (m - 4) in evac_of:
                            ctx.wait("tensor", ve, evac_of[m - 4])
                        ctx.op("tensor", lambda e, c=c, cc=cc, slot=slot, pt=pt: e.matmul(
                            PA[64:128, slot:slot + 129],
                            pt[:, cc * 64:(cc + 1) * 64],
                            vt[:, c // 2, 0:129],
                            start=True, stop=True, tile_position=(0, 64)), inc=None)
                    else:
                        pv = ctx.op("tensor", lambda e, c=c, cc=cc, slot=slot, pt=pt: e.matmul(
                            PA[0:64, slot:slot + 129],
                            pt[:, cc * 64:(cc + 1) * 64],
                            vt2[:, (c - 1) // 2, 0:129],
                            start=True, stop=True, tile_position=(0, 0)), inc=pe)
                        ctx.wait("vector", pe, pv)
                        evac_of[m] = ctx.op("vector", lambda e, m=m, slot=slot: e.tensor_copy(
                            staging[:, m, 0:129], PA[:, slot:slot + 129]), inc=ve, psum=True)

            ctx.waitt("tensor", r1)
            ctx.waitt("tensor", r2)
            ld_prev = dots_batch(0)
            for j8 in range(8):
                ld_cur = ld_prev
                if j8 < 7:
                    ld_prev = dots_batch(j8 + 1)
                exp_mask(j8, ld_cur)
                pv_batch(j8)
            stg_ve = ctx.cnt(ve)
            ctx.wait("sync", ve, stg_ve)
            boT_ap = bo_h.rearrange("(g p) r -> p g r", p=128)
            w_bo1 = ctx.dma("sync", boT_ap[0:64, :, :], staging[64:128, :, :], qS)
            w_bo = ctx.dma("sync", boT_ap[64:128, :, :], staging[0:64, :, :], qS)
            attend_state["bo_q"] = w_bo
            attend_state["bo_q1"] = w_bo1
            if ATT_PARTS < 6:
                return
            ctx.waitt("gpsimd", w_bo1)
            ctx.waitt("gpsimd", w_bo)
            ctx.wait("gpsimd", ve, attend_state["acc_ve"])
            for g in range(G):
                ctx.op("gpsimd", lambda e, g=g, h=h: e.indirect_dma_start(
                    out=gathered[:, g, :], out_offset=None,
                    in_=bo_h[:],
                    in_offset=bass.IndirectOffsetOnAxis(posbo[:, g, h:h + 1], 0)), inc=qG, n=16)
            gq = ctx.cnt(qG)
            ctx.wait("vector", qG, gq)
            v_acc = ctx.op("vector", lambda e: e.tensor_tensor(
                acc[:].rearrange("p g r -> p (g r)"), acc[:].rearrange("p g r -> p (g r)"),
                gathered[:].rearrange("p g r -> p (g r)"), ALU.add), inc=ve)
            attend_state["acc_ve"] = v_acc

        NHD = NHEADS_RUN if stage >= 5 else min(NHEADS_RUN, 1) if stage < 4 else NHEADS_RUN
        for hd in range(NHD):
            barrier(("vector", "scalar", "tensor", "gpsimd", "sync"))
            build_head_payload(hd)
            if hd == 0 and "dbg_payload" in dbg:
                pd = ctx.cnt(ve)
                ctx.wait("sync", ve, pd)
                ctx.wait("sync", ac, ctx.cnt(ac))
                ctx.dma("sync", dbg["dbg_payload"][:], payload[:].rearrange("p g r -> p (g r)"), qS)
            if hd == 0 and "dbg_buckets" in dbg:
                pd = ctx.cnt(ve)
                ctx.wait("sync", ve, pd)
                ctx.dma("sync", dbg["dbg_buckets"][:], buckets[:].rearrange("p g h -> p (g h)"), qS)
            if stage < 3:
                continue
            pay_done_ve = ctx.cnt(ve)
            pay_done_ac = ctx.cnt(ac)
            ctx.wait("gpsimd", ve, pay_done_ve)
            ctx.wait("gpsimd", ac, pay_done_ac)
            for h in range(NH):
                sv = sort_hash(hd, h)
                ctx.wait("gpsimd", ve, sv)
                scatter_hash(hd, h)
            if hd == 0 and "dbg_pos" in dbg:
                pd = ctx.cnt(ve)
                ctx.wait("sync", ve, pd)
                ctx.dma("sync", dbg["dbg_pos"][:], posbo[:].rearrange("p g h -> p (g h)"), qS)
            sc_done = ctx.cnt(qG)
            attend_state["scat_q"] = sc_done
            ctx.wait("sync", qG, sc_done)
            wr = ctx.dma("sync", sbuf_h[0:64, :], sbuf_h[7 * T + 4096:7 * T + 4160, :], qS)
            attend_state["wrap_q"] = wr
            if stage < 4:
                continue
            vz2 = ctx.op("vector", lambda e: e.memset(acc[:].rearrange("p g r -> p (g r)"), 0.0), inc=ve)
            attend_state["acc_ve"] = vz2
            for h in range(ATT_HASHES):
                emit_attend(hd, h)
            ad = ctx.cnt(ve)
            ctx.wait("vector", ve, ad)
            v_rc = ctx.op("vector", lambda e: e.reciprocal(rcp[:], acc[:, :, 128].rearrange("p g -> p g")), inc=ve)
            ctx.wait("vector", ve, v_rc)
            v_at = ctx.op("vector", lambda e: e.tensor_tensor(
                attnb[:], acc[:, :, 0:128],
                rcp[:].rearrange("p (g o) -> p g o", o=1).to_broadcast([128, G, DH]), ALU.mult), inc=ve)
            ctx.wait("sync", ve, v_at)
            ctx.dma("sync", attn_h[hd, :, :].rearrange("(g p) d -> p g d", p=128), attnb[:], qS)
            if hd == 0 and "dbg_attn" in dbg:
                ctx.dma("sync", dbg["dbg_attn"][:, 0:G * DH], attnb[:].rearrange("p g d -> p (g d)"), qS)


    barrier()
    if stage < 5:
        finish_zero(nc, ctx, outT, sb)
        return

    qL, qA, qG, pe, ve, ac, qS = "qL", "qA", "qG", "pe", "ve", "ac", "qS"
    with ExitStack() as g3:
        def sb3(name, shape, dt):
            return g3.enter_context(nc.sbuf_tensor(name, shape, dt))

        def ps3(name, shape, dt=F32):
            return g3.enter_context(nc.psum_tensor(name, shape, dt))

        wout_own = sb3("wout_own", [128, 4, 1024], BF16)
        wout_sib = sb3("wout_sib", [128, 4, 1024], BF16)
        wp3f = sb3("wp3f", [128, 4, 1024], F32)
        att_all = sb3("att_all", [128, 16, 8, 128], BF16)
        attnT = sb3("attnT", [128, 8, 512], BF16)
        x2slab = sb3("x2slab", [128, 8, 512], F32)
        sq3 = sb3("sq3", [128, 8, 512], F32)
        xln3 = sb3("xln3", [128, 8, 512], BF16)
        evac3 = sb3("evac3", [128, 8, 512], F32)
        rowstat3 = sb3("rowstat3", [1, 4 * 512], F32)
        outslab = sb3("outslab", [128, 8, 512], F32)
        PS_STAT3 = ps3("ps_stat3", [1, 1024])
        PS_REPa = ps3("ps_repa", [128, 512])
        PS_REPb = ps3("ps_repb", [128, 512])
        PS_F3 = ps3("ps_f3", [128, 1024])
        PS_T3 = ps3("ps_t3", [128, 512], BF16)

        def dma_branchy(eng, out_even, in_even, out_odd, in_odd, role):
            idx = ctx._roles.get(role, 0)
            ctx._roles[role] = (idx + 1) % ctx.NROT
            name = "%s_%d" % (role, idx)
            h = ctx.sem(name)
            prev = ctx._semcnt[name]
            if prev:
                ctx.emit(eng, lambda e, h=h, v=prev: e.wait_ge(h, v))
            v = ctx.bump(name, 16)

            def go(e, oe=out_even, ie=in_even, oo=out_odd, io_=in_odd, h=h):
                pr = ctx.parity_reg(eng)["reg"]
                with e.If_eq(pr, 0):
                    e.dma_start(out=oe, in_=ie).then_inc(h, 16)
                with e.Else():
                    e.dma_start(out=oo, in_=io_).then_inc(h, 16)
            ctx.emit(eng, go)
            return (name, v)

        if "dbg_x1h" in dbg:
            ctx.dma("sync", dbg["dbg_x1h"][:].rearrange("(c p) t -> p c t", p=128),
                    x1T_h.rearrange("(c p) t -> p c t", p=128)[:, :, 0:512], qS)
        lw1 = ctx.dma("sync", wp3f[:], ap["woutT_own"].rearrange("(c p) o -> p c o", p=128), qL)
        ctx.waitt("vector", lw1)
        cw1 = ctx.op("vector", lambda e: e.tensor_copy(wout_own[:], wp3f[:]), inc=ve)
        ctx.wait("sync", ve, cw1)
        lw2 = dma_branchy("gpsimd",
                          wp3f[:], ap["woutT_sibB"].rearrange("(c p) o -> p c o", p=128),
                          wp3f[:], ap["woutT_sibA"].rearrange("(c p) o -> p c o", p=128), qL)
        ctx.waitt("vector", lw2)
        cw2 = ctx.op("vector", lambda e: e.tensor_copy(wout_sib[:], wp3f[:]), inc=ve)

        HT = T // 2
        if P3_PARTS < 1:
            barrier()
            finish_zero(nc, ctx, outT, sb3)
            return
        cc_toks = []
        for hd in range(NHEADS_CORE):
            cc_toks.append(dma_branchy(
                "gpsimd",
                ccin_h[:, hd * 128:(hd + 1) * 128], attn_h[hd, HT:T, :],
                ccin_h[:, hd * 128:(hd + 1) * 128], attn_h[hd, 0:HT, :], qG + "c"))
        for t_ in cc_toks:
            ctx.waitt("gpsimd", t_)
        ccsem = ctx.sem("cc")
        groups = [[0, 1], [2, 3], [4, 5], [6, 7]]
        ctx.emit("gpsimd", lambda e: e.collective_compute(
            "AllGather", ALU.bypass, groups, [ccin_h.opt()], [ccout_h.opt()]).then_inc(ccsem, 1))
        ctx.bump("cc", 1)
        ctx.wait("gpsimd", "cc", 1)

        if P3_PARTS < 2:
            barrier()
            finish_zero(nc, ctx, outT, sb3)
            return
        at_toks = []
        for hd in range(NHEADS_CORE):
            at_toks.append(dma_branchy(
                "gpsimd",
                att_all[:, :, hd, :], attn_h[hd, 0:HT, :].rearrange("(tg p) d -> p tg d", p=128),
                att_all[:, :, hd, :], attn_h[hd, HT:T, :].rearrange("(tg p) d -> p tg d", p=128), qL))
        for hd in range(NHEADS_CORE):
            at_toks.append(dma_branchy(
                "gpsimd",
                att_all[:, :, 4 + hd, :],
                ccout_h[HT:T, hd * 128:(hd + 1) * 128].rearrange("(tg p) d -> p tg d", p=128),
                att_all[:, :, 4 + hd, :],
                ccout_h[0:HT, hd * 128:(hd + 1) * 128].rearrange("(tg p) d -> p tg d", p=128), qL))
        for t_ in at_toks:
            ctx.waitt("tensor", t_)

        if P3_PARTS < 3:
            barrier()
            finish_zero(nc, ctx, outT, sb3)
            return
        x1v = x1T_h.rearrange("(c p) t -> p c t", p=128)

        def ln_rows3(src_slab, slab_cols):
            c0 = ctx.cnt(ve)
            ctx.wait("scalar", ve, c0)
            ctx.wait("tensor", ve, c0)
            a_sq = ctx.op("scalar", lambda e: e.activation(out=sq3[:, :, 0:slab_cols], in_=src_slab,
                                                           func=AF.Square), inc=ac)
            for fc in range(8):
                ctx.op("tensor", lambda e, fc=fc: e.matmul(
                    PS_STAT3[0:1, 0:slab_cols], cst["ones128f"][:], src_slab[:, fc, 0:slab_cols],
                    start=(fc == 0), stop=(fc == 7)), inc=pe if fc == 7 else None)
            ctx.wait("tensor", ac, a_sq)
            for fc in range(8):
                ctx.op("tensor", lambda e, fc=fc: e.matmul(
                    PS_STAT3[0:1, 512:512 + slab_cols], cst["ones128f"][:], sq3[:, fc, 0:slab_cols],
                    start=(fc == 0), stop=(fc == 7)), inc=pe if fc == 7 else None)
            p_done = ctx.cnt(pe)
            ctx.wait("vector", pe, p_done)
            R = rowstat3[:].rearrange("a (k n) -> a k n", n=512)
            v1 = ctx.op("vector", lambda e: e.tensor_scalar(R[:, 0, 0:slab_cols], PS_STAT3[0:1, 0:slab_cols],
                                                            1.0 / E, None, op0=ALU.mult), inc=ve, psum=True)
            v2 = ctx.op("vector", lambda e: e.tensor_scalar(R[:, 1, 0:slab_cols], PS_STAT3[0:1, 512:512 + slab_cols],
                                                            1.0 / E, None, op0=ALU.mult), inc=ve, psum=True)
            ctx.wait("vector", ve, v2)
            v3 = ctx.op("vector", lambda e: e.tensor_tensor(R[:, 2, 0:slab_cols], R[:, 0, 0:slab_cols],
                                                            R[:, 0, 0:slab_cols], ALU.mult), inc=ve)
            ctx.wait("vector", ve, v3)
            v4 = ctx.op("vector", lambda e: e.tensor_tensor(R[:, 2, 0:slab_cols], R[:, 1, 0:slab_cols],
                                                            R[:, 2, 0:slab_cols], ALU.subtract), inc=ve)
            ctx.wait("scalar", ve, v4)
            a1 = ctx.op("scalar", lambda e: e.activation(out=R[:, 3, 0:slab_cols], in_=R[:, 2, 0:slab_cols],
                                                         func=AF.Ln, bias=cst["epscol"][0:1, 0:1]), inc=ac)
            ctx.wait("scalar", ac, a1)
            a2 = ctx.op("scalar", lambda e: e.activation(out=R[:, 1, 0:slab_cols], in_=R[:, 3, 0:slab_cols],
                                                         func=AF.Exp, scale=-0.5), inc=ac)
            ctx.wait("vector", ac, a2)
            v5 = ctx.op("vector", lambda e: e.tensor_tensor(R[:, 0, 0:slab_cols], R[:, 0, 0:slab_cols],
                                                            R[:, 1, 0:slab_cols], ALU.mult), inc=ve)
            ctx.wait("tensor", ve, v5)
            ctx.op("tensor", lambda e: e.matmul(PS_REPa[:, 0:slab_cols], cst["ones1x128f"][:],
                                                R[0:1, 1, 0:slab_cols], start=True, stop=True), inc=pe)
            pb = ctx.op("tensor", lambda e: e.matmul(PS_REPb[:, 0:slab_cols], cst["ones1x128f"][:],
                                                     R[0:1, 0, 0:slab_cols], start=True, stop=True), inc=pe)
            return pb

        SLAB3 = 512
        prev_wr = []
        for sidx in range(HT // SLAB3):
            c0 = ctx.cnt(ve)
            ctx.wait("tensor", ve, c0)
            ctx.wait("tensor", ac, ctx.cnt(ac))
            tr_evac = {}
            for idx3 in range(32):
                hdc, tt = idx3 // 4, idx3 % 4
                tg = sidx * 4 + tt
                tslot3 = idx3 % 4
                if idx3 >= 1:
                    ctx.wait("tensor", ve, tr_evac[idx3 - 1])
                ctx.op("tensor", lambda e, hdc=hdc, tg=tg, tslot3=tslot3: e.transpose(
                    PS_T3[:, tslot3 * 128:(tslot3 + 1) * 128],
                    att_all[:, tg, hdc, :], cst["ident128"][:]), inc=pe)
                p_tr = ctx.cnt(pe)
                if P3_PARTS == 31:
                    tr_evac[idx3] = ctx.op("vector", lambda e: e.memset(attnT[:, 0, 0:4], 0.0), inc=ve)
                    continue
                ctx.wait("vector", pe, p_tr)
                tr_evac[idx3] = ctx.op("vector", lambda e, hdc=hdc, tt=tt, tslot3=tslot3: e.tensor_copy(
                    attnT[:, hdc, tt * 128:(tt + 1) * 128],
                    PS_T3[:, tslot3 * 128:(tslot3 + 1) * 128]), inc=ve, psum=True)
            trd = ctx.cnt(ve)
            ctx.wait("tensor", ve, trd)
            if P3_PARTS < 4:
                continue
            ctx.wait("gpsimd", ve, ctx.cnt(ve))
            ctx.wait("gpsimd", ac, ctx.cnt(ac))
            lx1 = dma_branchy("gpsimd",
                              sq3[:], x1v[:, :, sidx * SLAB3:(sidx + 1) * SLAB3],
                              sq3[:], x1v[:, :, HT + sidx * SLAB3:HT + (sidx + 1) * SLAB3], qA)
            ctx.waitt("vector", lx1)
            if sidx == 0 and "dbg_x1r" in dbg:
                ctx.waitt("sync", lx1)
                ctx.dma("sync", dbg["dbg_x1r"][:].rearrange("p (c t) -> p c t", t=512), sq3[:], qS)
            ev_oc = {}
            for oc in range(8):
                slot = oc % 2
                if oc >= 2:
                    ctx.wait("tensor", ve, ev_oc[oc - 2])
                for hdc in range(8):
                    w = wout_own if hdc < 4 else wout_sib
                    ctx.op("tensor", lambda e, oc=oc, hdc=hdc, slot=slot, w=w: e.matmul(
                        PS_F3[:, slot * 512:slot * 512 + SLAB3],
                        w[:, hdc % 4, oc * 128:(oc + 1) * 128],
                        attnT[:, hdc, :],
                        start=(hdc == 0), stop=(hdc == 7)), inc=pe if hdc == 7 else None)
                pdone = ctx.cnt(pe)
                ctx.wait("vector", pe, pdone)
                if oc == 0:
                    ctx.wait("vector", ac, ctx.cnt(ac))
                e1 = ctx.op("vector", lambda e, oc=oc, slot=slot: e.tensor_scalar(
                    evac3[:, oc, :], PS_F3[:, slot * 512:slot * 512 + SLAB3],
                    boutc[:, oc:oc + 1], None, op0=ALU.add), inc=ve, psum=True)
                ev_oc[oc] = e1
                ctx.wait("vector", ve, e1)
                ctx.op("vector", lambda e, oc=oc: e.tensor_tensor(
                    x2slab[:, oc, :], sq3[:, oc, :], evac3[:, oc, :], ALU.add), inc=ve)
            if P3_PARTS < 5:
                continue
            if sidx == 0 and "dbg_x2" in dbg:
                dd = ctx.cnt(ve)
                ctx.wait("sync", ve, dd)
                ctx.dma("sync", dbg["dbg_x2"][:].rearrange("p (c t) -> p c t", t=512), x2slab[:], qS)
                ctx.dma("sync", dbg["dbg_attT"][:].rearrange("p (c t) -> p c t", t=512), attnT[:], qS)
            pb3 = ln_rows3(x2slab[:], SLAB3)
            ctx.wait("vector", pe, pb3)
            vA3 = ctx.op("vector", lambda e: e.tensor_tensor(
                sq3[:], x2slab[:],
                PS_REPa[:, 0:SLAB3].rearrange("p (c n) -> p c n", c=1).to_broadcast([128, 8, SLAB3]),
                ALU.mult), inc=ve, psum=True)
            ctx.wait("vector", ve, vA3)
            vB3 = ctx.op("vector", lambda e: e.tensor_tensor(
                xln3[:], sq3[:],
                PS_REPb[:, 0:SLAB3].rearrange("p (c n) -> p c n", c=1).to_broadcast([128, 8, SLAB3]),
                ALU.subtract), inc=ve, psum=True)
            ctx.wait("tensor", ve, vB3)
            ev2 = {}
            for oc in range(8):
                slot = oc % 2
                if oc >= 2:
                    ctx.wait("tensor", ve, ev2[oc - 2])
                for fc in range(8):
                    ctx.op("tensor", lambda e, fc=fc, oc=oc, slot=slot: e.matmul(
                        PS_F3[:, slot * 512:slot * 512 + SLAB3],
                        tern[:, fc, oc * 128:(oc + 1) * 128], xln3[:, fc, :],
                        start=(fc == 0), stop=(fc == 7)), inc=pe if fc == 7 else None)
                pdone = ctx.cnt(pe)
                ctx.wait("vector", pe, pdone)
                e2 = ctx.op("vector", lambda e, oc=oc, slot=slot: e.tensor_scalar(
                    evac3[:, oc, :], PS_F3[:, slot * 512:slot * 512 + SLAB3],
                    halfalpha[:, 0:1], halfbias[:, oc:oc + 1], op0=ALU.mult, op1=ALU.add), inc=ve, psum=True)
                ev2[oc] = e2
                ctx.wait("vector", ve, e2)
                ctx.op("vector", lambda e, oc=oc: e.tensor_tensor(
                    outslab[:, oc, :], x2slab[:, oc, :], evac3[:, oc, :], ALU.add), inc=ve)
            odone = ctx.cnt(ve)
            ctx.wait("sync", ve, odone)
            for tok in prev_wr:
                ctx.waitt("vector", tok)
            prev_wr = [ctx.dma("sync", outT.rearrange("(c p) t -> p c t", p=128)[:, :, sidx * SLAB3:(sidx + 1) * SLAB3],
                               outslab[:], qS)]
        for tok in prev_wr:
            ctx.waitt("sync", tok)
        if P3_PARTS < 5:
            barrier()
            finish_zero(nc, ctx, outT, sb3)
            return
        barrier(("sync",))


def finish_zero(nc, ctx, outT, sb):
    zrow = sb("zrow", [128, 8, 64], F32)
    vz = ctx.op("vector", lambda e: e.memset(zrow[:], 0.0), inc="ve")
    ctx.wait("sync", "ve", vz)
    toks = []
    for i in range(8):
        toks.append(ctx.dma("sync", outT.rearrange("(c p) t -> p c t", p=128)[:, :, i * 64:(i + 1) * 64], zrow[:], "qOut"))
    for t_ in toks:
        ctx.waitt("sync", t_)



def make_core_inputs(inputs, consts, c):
    b = c // 2
    hh = c % 2
    heads = list(range(4 * hh, 4 * hh + 4))
    x = np.asarray(inputs["L0_attn_x"], np.float32)
    w_qk = np.asarray(inputs["L0_attn_w_qk"], np.float32)
    w_v = np.asarray(inputs["L0_attn_w_v"], np.float32)
    w_out = np.asarray(inputs["L0_attn_w_out"], np.float32)
    b_out = np.asarray(inputs["L0_attn_b_out"], np.float32)
    rotations = np.asarray(inputs["L0_attn_rotations"], np.float32)
    fw = np.asarray(inputs["L0_ffn_weight"], np.float32)
    fa = np.asarray(inputs["L0_ffn_alpha"], np.float32)
    fb = np.asarray(inputs["L0_ffn_bias"], np.float32)

    d = dict(consts)
    d["xT"] = np.ascontiguousarray(x[b].T)
    d["wffnT"] = np.ascontiguousarray(fw.T)
    colsel = np.concatenate([np.arange(h * DH, (h + 1) * DH) for h in heads])
    d["wqkvT"] = np.ascontiguousarray(np.concatenate([w_qk.T[:, colsel], w_v.T[:, colsel]], axis=1))
    woutT = np.ascontiguousarray(w_out.T)
    d["woutT_own"] = np.ascontiguousarray(woutT[colsel, :])
    d["woutT_sibA"] = np.ascontiguousarray(woutT[0:512, :])
    d["woutT_sibB"] = np.ascontiguousarray(woutT[512:1024, :])
    d["boutRows"] = np.ascontiguousarray(b_out.reshape(8, 128).T)
    d["fbiasRows"] = np.ascontiguousarray(fb.reshape(8, 128).T)
    d["alpha"] = fa.reshape(1, 1)
    d["rotT"] = np.ascontiguousarray(rotations.reshape(DH, NH * 32))
    return d


def kernel(**inputs):
    consts = host_consts()
    nc = build_program()
    in_maps = [make_core_inputs(inputs, consts, c) for c in range(8)]
    res = run_bass_kernel_spmd(nc, in_maps, core_ids=list(range(8)))
    out = np.zeros((B, T, E), dtype=np.float32)
    for c in range(8):
        b = c // 2
        hh = c % 2
        o = res.results[c]["outT"]
        out[b, hh * (T // 2):(hh + 1) * (T // 2), :] = o.T
    return out


# version 04
# speedup vs baseline: 1.1691x; 1.1691x over previous
import numpy as np
import ml_dtypes
from contextlib import ExitStack

import concourse.bass as bass
import concourse.mybir as mybir
from concourse.bass_utils import run_bass_kernel_spmd

F32 = mybir.dt.float32
BF16 = mybir.dt.bfloat16
U32 = mybir.dt.uint32
I16 = mybir.dt.int16
AX = mybir.AxisListType
ALU = mybir.AluOpType
AF = mybir.ActivationFunctionType

E = 1024
T = 4096
B = 4
NH = 8
DH = 128
NB = 64
G = 32
NHEADS_CORE = 4
ROWL = 400
VOFF = 256
ONEOFF = 384
IDHI = 385
IDLO = 386
SBROWS = 7 * T + 4224
RSQD = float(DH) ** -0.5
P1B_TILES = G
P1B_HASH = True
P1B_PARTS = 9
ATT_HASHES = NH
ATT_PARTS = 9
NHEADS_RUN = NHEADS_CORE
P3_PARTS = 9


class Ctx:

    ENGS = ("sync", "scalar", "vector", "tensor", "gpsimd")

    def __init__(self, nc, st):
        self.nc = nc
        self.st = st
        self.steps = {e: [] for e in self.ENGS}
        self._semcnt = {}
        self._sems = {}
        self._roles = {}
        self.last_act_psum = 0
        self.last_dve_psum = 0
        self._parity_reg = {}

    def setup_parity(self, eng):
        cell = {}
        self._parity_reg[eng] = cell

        def go(e, cell=cell):
            pid = e.partition_id()
            pr = e.alloc_register()
            e.reg_mod(pr, pid, 2)
            cell["reg"] = pr
        self.emit(eng, go)

    def parity_reg(self, eng):
        return self._parity_reg[eng]

    def sem(self, name):
        if name not in self._sems:
            self._sems[name] = self.st.enter_context(self.nc.semaphore(name))
            self._semcnt[name] = 0
        return self._sems[name]

    def cnt(self, name):
        return self._semcnt[name]

    def bump(self, name, n):
        self._semcnt[name] += n
        return self._semcnt[name]

    def emit(self, eng, fn):
        self.steps[eng].append(fn)

    def wait(self, eng, sem_name, val):
        h = self.sem(sem_name)
        self.emit(eng, lambda e, h=h, v=val: e.wait_ge(h, v))

    def op(self, eng, fn, inc=None, n=1, psum=False):
        if psum:
            assert inc is not None
            if eng == "scalar" and self.last_dve_psum:
                self.wait(eng, "ve", self.last_dve_psum)
            if eng == "vector" and self.last_act_psum:
                self.wait(eng, "ac", self.last_act_psum)
        if inc is None:
            self.emit(eng, fn)
            return None
        h = self.sem(inc)
        v = self.bump(inc, n)
        self.emit(eng, lambda e, fn=fn, h=h, n=n: fn(e).then_inc(h, n))
        if psum:
            if eng == "scalar":
                self.last_act_psum = v
            elif eng == "vector":
                self.last_dve_psum = v
        return v

    NROT = 6

    def dma(self, eng, out_ap, in_ap, role, **kw):
        idx = self._roles.get(role, 0)
        self._roles[role] = (idx + 1) % self.NROT
        name = "%s_%d" % (role, idx)
        h = self.sem(name)
        prev = self._semcnt[name]
        if prev:
            self.emit(eng, lambda e, h=h, v=prev: e.wait_ge(h, v))
        v = self.bump(name, 16)
        self.emit(eng, lambda e, o=out_ap, i=in_ap, kw=kw, h=h: e.dma_start(out=o, in_=i, **kw).then_inc(h, 16))
        return (name, v)

    def waitt(self, eng, token):
        name, v = token
        self.wait(eng, name, v)

    def wait_role_all(self, eng, role):
        for idx in range(self.NROT):
            name = "%s_%d" % (role, idx)
            if name in self._semcnt and self._semcnt[name]:
                self.wait(eng, name, self._semcnt[name])

    def finalize(self, block):
        for name in self.ENGS:
            steps = self.steps[name]
            deco = getattr(block, name)

            def make(steps):
                def body(e):
                    for f in steps:
                        f(e)
                return body
            deco(make(steps))



def host_consts():
    c = {}
    c["slt128"] = (np.arange(128)[:, None] < np.arange(128)[None, :]).astype(ml_dtypes.bfloat16)
    c["slt32"] = (np.arange(32)[:, None] < np.arange(32)[None, :]).astype(np.float32)
    c["iota64"] = np.tile(np.arange(NB, dtype=np.float32)[None, :], (128, G))
    c["ones128b"] = np.ones((128, 1), dtype=ml_dtypes.bfloat16)
    c["ones128f"] = np.ones((128, 1), dtype=np.float32)
    c["ones32f"] = np.ones((32, 1), dtype=np.float32)
    c["ones1x128f"] = np.ones((1, 128), dtype=np.float32)
    c["ones1x128b"] = np.ones((1, 128), dtype=ml_dtypes.bfloat16)
    c["zeros1x64"] = np.zeros((1, 64), dtype=np.float32)
    c["epscol"] = np.full((128, 1), 1e-5, dtype=np.float32)
    c["ident128"] = np.eye(128, dtype=ml_dtypes.bfloat16)
    c["offs8"] = np.tile((np.arange(NH, dtype=np.float32) * 32)[None, :], (128, 1))
    pm = np.ones((128, 8, 64), dtype=ml_dtypes.bfloat16)
    for j in range(8):
        for i in range(64):
            pm[64 + i, j, i] = 0
    c["pmask8"] = pm.reshape(128, 512)
    tt = np.arange(T).reshape(G, 128).T
    c["idhi"] = (tt // 64).astype(np.float32)
    c["idlo"] = (tt % 64).astype(np.float32)
    return c


CONST_DTYPES = {
    "slt128": BF16, "slt32": F32, "iota64": F32, "ones128b": BF16,
    "ones128f": F32, "ones32f": F32, "ones1x128f": F32, "ones1x128b": BF16,
    "zeros1x64": F32, "epscol": F32, "ident128": BF16, "offs8": F32, "pmask8": BF16, "idhi": F32, "idlo": F32,
}




def build_program(debug=(), stage=5):
    nc = bass.Bass("TRN2", target_bir_lowering=False)

    ap = {}
    ap["xT"] = nc.dram_tensor("xT", [E, T], F32, kind="ExternalInput").ap()
    ap["wffnT"] = nc.dram_tensor("wffnT", [E, E], F32, kind="ExternalInput").ap()
    ap["wqkvT"] = nc.dram_tensor("wqkvT", [E, E], F32, kind="ExternalInput").ap()
    ap["woutT_own"] = nc.dram_tensor("woutT_own", [512, E], F32, kind="ExternalInput").ap()
    ap["woutT_sibA"] = nc.dram_tensor("woutT_sibA", [512, E], F32, kind="ExternalInput").ap()
    ap["woutT_sibB"] = nc.dram_tensor("woutT_sibB", [512, E], F32, kind="ExternalInput").ap()
    ap["boutRows"] = nc.dram_tensor("boutRows", [128, 8], F32, kind="ExternalInput").ap()
    ap["fbiasRows"] = nc.dram_tensor("fbiasRows", [128, 8], F32, kind="ExternalInput").ap()
    ap["alpha"] = nc.dram_tensor("alpha", [1, 1], F32, kind="ExternalInput").ap()
    ap["rotT"] = nc.dram_tensor("rotT", [128, 256], F32, kind="ExternalInput").ap()
    shapes = {"slt128": [128, 128], "slt32": [32, 32], "iota64": [128, G * NB],
              "ones128b": [128, 1], "ones128f": [128, 1], "ones32f": [32, 1],
              "ones1x128f": [1, 128], "ones1x128b": [1, 128], "zeros1x64": [1, 64],
              "epscol": [128, 1], "ident128": [128, 128],
              "offs8": [128, 8], "pmask8": [128, 512], "idhi": [128, G], "idlo": [128, G]}
    for k, dt in CONST_DTYPES.items():
        ap[k] = nc.dram_tensor(k, shapes[k], dt, kind="ExternalInput").ap()

    outT = nc.dram_tensor("outT", [E, T // 2], F32, kind="ExternalOutput").ap()
    dbg = {}
    if "x1" in debug:
        dbg["dbg_x1"] = nc.dram_tensor("dbg_x1", [E, T], F32, kind="ExternalOutput").ap()
    if "xa" in debug:
        dbg["dbg_xa"] = nc.dram_tensor("dbg_xa", [E, T], BF16, kind="ExternalOutput").ap()
    if "payload" in debug:
        dbg["dbg_payload"] = nc.dram_tensor("dbg_payload", [128, G * ROWL], BF16, kind="ExternalOutput").ap()
    if "buckets" in debug:
        dbg["dbg_buckets"] = nc.dram_tensor("dbg_buckets", [128, G * NH], F32, kind="ExternalOutput").ap()
    if "pos" in debug:
        dbg["dbg_pos"] = nc.dram_tensor("dbg_pos", [128, G * NH], U32, kind="ExternalOutput").ap()
    if "attn" in debug:
        dbg["dbg_attn"] = nc.dram_tensor("dbg_attn", [128, G * DH], BF16, kind="ExternalOutput").ap()
    if "p3" in debug:
        dbg["dbg_x1r"] = nc.dram_tensor("dbg_x1r", [128, 8 * 512], F32, kind="ExternalOutput").ap()
        dbg["dbg_x2"] = nc.dram_tensor("dbg_x2", [128, 8 * 512], F32, kind="ExternalOutput").ap()
        dbg["dbg_attT"] = nc.dram_tensor("dbg_attT", [128, 8 * 512], BF16, kind="ExternalOutput").ap()
        dbg["dbg_x1h"] = nc.dram_tensor("dbg_x1h", [E, 512], F32, kind="ExternalOutput").ap()

    x1T_h = nc.dram_tensor("x1T_h", [E, T], F32, kind="Internal").ap()
    xaT_h = nc.dram_tensor("xaT_h", [E, T], BF16, kind="Internal").ap()
    sbuf_h = nc.dram_tensor("sbuf_h", [SBROWS, ROWL], BF16, kind="Internal").ap()
    bo_h = nc.dram_tensor("bo_h", [T, 132], F32, kind="Internal").ap()
    ccin_h = nc.dram_tensor("ccin_h", [T // 2, 512], BF16, kind="Internal").ap()
    ccout_h = nc.dram_tensor("ccout_h", [T, 512], BF16, kind="Internal").ap()
    attn_h = nc.dram_tensor("attn_h", [NHEADS_CORE, T, DH], BF16, kind="Internal").ap()
    sort_scr1 = nc.dram_tensor("sort_scr1", [G * NB], F32, kind="Internal").ap()
    sort_scr2 = nc.dram_tensor("sort_scr2", [G * NB], F32, kind="Internal").ap()

    st = ExitStack()
    ctx = Ctx(nc, st)

    def sb(name, shape, dt):
        return st.enter_context(nc.sbuf_tensor(name, shape, dt))

    def psum(name, shape, dt=F32):
        return st.enter_context(nc.psum_tensor(name, shape, dt))

    with st:
        build_body(nc, ctx, ap, outT, dbg, x1T_h, xaT_h, sbuf_h, bo_h, ccin_h, ccout_h,
                   attn_h, sort_scr1, sort_scr2, sb, psum, debug, stage)
        block = st.enter_context(nc.Block())
        ctx.finalize(block)
    return nc


def build_body(nc, ctx, ap, outT, dbg, x1T_h, xaT_h, sbuf_h, bo_h, ccin_h, ccout_h, attn_h,
               sort_scr1, sort_scr2, sb, psum, debug, stage):
    from contextlib import ExitStack

    tern = sb("tern", [128, 8, 1024], BF16)
    wqkv = sb("wqkv", [128, 8, 1024], BF16)
    rotb = sb("rotb", [128, 256], BF16)
    cst = {}
    for k in CONST_DTYPES:
        cst[k] = sb("c_" + k, list(ap[k].shape), CONST_DTYPES[k])
    halfbias = sb("halfbias", [128, 8], F32)
    boutc = sb("boutc", [128, 8], F32)
    halfalpha = sb("halfalpha", [128, 1], F32)
    deltas = sb("deltas", [128, 8], F32)

    qL, qA, qG, pe, ve, ac, qS = "qL", "qA", "qG", "pe", "ve", "ac", "qS"
    ctx.setup_parity("gpsimd")

    def barrier(engines=("sync", "scalar", "vector", "tensor", "gpsimd")):
        for sem in (qG, pe, ve, ac):
            v = ctx.cnt(sem) if sem in ctx._semcnt else 0
            if v:
                for e in engines:
                    ctx.wait(e, sem, v)
        for role in (qL, qA, qS, "qDg"):
            for e in engines:
                ctx.wait_role_all(e, role)

    with ExitStack() as g1:
        def sb1(name, shape, dt):
            return g1.enter_context(nc.sbuf_tensor(name, shape, dt))

        def ps1(name, shape, dt=F32):
            return g1.enter_context(nc.psum_tensor(name, shape, dt))

        wf32 = sb1("wf32", [128, 8, 1024], F32)
        tpos = sb1("tpos", [128, 8, 1024], BF16)
        rotf = sb1("rotf", [128, 256], F32)
        xslab = sb1("xslab", [128, 8, 512], F32)
        sq = sb1("sq", [128, 8, 512], F32)
        xln = sb1("xln", [128, 8, 512], BF16)
        x1slab = sb1("x1slab", [128, 8, 512], F32)
        rowstat = sb1("rowstat", [1, 8 * 512], F32)
        evac = sb1("evac", [128, 8, 512], F32)
        PS_STAT = ps1("ps_stat", [1, 1024])
        PS_REP = ps1("ps_rep", [128, 512])
        PS_REP2 = ps1("ps_rep2", [128, 512])
        PS_F = ps1("ps_f", [128, 1024])
        PS_M = ps1("ps_m", [128, 512])

        for k in CONST_DTYPES:
            ctx.dma("sync", cst[k][:], ap[k][:], qL)
        ctx.dma("sync", rotf[:], ap["rotT"][:], qL)
        ctx.dma("sync", halfbias[:], ap["fbiasRows"][:], qL)
        ctx.dma("sync", boutc[:], ap["boutRows"][:], qL)
        ctx.dma("sync", wf32[:], ap["wffnT"].rearrange("(c p) o -> p c o", p=128), qL)

        ctx.wait_role_all("vector", qL)
        ctx.op("vector", lambda e: e.tensor_copy(rotb[:], rotf[:]), inc=ve)
        ctx.op("vector", lambda e: e.tensor_scalar(halfbias[:], halfbias[:], 0.5, None, op0=ALU.mult), inc=ve)

        v_abs = ctx.op("vector", lambda e: e.tensor_reduce(
            out=deltas[:, 0:1].rearrange("p (a o) -> p a o", a=1),
            in_=wf32[:].rearrange("p c o -> p (c o)").rearrange("p (a n) -> p a n", a=1),
            axis=AX.X, op=ALU.add, apply_absolute_value=True), inc=ve)
        ctx.wait("tensor", ve, v_abs)
        p_sum = ctx.op("tensor", lambda e: e.matmul(PS_M[0:1, 0:1], deltas[:, 0:1], cst["ones128f"][:],
                                                    start=True, stop=True), inc=pe)
        ctx.wait("vector", pe, p_sum)
        v_del = ctx.op("vector", lambda e: e.tensor_scalar(deltas[0:1, 1:2], PS_M[0:1, 0:1],
                                                           0.7 / (E * E), None, op0=ALU.mult), inc=ve, psum=True)
        ctx.wait("tensor", ve, v_del)
        p_drep = ctx.op("tensor", lambda e: e.matmul(PS_M[:, 1:2], cst["ones1x128f"][:], deltas[0:1, 1:2],
                                                     start=True, stop=True), inc=pe)
        ctx.wait("vector", pe, p_drep)
        v_dc = ctx.op("vector", lambda e: e.tensor_copy(deltas[:, 0:1], PS_M[:, 1:2]), inc=ve, psum=True)
        ctx.wait("vector", ve, v_dc)
        v_dn = ctx.op("vector", lambda e: e.tensor_scalar(deltas[:, 1:2], deltas[:, 0:1], -1.0, None, op0=ALU.mult), inc=ve)
        ctx.wait("vector", ve, v_dn)
        v_t1 = ctx.op("vector", lambda e: e.tensor_scalar(tpos[:], wf32[:], deltas[:, 0:1], None, op0=ALU.is_gt), inc=ve)
        v_t2 = ctx.op("vector", lambda e: e.tensor_scalar(tern[:], wf32[:], deltas[:, 1:2], None, op0=ALU.is_lt), inc=ve)
        ctx.wait("vector", ve, v_t2)
        v_t3 = ctx.op("vector", lambda e: e.tensor_tensor(tern[:], tpos[:], tern[:], ALU.subtract), inc=ve)

        ctx.wait("sync", ve, v_t3)
        l_wq = ctx.dma("sync", wf32[:], ap["wqkvT"].rearrange("(c p) o -> p c o", p=128), qL)
        ctx.waitt("vector", l_wq)
        v_wq = ctx.op("vector", lambda e: e.tensor_copy(wqkv[:], wf32[:]), inc=ve)
        l_al = ctx.dma("sync", deltas[0:1, 3:4], ap["alpha"][:], qL)
        ctx.waitt("tensor", l_al)
        ctx.wait("tensor", ve, v_wq)
        p_arep = ctx.op("tensor", lambda e: e.matmul(PS_M[:, 2:3], cst["ones1x128f"][:], deltas[0:1, 3:4],
                                                     start=True, stop=True), inc=pe)
        ctx.wait("vector", pe, p_arep)
        ctx.op("vector", lambda e: e.tensor_scalar(halfalpha[:], PS_M[:, 2:3], 0.5, None, op0=ALU.mult), inc=ve, psum=True)

        def ln_rows(src_slab, slab_cols):
            c0 = ctx.cnt(ve)
            ctx.wait("scalar", ve, c0)
            ctx.wait("tensor", ve, c0)
            a_sq = ctx.op("scalar", lambda e: e.activation(out=sq[:, :, 0:slab_cols], in_=src_slab,
                                                           func=AF.Square), inc=ac)
            for fc in range(8):
                ctx.op("tensor", lambda e, fc=fc: e.matmul(
                    PS_STAT[0:1, 0:slab_cols], cst["ones128f"][:], src_slab[:, fc, 0:slab_cols],
                    start=(fc == 0), stop=(fc == 7)), inc=pe if fc == 7 else None)
            ctx.wait("tensor", ac, a_sq)
            for fc in range(8):
                ctx.op("tensor", lambda e, fc=fc: e.matmul(
                    PS_STAT[0:1, 512:512 + slab_cols], cst["ones128f"][:], sq[:, fc, 0:slab_cols],
                    start=(fc == 0), stop=(fc == 7)), inc=pe if fc == 7 else None)
            p_done = ctx.cnt(pe)
            ctx.wait("vector", pe, p_done)
            R = rowstat[:].rearrange("a (k n) -> a k n", n=512)
            v1 = ctx.op("vector", lambda e: e.tensor_scalar(R[:, 0, 0:slab_cols], PS_STAT[0:1, 0:slab_cols],
                                                            1.0 / E, None, op0=ALU.mult), inc=ve, psum=True)
            v2 = ctx.op("vector", lambda e: e.tensor_scalar(R[:, 1, 0:slab_cols], PS_STAT[0:1, 512:512 + slab_cols],
                                                            1.0 / E, None, op0=ALU.mult), inc=ve, psum=True)
            ctx.wait("vector", ve, v2)
            v3 = ctx.op("vector", lambda e: e.tensor_tensor(R[:, 2, 0:slab_cols], R[:, 0, 0:slab_cols],
                                                            R[:, 0, 0:slab_cols], ALU.mult), inc=ve)
            ctx.wait("vector", ve, v3)
            v4 = ctx.op("vector", lambda e: e.tensor_tensor(R[:, 2, 0:slab_cols], R[:, 1, 0:slab_cols],
                                                            R[:, 2, 0:slab_cols], ALU.subtract), inc=ve)
            ctx.wait("scalar", ve, v4)
            a1 = ctx.op("scalar", lambda e: e.activation(out=R[:, 3, 0:slab_cols], in_=R[:, 2, 0:slab_cols],
                                                         func=AF.Ln, bias=cst["epscol"][0:1, 0:1]), inc=ac)
            ctx.wait("scalar", ac, a1)
            a2 = ctx.op("scalar", lambda e: e.activation(out=R[:, 1, 0:slab_cols], in_=R[:, 3, 0:slab_cols],
                                                         func=AF.Exp, scale=-0.5), inc=ac)
            ctx.wait("vector", ac, a2)
            v5 = ctx.op("vector", lambda e: e.tensor_tensor(R[:, 0, 0:slab_cols], R[:, 0, 0:slab_cols],
                                                            R[:, 1, 0:slab_cols], ALU.mult), inc=ve)
            ctx.wait("tensor", ve, v5)
            pa = ctx.op("tensor", lambda e: e.matmul(PS_REP[:, 0:slab_cols], cst["ones1x128f"][:],
                                                     R[0:1, 1, 0:slab_cols], start=True, stop=True), inc=pe)
            pb = ctx.op("tensor", lambda e: e.matmul(PS_REP2[:, 0:slab_cols], cst["ones1x128f"][:],
                                                     R[0:1, 0, 0:slab_cols], start=True, stop=True), inc=pe)
            return pb

        SLAB = 512
        prev_w = []
        for s in range(T // SLAB):
            for tok in prev_w:
                ctx.waitt("vector", tok)
            prev_w = []
            ld = ctx.dma("sync", xslab[:],
                         ap["xT"].rearrange("(c p) t -> p c t", p=128)[:, :, s * SLAB:(s + 1) * SLAB], qL)
            ctx.waitt("scalar", ld)
            ctx.waitt("vector", ld)
            ctx.waitt("tensor", ld)
            pb = ln_rows(xslab[:], SLAB)
            ctx.wait("vector", pe, pb)
            vA = ctx.op("vector", lambda e: e.tensor_tensor(
                sq[:], xslab[:],
                PS_REP[:, 0:SLAB].rearrange("p (c n) -> p c n", c=1).to_broadcast([128, 8, SLAB]),
                ALU.mult), inc=ve, psum=True)
            ctx.wait("vector", ve, vA)
            vB = ctx.op("vector", lambda e: e.tensor_tensor(
                xln[:], sq[:],
                PS_REP2[:, 0:SLAB].rearrange("p (c n) -> p c n", c=1).to_broadcast([128, 8, SLAB]),
                ALU.subtract), inc=ve, psum=True)
            ctx.wait("tensor", ve, vB)
            ev_by_oc = {}
            for oc in range(8):
                slot = oc % 2
                if oc >= 2:
                    ctx.wait("tensor", ve, ev_by_oc[oc - 2])
                for fc in range(8):
                    ctx.op("tensor", lambda e, fc=fc, oc=oc, slot=slot: e.matmul(
                        PS_F[:, slot * 512:slot * 512 + SLAB],
                        tern[:, fc, oc * 128:(oc + 1) * 128], xln[:, fc, :],
                        start=(fc == 0), stop=(fc == 7)), inc=pe if fc == 7 else None)
                pdone = ctx.cnt(pe)
                ctx.wait("vector", pe, pdone)
                ev = ctx.op("vector", lambda e, oc=oc, slot=slot: e.tensor_scalar(
                    evac[:, oc, :], PS_F[:, slot * 512:slot * 512 + SLAB],
                    halfalpha[:, 0:1], halfbias[:, oc:oc + 1], op0=ALU.mult, op1=ALU.add), inc=ve, psum=True)
                ev_by_oc[oc] = ev
                ctx.wait("vector", ve, ev)
                ctx.op("vector", lambda e, oc=oc: e.tensor_tensor(
                    x1slab[:, oc, :], xslab[:, oc, :], evac[:, oc, :], ALU.add), inc=ve)
            x1done = ctx.cnt(ve)
            ctx.wait("sync", ve, x1done)
            prev_w.append(ctx.dma("sync", x1T_h.rearrange("(c p) t -> p c t", p=128)[:, :, s * SLAB:(s + 1) * SLAB], x1slab[:], qS))
            if "dbg_x1" in dbg:
                prev_w.append(ctx.dma("sync", dbg["dbg_x1"].rearrange("(c p) t -> p c t", p=128)[:, :, s * SLAB:(s + 1) * SLAB], x1slab[:], qS))
            pb2 = ln_rows(x1slab[:], SLAB)
            ctx.wait("vector", pe, pb2)
            vA2 = ctx.op("vector", lambda e: e.tensor_tensor(
                sq[:], x1slab[:],
                PS_REP[:, 0:SLAB].rearrange("p (c n) -> p c n", c=1).to_broadcast([128, 8, SLAB]),
                ALU.mult), inc=ve, psum=True)
            ctx.wait("vector", ve, vA2)
            vB2 = ctx.op("vector", lambda e: e.tensor_tensor(
                xln[:], sq[:],
                PS_REP2[:, 0:SLAB].rearrange("p (c n) -> p c n", c=1).to_broadcast([128, 8, SLAB]),
                ALU.subtract), inc=ve, psum=True)
            ctx.wait("sync", ve, vB2)
            prev_w.append(ctx.dma("sync", xaT_h.rearrange("(c p) t -> p c t", p=128)[:, :, s * SLAB:(s + 1) * SLAB], xln[:], qS))
            if "dbg_xa" in dbg:
                prev_w.append(ctx.dma("sync", dbg["dbg_xa"].rearrange("(c p) t -> p c t", p=128)[:, :, s * SLAB:(s + 1) * SLAB], xln[:], qS))
            ctx.wait("sync", ve, ctx.cnt(ve))
        barrier()

    if stage < 2:
        finish_zero(nc, ctx, outT, sb)
        return

    with ExitStack() as g2:
        def sb2(name, shape, dt):
            return g2.enter_context(nc.sbuf_tensor(name, shape, dt))

        def ps2(name, shape, dt=F32):
            return g2.enter_context(nc.psum_tensor(name, shape, dt))

        payload = sb2("payload", [128, G, ROWL], BF16)
        buckets = sb2("buckets", [128, G, NH], F32)
        xat = sb2("xat", [128, 8, 2 * 128], BF16)
        qkTt = sb2("qkTt", [128, 2 * 128], BF16)
        u1 = sb2("u1", [128, G, 264], BF16)
        rn_all = sb2("rn_all", [128, G], F32)
        rn2 = sb2("rn2", [128, G], F32)
        rn3 = sb2("rn3", [128, G], F32)
        amax_all = sb2("amax_all", [128, 5 * G * NH], F32)
        aidx_all = sb2("aidx_all", [128, 2 * G * NH], U32)
        acond_all = sb2("acond_all", [128, G * NH], U32)
        onehot = sb2("onehot", [128, G * NB], BF16)
        ttrs = sb2("ttrs", [128, G * NB], F32)
        crow = sb2("crow", [1, G * NB], F32)
        ctT = sb2("ctT", [32, NB], F32)
        srow = sb2("srow", [1, 4 * NB], F32)
        ccs_row = sb2("ccs_row", [1, G * NB], F32)
        posf = sb2("posf", [128, G], F32)
        posfw = sb2("posfw", [128, G, NH], U32)
        posbo = sb2("posbo", [128, G, NH], U32)
        qkT_s = sb2("qkT_s", [128, 4224], BF16)
        knT_s = sb2("knT_s", [128, 4224], BF16)
        vt = sb2("vt", [128, 33, 132], BF16)
        vt2 = sb2("vt2", [128, 32, 132], BF16)
        PT = sb2("PT", [128, 512], BF16)
        PT2 = sb2("PT2", [128, 512], BF16)
        qidr = sb2("qidr", [1, 128], BF16)
        bmask = sb2("bmask", [128, 64], BF16)
        bm1 = sb2("bm1", [128, 64], F32)
        bm2 = sb2("bm2", [128, 64], F32)
        staging = sb2("staging", [128, G, 132], F32)
        acc = sb2("acc", [128, G, 132], F32)
        rcp = sb2("rcp", [128, G], F32)
        PA = ps2("pa", [128, 2048])
        PB = ps2("pb", [128, 1024])
        PC = ps2("pc", [128, 512])
        PD = ps2("pd", [128, 256], BF16)

        vz = ctx.op("vector", lambda e: e.memset(staging[:, :, 129:132], 0.0), inc=ve)
        ctx.op("vector", lambda e: e.memset(payload[:, :, IDLO + 1:ROWL], 0.0), inc=ve)
        ctx.op("vector", lambda e: e.memset(payload[:, :, ONEOFF:ONEOFF + 1], 1.0), inc=ve)
        ctx.op("vector", lambda e: e.tensor_copy(payload[:, :, IDHI:IDHI + 1],
                                                 cst["idhi"][:].rearrange("p (g o) -> p g o", o=1)), inc=ve)
        vlast = ctx.op("vector", lambda e: e.tensor_copy(payload[:, :, IDLO:IDLO + 1],
                                                         cst["idlo"][:].rearrange("p (g o) -> p g o", o=1)), inc=ve)

        def build_head_payload(hd):
            xaT_t = xaT_h.rearrange("(c p) t -> p c t", p=128)
            war_ve, war_ac, war_tr, war_rot = {}, {}, {}, {}
            for g in range(P1B_TILES):
                dbuf = g % 2
                if g >= 2:
                    ctx.wait("sync", ve, war_rot[g - 2])
                ld = ctx.dma("sync", xat[:, :, dbuf * 128:(dbuf + 1) * 128],
                             xaT_t[:, :, g * 128:(g + 1) * 128], qA)
                ctx.waitt("tensor", ld)
                slot = g % 2
                if g >= 2:
                    ctx.wait("tensor", ve, war_ve[g - 2])
                    ctx.wait("tensor", ac, war_ac[g - 2])
                for fc in range(8):
                    ctx.op("tensor", lambda e, fc=fc, slot=slot, dbuf=dbuf, hd=hd: e.matmul(
                        PB[:, slot * 256:slot * 256 + 128],
                        xat[:, fc, dbuf * 128:(dbuf + 1) * 128],
                        wqkv[:, fc, hd * 128:(hd + 1) * 128],
                        start=(fc == 0), stop=(fc == 7)), inc=None)
                for fc in range(8):
                    ctx.op("tensor", lambda e, fc=fc, slot=slot, dbuf=dbuf, hd=hd: e.matmul(
                        PB[:, slot * 256 + 128:slot * 256 + 256],
                        xat[:, fc, dbuf * 128:(dbuf + 1) * 128],
                        wqkv[:, fc, 512 + hd * 128:512 + (hd + 1) * 128],
                        start=(fc == 0), stop=(fc == 7)), inc=pe if fc == 7 else None)
                pq = ctx.cnt(pe)
                ctx.wait("scalar", pe, pq)
                aq = ctx.op("scalar", lambda e, g=g, slot=slot: e.activation(
                    out=payload[:, g, 0:128], in_=PB[:, slot * 256:slot * 256 + 128],
                    func=AF.Copy), inc=ac, psum=True)
                ctx.wait("vector", pe, pq)
                vv = ctx.op("vector", lambda e, g=g, slot=slot: e.tensor_copy(
                    payload[:, g, VOFF:VOFF + 128], PB[:, slot * 256 + 128:slot * 256 + 256]), inc=ve, psum=True)
                tslot = g % 2
                if g >= 2:
                    ctx.wait("tensor", ve, war_tr[g - 2])
                ctx.wait("tensor", ac, aq)
                p_tr = ctx.op("tensor", lambda e, g=g, tslot=tslot: e.transpose(
                    PD[:, tslot * 128:(tslot + 1) * 128], payload[:, g, 0:128], cst["ident128"][:]), inc=pe)
                ctx.wait("vector", pe, p_tr)
                v_tr = ctx.op("vector", lambda e, tslot=tslot: e.tensor_copy(
                    qkTt[:, tslot * 128:(tslot + 1) * 128], PD[:, tslot * 128:(tslot + 1) * 128]), inc=ve, psum=True)
                war_tr[g] = v_tr
                ctx.wait("tensor", ve, v_tr)
                if g >= 2:
                    ctx.wait("tensor", ve, war_rot[g - 2])
                p_rot = ctx.op("tensor", lambda e, tslot=tslot: e.matmul(
                    PA[:, tslot * 256:(tslot + 1) * 256], qkTt[:, tslot * 128:(tslot + 1) * 128],
                    rotb[:], start=True, stop=True), inc=pe)
                ctx.wait("vector", pe, p_rot)
                v_re = ctx.op("vector", lambda e, g=g, tslot=tslot: e.tensor_copy(
                    u1[:, g, 0:256], PA[:, tslot * 256:(tslot + 1) * 256]), inc=ve, psum=True)
                war_rot[g] = v_re
                war_ve[g] = vv
                war_ac[g] = aq
            cA = ctx.cnt(ve)
            ctx.wait("vector", ve, cA)
            ctx.wait("vector", ac, ctx.cnt(ac))
            for half in range(2):
                hs = half * (G // 2)
                v_sq = ctx.op("vector", lambda e, hs=hs: e.tensor_tensor(
                    ttrs[:].rearrange("p (g d) -> p g d", d=128),
                    payload[:, hs:hs + G // 2, 0:128], payload[:, hs:hs + G // 2, 0:128], ALU.mult), inc=ve)
                ctx.wait("vector", ve, v_sq)
                v_ss = ctx.op("vector", lambda e, hs=hs: e.tensor_reduce(
                    out=rn_all[:, hs:hs + G // 2].rearrange("p (g o) -> p g o", o=1),
                    in_=ttrs[:].rearrange("p (g d) -> p g d", d=128),
                    axis=AX.X, op=ALU.add), inc=ve)
                ctx.wait("vector", ve, v_ss)
            v_cl = ctx.op("vector", lambda e: e.tensor_scalar(rn2[:], rn_all[:], 1e-24, None, op0=ALU.max), inc=ve)
            ctx.wait("scalar", ve, v_cl)
            a_ln = ctx.op("scalar", lambda e: e.activation(out=rn_all[:], in_=rn2[:], func=AF.Ln), inc=ac)
            ctx.wait("scalar", ac, a_ln)
            a_ex = ctx.op("scalar", lambda e: e.activation(out=rn3[:], in_=rn_all[:], func=AF.Exp, scale=-0.5), inc=ac)
            ctx.wait("vector", ac, a_ex)
            v_n1 = ctx.op("vector", lambda e: e.tensor_tensor(rn_all[:], rn3[:], rn3[:], ALU.mult), inc=ve)
            ctx.wait("vector", ve, v_n1)
            v_n2 = ctx.op("vector", lambda e: e.tensor_tensor(rn_all[:], rn_all[:], rn2[:], ALU.mult), inc=ve)
            ctx.wait("vector", ve, v_n2)
            v_n3 = ctx.op("vector", lambda e: e.tensor_scalar(rn_all[:], rn_all[:], -0.5, 1.5, op0=ALU.mult, op1=ALU.add), inc=ve)
            ctx.wait("vector", ve, v_n3)
            v_n4 = ctx.op("vector", lambda e: e.tensor_tensor(rn2[:], rn3[:], rn_all[:], ALU.mult), inc=ve)
            ctx.wait("vector", ve, v_n4)
            v_kn = ctx.op("vector", lambda e: e.tensor_tensor(
                payload[:, :, 128:256], payload[:, :, 0:128],
                rn2[:].rearrange("p (g o) -> p g o", o=1).to_broadcast([128, G, 128]), ALU.mult), inc=ve)
            R4 = u1[:, :, 0:256].rearrange("p g (h r) -> p g h r", r=32)
            v1 = ctx.op("vector", lambda e: e.tensor_reduce(
                out=amax_all[:, 0:G * NH].rearrange("p (g h o) -> p g h o", h=NH, o=1),
                in_=R4, axis=AX.X, op=ALU.max), inc=ve)
            v2 = ctx.op("vector", lambda e: e.tensor_reduce(
                out=amax_all[:, G * NH:2 * G * NH].rearrange("p (g h o) -> p g h o", h=NH, o=1),
                in_=R4, axis=AX.X, op=ALU.min), inc=ve)
            ctx.wait("vector", ve, v2)
            for g in range(P1B_TILES):
                ctx.op("vector", lambda e, g=g: e.max_index(
                    aidx_all[:, g * NH:(g + 1) * NH], amax_all[:, g * NH:(g + 1) * NH],
                    u1[:, g, 0:256]), inc=ve)
                ctx.op("vector", lambda e, g=g: e.max_index(
                    aidx_all[:, G * NH + g * NH:G * NH + (g + 1) * NH],
                    amax_all[:, G * NH + g * NH:G * NH + (g + 1) * NH],
                    u1[:, g, 0:256]), inc=ve)
            mi = ctx.cnt(ve)
            ctx.wait("vector", ve, mi)
            vc1 = ctx.op("vector", lambda e: e.tensor_copy(amax_all[:, 2 * G * NH:3 * G * NH], aidx_all[:, 0:G * NH]), inc=ve)
            vc2 = ctx.op("vector", lambda e: e.tensor_copy(amax_all[:, 3 * G * NH:4 * G * NH], aidx_all[:, G * NH:2 * G * NH]), inc=ve)
            v7 = ctx.op("vector", lambda e: e.tensor_tensor(amax_all[:, 4 * G * NH:5 * G * NH],
                                                            amax_all[:, 0:G * NH], amax_all[:, G * NH:2 * G * NH], ALU.add), inc=ve)
            ctx.wait("vector", ve, v7)
            v8 = ctx.op("vector", lambda e: e.tensor_scalar(acond_all[:], amax_all[:, 4 * G * NH:5 * G * NH],
                                                            0.0, None, op0=ALU.is_ge), inc=ve)
            v9 = ctx.op("vector", lambda e: e.tensor_scalar(amax_all[:, 3 * G * NH:4 * G * NH],
                                                            amax_all[:, 3 * G * NH:4 * G * NH], 32.0, None, op0=ALU.add), inc=ve)
            ctx.wait("vector", ve, v9)
            v10 = ctx.op("vector", lambda e: e.tensor_copy(
                buckets[:].rearrange("p g h -> p (g h)"), amax_all[:, 3 * G * NH:4 * G * NH]), inc=ve)
            ctx.wait("vector", ve, v10)
            v11 = ctx.op("vector", lambda e: e.copy_predicated(
                buckets[:].rearrange("p g h -> p (g h)"), acond_all[:], amax_all[:, 2 * G * NH:3 * G * NH]), inc=ve)
            ctx.wait("vector", ve, v11)
            v12 = ctx.op("vector", lambda e: e.tensor_tensor(
                buckets[:], buckets[:],
                cst["offs8"][:].rearrange("p (o h) -> p o h", o=1).to_broadcast([128, G, NH]), ALU.subtract), inc=ve)
            ctx.wait("vector", ve, v12)
            ctx.op("vector", lambda e: e.tensor_scalar(
                buckets[:].rearrange("p g h -> p (g h)"), buckets[:].rearrange("p g h -> p (g h)"),
                0.0, 63.0, op0=ALU.max, op1=ALU.min), inc=ve)

        def sort_hash(hd, h):
            ctx.wait("vector", pe, ctx.cnt(pe))
            ctx.wait("vector", ve, ctx.cnt(ve))
            ctx.wait("vector", ac, ctx.cnt(ac))
            buck = buckets[:, :, h:h + 1].rearrange("p g o -> p (g o)")
            v_oh = ctx.op("vector", lambda e: e.tensor_tensor(
                onehot[:].rearrange("p (g k) -> p g k", k=NB), cst["iota64"][:].rearrange("p (g k) -> p g k", k=NB),
                buckets[:, :, h:h + 1].to_broadcast([128, G, NB]), ALU.is_equal), inc=ve)
            ctx.wait("tensor", ve, v_oh)
            for i in range(4):
                mm = ctx.op("tensor", lambda e, i=i: e.matmul(
                    PA[0:1, i * 512:(i + 1) * 512], cst["ones128b"][:], onehot[:, i * 512:(i + 1) * 512],
                    start=True, stop=True), inc=pe if i == 3 else None)
            ctx.wait("vector", pe, ctx.cnt(pe))
            v_cr = ctx.op("vector", lambda e: e.tensor_copy(crow[:], PA[0:1, :]), inc=ve, psum=True)
            ctx.wait("sync", ve, v_cr)
            d1 = ctx.dma("sync", sort_scr1.rearrange("(o n) -> o n", o=1), crow[:], "qDg")
            ctx.waitt("sync", d1)
            d2 = ctx.dma("sync", ctT[:], sort_scr1.rearrange("(g k) -> g k", g=G), "qDg")
            ctx.waitt("tensor", d2)
            p_cc = ctx.op("tensor", lambda e: e.matmul(PB[0:32, 0:NB], cst["slt32"][:], ctT[:],
                                                       start=True, stop=False), inc=pe)
            p_tot = ctx.op("tensor", lambda e: e.matmul(PB[0:1, 512:512 + NB], cst["ones32f"][:], ctT[:],
                                                        start=True, stop=True), inc=pe)
            ctx.wait("vector", pe, p_tot)
            v_tr = ctx.op("vector", lambda e: e.tensor_copy(srow[:, 0:NB], PB[0:1, 512:512 + NB]), inc=ve, psum=True)
            ctx.wait("vector", ve, v_tr)
            v_sc = ctx.op("vector", lambda e: e.tensor_tensor_scan(
                srow[:, NB:2 * NB], srow[:, 0:NB], cst["zeros1x64"][:], 0.0, ALU.add, ALU.add), inc=ve)
            ctx.wait("vector", ve, v_sc)
            v_st = ctx.op("vector", lambda e: e.tensor_tensor(
                srow[:, 2 * NB:3 * NB], srow[:, NB:2 * NB], srow[:, 0:NB], ALU.subtract), inc=ve)
            ctx.wait("tensor", ve, v_st)
            p_add = ctx.op("tensor", lambda e: e.matmul(PB[0:32, 0:NB], cst["ones1x128f"][:, 0:32],
                                                        srow[0:1, 2 * NB:3 * NB], start=False, stop=True), inc=pe)
            ctx.wait("vector", pe, p_add)
            v_ccs = ctx.op("vector", lambda e: e.tensor_copy(ctT[:], PB[0:32, 0:NB]), inc=ve, psum=True)
            ctx.wait("sync", ve, v_ccs)
            d3 = ctx.dma("sync", sort_scr2.rearrange("(g k) -> g k", g=G), ctT[:], "qDg")
            ctx.waitt("sync", d3)
            d4 = ctx.dma("sync", ccs_row[:], sort_scr2.rearrange("(o n) -> o n", o=1), "qDg")
            ctx.waitt("tensor", d4)
            for i in range(4):
                ctx.op("tensor", lambda e, i=i: e.matmul(
                    PA[:, i * 512:(i + 1) * 512], cst["ones1x128f"][:], ccs_row[:, i * 512:(i + 1) * 512],
                    start=True, stop=False), inc=None)
            for i in range(4):
                ctx.op("tensor", lambda e, i=i: e.matmul(
                    PA[:, i * 512:(i + 1) * 512], cst["slt128"][:], onehot[:, i * 512:(i + 1) * 512],
                    start=False, stop=True), inc=pe if i == 3 else None)
            p_r = ctx.cnt(pe)
            ctx.wait("vector", pe, p_r)
            v_mu = ctx.op("vector", lambda e: e.tensor_tensor(ttrs[:], onehot[:], PA[:], ALU.mult), inc=ve, psum=True)
            ctx.wait("vector", ve, v_mu)
            v_ps = ctx.op("vector", lambda e: e.tensor_reduce(
                out=posf[:].rearrange("p (g o) -> p g o", o=1),
                in_=ttrs[:].rearrange("p (g k) -> p g k", k=NB), axis=AX.X, op=ALU.add), inc=ve)
            ctx.wait("vector", ve, v_ps)
            v_pf = ctx.op("vector", lambda e, h=h: e.tensor_scalar(
                posfw[:, :, h], posf[:], float(64 + h * T), None, op0=ALU.add), inc=ve)
            v_pb = ctx.op("vector", lambda e, h=h: e.tensor_scalar(
                posbo[:, :, h], posf[:], 0.0, None, op0=ALU.add), inc=ve)
            return ctx.cnt(ve)

        def scatter_hash(hd, h):
            for g in range(G):
                ctx.op("gpsimd", lambda e, g=g, h=h: e.indirect_dma_start(
                    out=sbuf_h[:],
                    out_offset=bass.IndirectOffsetOnAxis(posfw[:, g, h:h + 1], 0),
                    in_=payload[:, g, :], in_offset=None), inc=qG, n=16)

        beq = sb2("beq", [128, 128], F32)
        kid = sb2("kid", [128, 2], F32)
        attend_state = {"acc_ve": 0, "bo_q": None, "bo_q1": None, "wrap_q": None, "scat_q": 0}

        def emit_attend(hd, h):
            base = h * T
            ctx.wait("sync", pe, ctx.cnt(pe))
            ctx.wait("scalar", pe, ctx.cnt(pe))
            ctx.wait("scalar", qG, attend_state["scat_q"])
            if h == 0:
                ctx.waitt("sync", attend_state["wrap_q"])
                ctx.waitt("scalar", attend_state["wrap_q"])
            r1 = ctx.dma("sync", qkT_s[:], sbuf_h[base:base + 4224, 0:128], qL, transpose=True)
            r2 = ctx.dma("sync", knT_s[:], sbuf_h[base:base + 4224, 128:256], qL, transpose=True)
            r3 = ctx.dma("scalar", vt[:],
                         sbuf_h[base:base + 4224, :].rearrange("(b p) r -> p b r", p=128)[:, :, VOFF:VOFF + 132], qA)
            r3b = ctx.dma("scalar", vt2[:],
                          sbuf_h[base + 64:base + 64 + 4096, :].rearrange("(b p) r -> p b r", p=128)[:, :, VOFF:VOFF + 132], qA)
            r4 = ctx.dma("sync", qidr[0:1, 0:64],
                         sbuf_h[base + 64:base + 128, IDHI:IDHI + 1].rearrange("t c -> c t"), qL,
                         allow_slow_non_contiguous=True)
            r5 = ctx.dma("sync", qidr[0:1, 64:128],
                         sbuf_h[base + 64:base + 128, IDLO:IDLO + 1].rearrange("t c -> c t"), qL,
                         allow_slow_non_contiguous=True)
            if ATT_PARTS < 2:
                return
            ctx.waitt("tensor", r4)
            ctx.waitt("tensor", r5)
            p_qh = ctx.op("tensor", lambda e: e.matmul(PB[:, 896:960], cst["ones1x128b"][:],
                                                       qidr[0:1, 0:64], start=True, stop=True), inc=pe)
            p_ql = ctx.op("tensor", lambda e: e.matmul(PB[:, 960:1024], cst["ones1x128b"][:],
                                                       qidr[0:1, 64:128], start=True, stop=True), inc=pe)
            ctx.waitt("vector", r3)
            vk1 = ctx.op("vector", lambda e: e.tensor_copy(kid[:, 0:1], vt[:, 0, 129:130]), inc=ve)
            vk2 = ctx.op("vector", lambda e: e.tensor_copy(kid[:, 1:2], vt[:, 0, 130:131]), inc=ve)
            ctx.wait("vector", pe, p_ql)
            ctx.wait("vector", ve, vk2)
            b1 = ctx.op("vector", lambda e: e.tensor_scalar(beq[:, 0:64], PB[:, 896:960],
                                                            kid[:, 0:1], None, op0=ALU.is_equal), inc=ve, psum=True)
            b2 = ctx.op("vector", lambda e: e.tensor_scalar(beq[:, 64:128], PB[:, 960:1024],
                                                            kid[:, 1:2], None, op0=ALU.is_equal), inc=ve, psum=True)
            ctx.wait("vector", ve, b2)
            b3 = ctx.op("vector", lambda e: e.tensor_tensor(beq[:, 0:64], beq[:, 0:64],
                                                            beq[:, 64:128], ALU.mult), inc=ve)
            ctx.wait("vector", ve, b3)
            b4 = ctx.op("vector", lambda e: e.tensor_scalar(bmask[:], beq[:, 0:64],
                                                            -1.0, 1.0, op0=ALU.mult, op1=ALU.add), inc=ve)
            if ATT_PARTS < 3:
                return
            evac_of = {}
            exp_of = {}
            mask_of = {}

            def dots_batch(j8):
                bank = PC if j8 % 2 == 0 else PB
                if j8 >= 2:
                    ctx.wait("tensor", ac, exp_of[j8 - 2])
                for cc in range(8):
                    c = 8 * j8 + cc
                    ctx.op("tensor", lambda e, c=c, cc=cc, bank=bank: e.matmul(
                        bank[:, cc * 64:(cc + 1) * 64],
                        knT_s[:, 64 * c:64 * c + 128],
                        qkT_s[:, 64 * c + 64:64 * c + 128],
                        start=True, stop=True), inc=pe if cc == 7 else None)
                return ctx.cnt(pe)

            def exp_mask(j8, last_dots):
                bank = PC if j8 % 2 == 0 else PB
                pt = PT if j8 % 2 == 0 else PT2
                ctx.wait("scalar", pe, last_dots)
                if j8 >= 2:
                    ctx.wait("scalar", ve, mask_of[j8 - 2])
                a_exp = ctx.op("scalar", lambda e, bank=bank, pt=pt: e.activation(
                    out=pt[:], in_=bank[:, 0:512], func=AF.Exp, scale=RSQD), inc=ac, psum=True)
                exp_of[j8] = a_exp
                ctx.wait("vector", ac, a_exp)
                v_m = ctx.op("vector", lambda e, pt=pt: e.tensor_tensor(pt[:], pt[:], cst["pmask8"][:], ALU.mult), inc=ve)
                if j8 == 0:
                    ctx.wait("vector", ve, max(v_m, b4))
                    v_m = ctx.op("vector", lambda e, pt=pt: e.tensor_tensor(pt[:, 0:64], pt[:, 0:64],
                                                                            bmask[:], ALU.mult), inc=ve)
                mask_of[j8] = v_m
                return v_m

            def pv_batch(j8):
                pt = PT if j8 % 2 == 0 else PT2
                ctx.wait("tensor", ve, mask_of[j8])
                if j8 == 0:
                    ctx.waitt("tensor", r3)
                    ctx.waitt("tensor", r3b)
                    if attend_state["bo_q"] is not None:
                        ctx.waitt("vector", attend_state["bo_q"])
                        ctx.waitt("vector", attend_state["bo_q1"])
                for cc in range(8):
                    c = 8 * j8 + cc
                    m = c // 2
                    slot = [0, 512, 1024, 1536][m % 4]
                    if c % 2 == 0:
                        if m >= 4 and (m - 4) in evac_of:
                            ctx.wait("tensor", ve, evac_of[m - 4])
                        ctx.op("tensor", lambda e, c=c, cc=cc, slot=slot, pt=pt: e.matmul(
                            PA[64:128, slot:slot + 129],
                            pt[:, cc * 64:(cc + 1) * 64],
                            vt[:, c // 2, 0:129],
                            start=True, stop=True, tile_position=(0, 64)), inc=None)
                    else:
                        pv = ctx.op("tensor", lambda e, c=c, cc=cc, slot=slot, pt=pt: e.matmul(
                            PA[0:64, slot:slot + 129],
                            pt[:, cc * 64:(cc + 1) * 64],
                            vt2[:, (c - 1) // 2, 0:129],
                            start=True, stop=True, tile_position=(0, 0)), inc=pe)
                        ctx.wait("vector", pe, pv)
                        evac_of[m] = ctx.op("vector", lambda e, m=m, slot=slot: e.tensor_copy(
                            staging[:, m, 0:129], PA[:, slot:slot + 129]), inc=ve, psum=True)

            ctx.waitt("tensor", r1)
            ctx.waitt("tensor", r2)
            ld_prev = dots_batch(0)
            for j8 in range(8):
                ld_cur = ld_prev
                if j8 < 7:
                    ld_prev = dots_batch(j8 + 1)
                exp_mask(j8, ld_cur)
                pv_batch(j8)
            stg_ve = ctx.cnt(ve)
            ctx.wait("sync", ve, stg_ve)
            boT_ap = bo_h.rearrange("(g p) r -> p g r", p=128)
            w_bo1 = ctx.dma("sync", boT_ap[0:64, :, :], staging[64:128, :, :], qS)
            w_bo = ctx.dma("sync", boT_ap[64:128, :, :], staging[0:64, :, :], qS)
            attend_state["bo_q"] = w_bo
            attend_state["bo_q1"] = w_bo1
            if ATT_PARTS < 6:
                return
            ctx.waitt("gpsimd", w_bo1)
            ctx.waitt("gpsimd", w_bo)
            ctx.wait("gpsimd", ve, attend_state["acc_ve"])
            for g in range(G):
                ctx.op("gpsimd", lambda e, g=g, h=h: e.indirect_dma_start(
                    out=u1[:, g, 0:132], out_offset=None,
                    in_=bo_h[:],
                    in_offset=bass.IndirectOffsetOnAxis(posbo[:, g, h:h + 1], 0)), inc=qG, n=16)
            gq = ctx.cnt(qG)
            ctx.wait("vector", qG, gq)
            v_acc = ctx.op("vector", lambda e: e.tensor_tensor(
                acc[:], acc[:], u1[:, :, 0:132], ALU.add), inc=ve)
            attend_state["acc_ve"] = v_acc

        NHD = NHEADS_RUN if stage >= 5 else min(NHEADS_RUN, 1) if stage < 4 else NHEADS_RUN
        for hd in range(NHD):
            barrier(("vector", "scalar", "tensor", "gpsimd", "sync"))
            build_head_payload(hd)
            if hd == 0 and "dbg_payload" in dbg:
                pd = ctx.cnt(ve)
                ctx.wait("sync", ve, pd)
                ctx.wait("sync", ac, ctx.cnt(ac))
                ctx.dma("sync", dbg["dbg_payload"][:], payload[:].rearrange("p g r -> p (g r)"), qS)
            if hd == 0 and "dbg_buckets" in dbg:
                pd = ctx.cnt(ve)
                ctx.wait("sync", ve, pd)
                ctx.dma("sync", dbg["dbg_buckets"][:], buckets[:].rearrange("p g h -> p (g h)"), qS)
            if stage < 3:
                continue
            pay_done_ve = ctx.cnt(ve)
            pay_done_ac = ctx.cnt(ac)
            ctx.wait("gpsimd", ve, pay_done_ve)
            ctx.wait("gpsimd", ac, pay_done_ac)
            for h in range(NH):
                sv = sort_hash(hd, h)
                ctx.wait("gpsimd", ve, sv)
                scatter_hash(hd, h)
            if hd == 0 and "dbg_pos" in dbg:
                pd = ctx.cnt(ve)
                ctx.wait("sync", ve, pd)
                ctx.dma("sync", dbg["dbg_pos"][:], posbo[:].rearrange("p g h -> p (g h)"), qS)
            sc_done = ctx.cnt(qG)
            attend_state["scat_q"] = sc_done
            ctx.wait("sync", qG, sc_done)
            wr = ctx.dma("sync", sbuf_h[0:64, :], sbuf_h[7 * T + 4096:7 * T + 4160, :], qS)
            attend_state["wrap_q"] = wr
            if stage < 4:
                continue
            vz2 = ctx.op("vector", lambda e: e.memset(acc[:].rearrange("p g r -> p (g r)"), 0.0), inc=ve)
            attend_state["acc_ve"] = vz2
            for h in range(ATT_HASHES):
                emit_attend(hd, h)
            ad = ctx.cnt(ve)
            ctx.wait("vector", ve, ad)
            v_rc = ctx.op("vector", lambda e: e.reciprocal(rcp[:], acc[:, :, 128].rearrange("p g -> p g")), inc=ve)
            ctx.wait("vector", ve, v_rc)
            v_at = ctx.op("vector", lambda e: e.tensor_tensor(
                u1[:, :, 132:260], acc[:, :, 0:128],
                rcp[:].rearrange("p (g o) -> p g o", o=1).to_broadcast([128, G, DH]), ALU.mult), inc=ve)
            ctx.wait("sync", ve, v_at)
            ctx.dma("sync", attn_h[hd, :, :].rearrange("(g p) d -> p g d", p=128), u1[:, :, 132:260], qS)
            if hd == 0 and "dbg_attn" in dbg:
                ctx.dma("sync", dbg["dbg_attn"][:, 0:G * DH].rearrange("p (g d) -> p g d", d=DH), u1[:, :, 132:260], qS)


    barrier()
    if stage < 5:
        finish_zero(nc, ctx, outT, sb)
        return

    qL, qA, qG, pe, ve, ac, qS = "qL", "qA", "qG", "pe", "ve", "ac", "qS"
    with ExitStack() as g3:
        def sb3(name, shape, dt):
            return g3.enter_context(nc.sbuf_tensor(name, shape, dt))

        def ps3(name, shape, dt=F32):
            return g3.enter_context(nc.psum_tensor(name, shape, dt))

        wout_own = sb3("wout_own", [128, 4, 1024], BF16)
        wout_sib = sb3("wout_sib", [128, 4, 1024], BF16)
        wp3f = sb3("wp3f", [128, 4, 1024], F32)
        att_all = sb3("att_all", [128, 16, 8, 128], BF16)
        attnT = sb3("attnT", [128, 8, 512], BF16)
        x2slab = sb3("x2slab", [128, 8, 512], F32)
        sq3 = sb3("sq3", [128, 8, 512], F32)
        xln3 = sb3("xln3", [128, 8, 512], BF16)
        evac3 = sb3("evac3", [128, 8, 512], F32)
        rowstat3 = sb3("rowstat3", [1, 4 * 512], F32)
        outslab = sb3("outslab", [128, 8, 512], F32)
        PS_STAT3 = ps3("ps_stat3", [1, 1024])
        PS_REPa = ps3("ps_repa", [128, 512])
        PS_REPb = ps3("ps_repb", [128, 512])
        PS_F3 = ps3("ps_f3", [128, 1024])
        PS_T3 = ps3("ps_t3", [128, 512], BF16)

        def dma_branchy(eng, out_even, in_even, out_odd, in_odd, role):
            idx = ctx._roles.get(role, 0)
            ctx._roles[role] = (idx + 1) % ctx.NROT
            name = "%s_%d" % (role, idx)
            h = ctx.sem(name)
            prev = ctx._semcnt[name]
            if prev:
                ctx.emit(eng, lambda e, h=h, v=prev: e.wait_ge(h, v))
            v = ctx.bump(name, 16)

            def go(e, oe=out_even, ie=in_even, oo=out_odd, io_=in_odd, h=h):
                pr = ctx.parity_reg(eng)["reg"]
                with e.If_eq(pr, 0):
                    e.dma_start(out=oe, in_=ie).then_inc(h, 16)
                with e.Else():
                    e.dma_start(out=oo, in_=io_).then_inc(h, 16)
            ctx.emit(eng, go)
            return (name, v)

        if "dbg_x1h" in dbg:
            ctx.dma("sync", dbg["dbg_x1h"][:].rearrange("(c p) t -> p c t", p=128),
                    x1T_h.rearrange("(c p) t -> p c t", p=128)[:, :, 0:512], qS)
        lw1 = ctx.dma("sync", wp3f[:], ap["woutT_own"].rearrange("(c p) o -> p c o", p=128), qL)
        ctx.waitt("vector", lw1)
        cw1 = ctx.op("vector", lambda e: e.tensor_copy(wout_own[:], wp3f[:]), inc=ve)
        ctx.wait("sync", ve, cw1)
        lw2 = dma_branchy("gpsimd",
                          wp3f[:], ap["woutT_sibB"].rearrange("(c p) o -> p c o", p=128),
                          wp3f[:], ap["woutT_sibA"].rearrange("(c p) o -> p c o", p=128), qL)
        ctx.waitt("vector", lw2)
        cw2 = ctx.op("vector", lambda e: e.tensor_copy(wout_sib[:], wp3f[:]), inc=ve)

        HT = T // 2
        if P3_PARTS < 1:
            barrier()
            finish_zero(nc, ctx, outT, sb3)
            return
        cc_toks = []
        for hd in range(NHEADS_CORE):
            cc_toks.append(dma_branchy(
                "gpsimd",
                ccin_h[:, hd * 128:(hd + 1) * 128], attn_h[hd, HT:T, :],
                ccin_h[:, hd * 128:(hd + 1) * 128], attn_h[hd, 0:HT, :], qG + "c"))
        for t_ in cc_toks:
            ctx.waitt("gpsimd", t_)
        ccsem = ctx.sem("cc")
        groups = [[0, 1], [2, 3], [4, 5], [6, 7]]
        ctx.emit("gpsimd", lambda e: e.collective_compute(
            "AllGather", ALU.bypass, groups, [ccin_h.opt()], [ccout_h.opt()]).then_inc(ccsem, 1))
        ctx.bump("cc", 1)
        ctx.wait("gpsimd", "cc", 1)

        if P3_PARTS < 2:
            barrier()
            finish_zero(nc, ctx, outT, sb3)
            return
        at_toks = []
        for hd in range(NHEADS_CORE):
            at_toks.append(dma_branchy(
                "gpsimd",
                att_all[:, :, hd, :], attn_h[hd, 0:HT, :].rearrange("(tg p) d -> p tg d", p=128),
                att_all[:, :, hd, :], attn_h[hd, HT:T, :].rearrange("(tg p) d -> p tg d", p=128), qL))
        for hd in range(NHEADS_CORE):
            at_toks.append(dma_branchy(
                "gpsimd",
                att_all[:, :, 4 + hd, :],
                ccout_h[HT:T, hd * 128:(hd + 1) * 128].rearrange("(tg p) d -> p tg d", p=128),
                att_all[:, :, 4 + hd, :],
                ccout_h[0:HT, hd * 128:(hd + 1) * 128].rearrange("(tg p) d -> p tg d", p=128), qL))
        for t_ in at_toks:
            ctx.waitt("tensor", t_)

        if P3_PARTS < 3:
            barrier()
            finish_zero(nc, ctx, outT, sb3)
            return
        x1v = x1T_h.rearrange("(c p) t -> p c t", p=128)

        def ln_rows3(src_slab, slab_cols):
            c0 = ctx.cnt(ve)
            ctx.wait("scalar", ve, c0)
            ctx.wait("tensor", ve, c0)
            a_sq = ctx.op("scalar", lambda e: e.activation(out=sq3[:, :, 0:slab_cols], in_=src_slab,
                                                           func=AF.Square), inc=ac)
            for fc in range(8):
                ctx.op("tensor", lambda e, fc=fc: e.matmul(
                    PS_STAT3[0:1, 0:slab_cols], cst["ones128f"][:], src_slab[:, fc, 0:slab_cols],
                    start=(fc == 0), stop=(fc == 7)), inc=pe if fc == 7 else None)
            ctx.wait("tensor", ac, a_sq)
            for fc in range(8):
                ctx.op("tensor", lambda e, fc=fc: e.matmul(
                    PS_STAT3[0:1, 512:512 + slab_cols], cst["ones128f"][:], sq3[:, fc, 0:slab_cols],
                    start=(fc == 0), stop=(fc == 7)), inc=pe if fc == 7 else None)
            p_done = ctx.cnt(pe)
            ctx.wait("vector", pe, p_done)
            R = rowstat3[:].rearrange("a (k n) -> a k n", n=512)
            v1 = ctx.op("vector", lambda e: e.tensor_scalar(R[:, 0, 0:slab_cols], PS_STAT3[0:1, 0:slab_cols],
                                                            1.0 / E, None, op0=ALU.mult), inc=ve, psum=True)
            v2 = ctx.op("vector", lambda e: e.tensor_scalar(R[:, 1, 0:slab_cols], PS_STAT3[0:1, 512:512 + slab_cols],
                                                            1.0 / E, None, op0=ALU.mult), inc=ve, psum=True)
            ctx.wait("vector", ve, v2)
            v3 = ctx.op("vector", lambda e: e.tensor_tensor(R[:, 2, 0:slab_cols], R[:, 0, 0:slab_cols],
                                                            R[:, 0, 0:slab_cols], ALU.mult), inc=ve)
            ctx.wait("vector", ve, v3)
            v4 = ctx.op("vector", lambda e: e.tensor_tensor(R[:, 2, 0:slab_cols], R[:, 1, 0:slab_cols],
                                                            R[:, 2, 0:slab_cols], ALU.subtract), inc=ve)
            ctx.wait("scalar", ve, v4)
            a1 = ctx.op("scalar", lambda e: e.activation(out=R[:, 3, 0:slab_cols], in_=R[:, 2, 0:slab_cols],
                                                         func=AF.Ln, bias=cst["epscol"][0:1, 0:1]), inc=ac)
            ctx.wait("scalar", ac, a1)
            a2 = ctx.op("scalar", lambda e: e.activation(out=R[:, 1, 0:slab_cols], in_=R[:, 3, 0:slab_cols],
                                                         func=AF.Exp, scale=-0.5), inc=ac)
            ctx.wait("vector", ac, a2)
            v5 = ctx.op("vector", lambda e: e.tensor_tensor(R[:, 0, 0:slab_cols], R[:, 0, 0:slab_cols],
                                                            R[:, 1, 0:slab_cols], ALU.mult), inc=ve)
            ctx.wait("tensor", ve, v5)
            ctx.op("tensor", lambda e: e.matmul(PS_REPa[:, 0:slab_cols], cst["ones1x128f"][:],
                                                R[0:1, 1, 0:slab_cols], start=True, stop=True), inc=pe)
            pb = ctx.op("tensor", lambda e: e.matmul(PS_REPb[:, 0:slab_cols], cst["ones1x128f"][:],
                                                     R[0:1, 0, 0:slab_cols], start=True, stop=True), inc=pe)
            return pb

        SLAB3 = 512
        prev_wr = []
        for sidx in range(HT // SLAB3):
            c0 = ctx.cnt(ve)
            ctx.wait("tensor", ve, c0)
            ctx.wait("tensor", ac, ctx.cnt(ac))
            tr_evac = {}
            for idx3 in range(32):
                hdc, tt = idx3 // 4, idx3 % 4
                tg = sidx * 4 + tt
                tslot3 = idx3 % 4
                if idx3 >= 1:
                    ctx.wait("tensor", ve, tr_evac[idx3 - 1])
                ctx.op("tensor", lambda e, hdc=hdc, tg=tg, tslot3=tslot3: e.transpose(
                    PS_T3[:, tslot3 * 128:(tslot3 + 1) * 128],
                    att_all[:, tg, hdc, :], cst["ident128"][:]), inc=pe)
                p_tr = ctx.cnt(pe)
                if P3_PARTS == 31:
                    tr_evac[idx3] = ctx.op("vector", lambda e: e.memset(attnT[:, 0, 0:4], 0.0), inc=ve)
                    continue
                ctx.wait("vector", pe, p_tr)
                tr_evac[idx3] = ctx.op("vector", lambda e, hdc=hdc, tt=tt, tslot3=tslot3: e.tensor_copy(
                    attnT[:, hdc, tt * 128:(tt + 1) * 128],
                    PS_T3[:, tslot3 * 128:(tslot3 + 1) * 128]), inc=ve, psum=True)
            trd = ctx.cnt(ve)
            ctx.wait("tensor", ve, trd)
            if P3_PARTS < 4:
                continue
            ctx.wait("gpsimd", ve, ctx.cnt(ve))
            ctx.wait("gpsimd", ac, ctx.cnt(ac))
            lx1 = dma_branchy("gpsimd",
                              sq3[:], x1v[:, :, sidx * SLAB3:(sidx + 1) * SLAB3],
                              sq3[:], x1v[:, :, HT + sidx * SLAB3:HT + (sidx + 1) * SLAB3], qA)
            ctx.waitt("vector", lx1)
            if sidx == 0 and "dbg_x1r" in dbg:
                ctx.waitt("sync", lx1)
                ctx.dma("sync", dbg["dbg_x1r"][:].rearrange("p (c t) -> p c t", t=512), sq3[:], qS)
            ev_oc = {}
            for oc in range(8):
                slot = oc % 2
                if oc >= 2:
                    ctx.wait("tensor", ve, ev_oc[oc - 2])
                for hdc in range(8):
                    w = wout_own if hdc < 4 else wout_sib
                    ctx.op("tensor", lambda e, oc=oc, hdc=hdc, slot=slot, w=w: e.matmul(
                        PS_F3[:, slot * 512:slot * 512 + SLAB3],
                        w[:, hdc % 4, oc * 128:(oc + 1) * 128],
                        attnT[:, hdc, :],
                        start=(hdc == 0), stop=(hdc == 7)), inc=pe if hdc == 7 else None)
                pdone = ctx.cnt(pe)
                ctx.wait("vector", pe, pdone)
                if oc == 0:
                    ctx.wait("vector", ac, ctx.cnt(ac))
                e1 = ctx.op("vector", lambda e, oc=oc, slot=slot: e.tensor_scalar(
                    evac3[:, oc, :], PS_F3[:, slot * 512:slot * 512 + SLAB3],
                    boutc[:, oc:oc + 1], None, op0=ALU.add), inc=ve, psum=True)
                ev_oc[oc] = e1
                ctx.wait("vector", ve, e1)
                ctx.op("vector", lambda e, oc=oc: e.tensor_tensor(
                    x2slab[:, oc, :], sq3[:, oc, :], evac3[:, oc, :], ALU.add), inc=ve)
            if P3_PARTS < 5:
                continue
            if sidx == 0 and "dbg_x2" in dbg:
                dd = ctx.cnt(ve)
                ctx.wait("sync", ve, dd)
                ctx.dma("sync", dbg["dbg_x2"][:].rearrange("p (c t) -> p c t", t=512), x2slab[:], qS)
                ctx.dma("sync", dbg["dbg_attT"][:].rearrange("p (c t) -> p c t", t=512), attnT[:], qS)
            pb3 = ln_rows3(x2slab[:], SLAB3)
            ctx.wait("vector", pe, pb3)
            vA3 = ctx.op("vector", lambda e: e.tensor_tensor(
                sq3[:], x2slab[:],
                PS_REPa[:, 0:SLAB3].rearrange("p (c n) -> p c n", c=1).to_broadcast([128, 8, SLAB3]),
                ALU.mult), inc=ve, psum=True)
            ctx.wait("vector", ve, vA3)
            vB3 = ctx.op("vector", lambda e: e.tensor_tensor(
                xln3[:], sq3[:],
                PS_REPb[:, 0:SLAB3].rearrange("p (c n) -> p c n", c=1).to_broadcast([128, 8, SLAB3]),
                ALU.subtract), inc=ve, psum=True)
            ctx.wait("tensor", ve, vB3)
            ev2 = {}
            for oc in range(8):
                slot = oc % 2
                if oc >= 2:
                    ctx.wait("tensor", ve, ev2[oc - 2])
                for fc in range(8):
                    ctx.op("tensor", lambda e, fc=fc, oc=oc, slot=slot: e.matmul(
                        PS_F3[:, slot * 512:slot * 512 + SLAB3],
                        tern[:, fc, oc * 128:(oc + 1) * 128], xln3[:, fc, :],
                        start=(fc == 0), stop=(fc == 7)), inc=pe if fc == 7 else None)
                pdone = ctx.cnt(pe)
                ctx.wait("vector", pe, pdone)
                e2 = ctx.op("vector", lambda e, oc=oc, slot=slot: e.tensor_scalar(
                    evac3[:, oc, :], PS_F3[:, slot * 512:slot * 512 + SLAB3],
                    halfalpha[:, 0:1], halfbias[:, oc:oc + 1], op0=ALU.mult, op1=ALU.add), inc=ve, psum=True)
                ev2[oc] = e2
                ctx.wait("vector", ve, e2)
                ctx.op("vector", lambda e, oc=oc: e.tensor_tensor(
                    outslab[:, oc, :], x2slab[:, oc, :], evac3[:, oc, :], ALU.add), inc=ve)
            odone = ctx.cnt(ve)
            ctx.wait("sync", ve, odone)
            for tok in prev_wr:
                ctx.waitt("vector", tok)
            prev_wr = [ctx.dma("sync", outT.rearrange("(c p) t -> p c t", p=128)[:, :, sidx * SLAB3:(sidx + 1) * SLAB3],
                               outslab[:], qS)]
        for tok in prev_wr:
            ctx.waitt("sync", tok)
        if P3_PARTS < 5:
            barrier()
            finish_zero(nc, ctx, outT, sb3)
            return
        barrier(("sync",))


def finish_zero(nc, ctx, outT, sb):
    zrow = sb("zrow", [128, 8, 64], F32)
    vz = ctx.op("vector", lambda e: e.memset(zrow[:], 0.0), inc="ve")
    ctx.wait("sync", "ve", vz)
    toks = []
    for i in range(8):
        toks.append(ctx.dma("sync", outT.rearrange("(c p) t -> p c t", p=128)[:, :, i * 64:(i + 1) * 64], zrow[:], "qOut"))
    for t_ in toks:
        ctx.waitt("sync", t_)



def make_core_inputs(inputs, consts, c):
    b = c // 2
    hh = c % 2
    heads = list(range(4 * hh, 4 * hh + 4))
    x = np.asarray(inputs["L0_attn_x"], np.float32)
    w_qk = np.asarray(inputs["L0_attn_w_qk"], np.float32)
    w_v = np.asarray(inputs["L0_attn_w_v"], np.float32)
    w_out = np.asarray(inputs["L0_attn_w_out"], np.float32)
    b_out = np.asarray(inputs["L0_attn_b_out"], np.float32)
    rotations = np.asarray(inputs["L0_attn_rotations"], np.float32)
    fw = np.asarray(inputs["L0_ffn_weight"], np.float32)
    fa = np.asarray(inputs["L0_ffn_alpha"], np.float32)
    fb = np.asarray(inputs["L0_ffn_bias"], np.float32)

    d = dict(consts)
    d["xT"] = np.ascontiguousarray(x[b].T)
    d["wffnT"] = np.ascontiguousarray(fw.T)
    colsel = np.concatenate([np.arange(h * DH, (h + 1) * DH) for h in heads])
    d["wqkvT"] = np.ascontiguousarray(np.concatenate([w_qk.T[:, colsel], w_v.T[:, colsel]], axis=1))
    woutT = np.ascontiguousarray(w_out.T)
    d["woutT_own"] = np.ascontiguousarray(woutT[colsel, :])
    d["woutT_sibA"] = np.ascontiguousarray(woutT[0:512, :])
    d["woutT_sibB"] = np.ascontiguousarray(woutT[512:1024, :])
    d["boutRows"] = np.ascontiguousarray(b_out.reshape(8, 128).T)
    d["fbiasRows"] = np.ascontiguousarray(fb.reshape(8, 128).T)
    d["alpha"] = fa.reshape(1, 1)
    d["rotT"] = np.ascontiguousarray(rotations.reshape(DH, NH * 32))
    return d


def kernel(**inputs):
    consts = host_consts()
    nc = build_program()
    in_maps = [make_core_inputs(inputs, consts, c) for c in range(8)]
    res = run_bass_kernel_spmd(nc, in_maps, core_ids=list(range(8)))
    out = np.zeros((B, T, E), dtype=np.float32)
    for c in range(8):
        b = c // 2
        hh = c % 2
        o = res.results[c]["outT"]
        out[b, hh * (T // 2):(hh + 1) * (T // 2), :] = o.T
    return out


# version 05
# speedup vs baseline: 1.2818x; 1.0964x over previous
import numpy as np
import ml_dtypes
from contextlib import ExitStack

import concourse.bass as bass
import concourse.mybir as mybir
from concourse.bass_utils import run_bass_kernel_spmd

F32 = mybir.dt.float32
BF16 = mybir.dt.bfloat16
U32 = mybir.dt.uint32
I16 = mybir.dt.int16
AX = mybir.AxisListType
ALU = mybir.AluOpType
AF = mybir.ActivationFunctionType

E = 1024
T = 4096
B = 4
NH = 8
DH = 128
NB = 64
G = 32
NHEADS_CORE = 4
ROWL = 400
VOFF = 256
ONEOFF = 384
IDHI = 385
IDLO = 386
SBROWS = 7 * T + 4224
RSQD = float(DH) ** -0.5
P1B_TILES = G
P1B_HASH = True
P1B_PARTS = 9
ATT_HASHES = NH
ATT_PARTS = 9
NHEADS_RUN = NHEADS_CORE
P3_PARTS = 9


class Ctx:

    ENGS = ("sync", "scalar", "vector", "tensor", "gpsimd")

    def __init__(self, nc, st):
        self.nc = nc
        self.st = st
        self.steps = {e: [] for e in self.ENGS}
        self._semcnt = {}
        self._sems = {}
        self._roles = {}
        self.last_act_psum = 0
        self.last_dve_psum = 0
        self._parity_reg = {}

    def setup_parity(self, eng):
        cell = {}
        self._parity_reg[eng] = cell

        def go(e, cell=cell):
            pid = e.partition_id()
            pr = e.alloc_register()
            e.reg_mod(pr, pid, 2)
            cell["reg"] = pr
        self.emit(eng, go)

    def parity_reg(self, eng):
        return self._parity_reg[eng]

    def sem(self, name):
        if name not in self._sems:
            self._sems[name] = self.st.enter_context(self.nc.semaphore(name))
            self._semcnt[name] = 0
        return self._sems[name]

    def cnt(self, name):
        return self._semcnt[name]

    def bump(self, name, n):
        self._semcnt[name] += n
        return self._semcnt[name]

    def emit(self, eng, fn):
        self.steps[eng].append(fn)

    def wait(self, eng, sem_name, val):
        h = self.sem(sem_name)
        self.emit(eng, lambda e, h=h, v=val: e.wait_ge(h, v))

    def op(self, eng, fn, inc=None, n=1, psum=False):
        if psum:
            assert inc is not None
            if eng == "scalar" and self.last_dve_psum:
                self.wait(eng, "ve", self.last_dve_psum)
            if eng == "vector" and self.last_act_psum:
                self.wait(eng, "ac", self.last_act_psum)
        if inc is None:
            self.emit(eng, fn)
            return None
        h = self.sem(inc)
        v = self.bump(inc, n)
        self.emit(eng, lambda e, fn=fn, h=h, n=n: fn(e).then_inc(h, n))
        if psum:
            if eng == "scalar":
                self.last_act_psum = v
            elif eng == "vector":
                self.last_dve_psum = v
        return v

    NROT = 6

    def dma(self, eng, out_ap, in_ap, role, **kw):
        idx = self._roles.get(role, 0)
        self._roles[role] = (idx + 1) % self.NROT
        name = "%s_%d" % (role, idx)
        h = self.sem(name)
        prev = self._semcnt[name]
        if prev:
            self.emit(eng, lambda e, h=h, v=prev: e.wait_ge(h, v))
        v = self.bump(name, 16)
        self.emit(eng, lambda e, o=out_ap, i=in_ap, kw=kw, h=h: e.dma_start(out=o, in_=i, **kw).then_inc(h, 16))
        return (name, v)

    def waitt(self, eng, token):
        name, v = token
        self.wait(eng, name, v)

    def wait_role_all(self, eng, role):
        for idx in range(self.NROT):
            name = "%s_%d" % (role, idx)
            if name in self._semcnt and self._semcnt[name]:
                self.wait(eng, name, self._semcnt[name])

    def finalize(self, block):
        for name in self.ENGS:
            steps = self.steps[name]
            deco = getattr(block, name)

            def make(steps):
                def body(e):
                    for f in steps:
                        f(e)
                return body
            deco(make(steps))



def host_consts():
    c = {}
    c["slt128"] = (np.arange(128)[:, None] < np.arange(128)[None, :]).astype(ml_dtypes.bfloat16)
    c["slt32"] = (np.arange(32)[:, None] < np.arange(32)[None, :]).astype(np.float32)
    c["iota64"] = np.tile(np.arange(NB, dtype=np.float32)[None, :], (128, G))
    c["ones128b"] = np.ones((128, 1), dtype=ml_dtypes.bfloat16)
    c["ones128f"] = np.ones((128, 1), dtype=np.float32)
    c["ones32f"] = np.ones((32, 1), dtype=np.float32)
    c["ones1x128f"] = np.ones((1, 128), dtype=np.float32)
    c["ones1x128b"] = np.ones((1, 128), dtype=ml_dtypes.bfloat16)
    c["zeros1x64"] = np.zeros((1, 64), dtype=np.float32)
    c["epscol"] = np.full((128, 1), 1e-5, dtype=np.float32)
    c["ident128"] = np.eye(128, dtype=ml_dtypes.bfloat16)
    c["offs8"] = np.tile((np.arange(NH, dtype=np.float32) * 32)[None, :], (128, 1))
    pm = np.ones((128, 8, 64), dtype=ml_dtypes.bfloat16)
    for j in range(8):
        for i in range(64):
            pm[64 + i, j, i] = 0
    c["pmask8"] = pm.reshape(128, 512)
    tt = np.arange(T).reshape(G, 128).T
    c["idhi"] = (tt // 64).astype(np.float32)
    c["idlo"] = (tt % 64).astype(np.float32)
    return c


CONST_DTYPES = {
    "slt128": BF16, "slt32": F32, "iota64": F32, "ones128b": BF16,
    "ones128f": F32, "ones32f": F32, "ones1x128f": F32, "ones1x128b": BF16,
    "zeros1x64": F32, "epscol": F32, "ident128": BF16, "offs8": F32, "pmask8": BF16, "idhi": F32, "idlo": F32,
}




def build_program(debug=(), stage=5):
    nc = bass.Bass("TRN2", target_bir_lowering=False)

    ap = {}
    ap["xT"] = nc.dram_tensor("xT", [E, T], F32, kind="ExternalInput").ap()
    ap["wffnT"] = nc.dram_tensor("wffnT", [E, E], F32, kind="ExternalInput").ap()
    ap["wqkvT"] = nc.dram_tensor("wqkvT", [E, E], F32, kind="ExternalInput").ap()
    ap["woutT_own"] = nc.dram_tensor("woutT_own", [512, E], F32, kind="ExternalInput").ap()
    ap["woutT_sibA"] = nc.dram_tensor("woutT_sibA", [512, E], F32, kind="ExternalInput").ap()
    ap["woutT_sibB"] = nc.dram_tensor("woutT_sibB", [512, E], F32, kind="ExternalInput").ap()
    ap["boutRows"] = nc.dram_tensor("boutRows", [128, 8], F32, kind="ExternalInput").ap()
    ap["fbiasRows"] = nc.dram_tensor("fbiasRows", [128, 8], F32, kind="ExternalInput").ap()
    ap["alpha"] = nc.dram_tensor("alpha", [1, 1], F32, kind="ExternalInput").ap()
    ap["rotT"] = nc.dram_tensor("rotT", [128, 256], F32, kind="ExternalInput").ap()
    shapes = {"slt128": [128, 128], "slt32": [32, 32], "iota64": [128, G * NB],
              "ones128b": [128, 1], "ones128f": [128, 1], "ones32f": [32, 1],
              "ones1x128f": [1, 128], "ones1x128b": [1, 128], "zeros1x64": [1, 64],
              "epscol": [128, 1], "ident128": [128, 128],
              "offs8": [128, 8], "pmask8": [128, 512], "idhi": [128, G], "idlo": [128, G]}
    for k, dt in CONST_DTYPES.items():
        ap[k] = nc.dram_tensor(k, shapes[k], dt, kind="ExternalInput").ap()

    outT = nc.dram_tensor("outT", [E, T // 2], F32, kind="ExternalOutput").ap()
    dbg = {}
    if "x1" in debug:
        dbg["dbg_x1"] = nc.dram_tensor("dbg_x1", [E, T], F32, kind="ExternalOutput").ap()
    if "xa" in debug:
        dbg["dbg_xa"] = nc.dram_tensor("dbg_xa", [E, T], BF16, kind="ExternalOutput").ap()
    if "payload" in debug:
        dbg["dbg_payload"] = nc.dram_tensor("dbg_payload", [128, G * ROWL], BF16, kind="ExternalOutput").ap()
    if "buckets" in debug:
        dbg["dbg_buckets"] = nc.dram_tensor("dbg_buckets", [128, G * NH], F32, kind="ExternalOutput").ap()
    if "pos" in debug:
        dbg["dbg_pos"] = nc.dram_tensor("dbg_pos", [128, G * NH], U32, kind="ExternalOutput").ap()
    if "attn" in debug:
        dbg["dbg_attn"] = nc.dram_tensor("dbg_attn", [128, G * DH], BF16, kind="ExternalOutput").ap()
    if "p3" in debug:
        dbg["dbg_x1r"] = nc.dram_tensor("dbg_x1r", [128, 8 * 512], F32, kind="ExternalOutput").ap()
        dbg["dbg_x2"] = nc.dram_tensor("dbg_x2", [128, 8 * 512], F32, kind="ExternalOutput").ap()
        dbg["dbg_attT"] = nc.dram_tensor("dbg_attT", [128, 8 * 512], BF16, kind="ExternalOutput").ap()
        dbg["dbg_x1h"] = nc.dram_tensor("dbg_x1h", [E, 512], F32, kind="ExternalOutput").ap()

    x1T_h = nc.dram_tensor("x1T_h", [E, T], F32, kind="Internal").ap()
    xaT_h = nc.dram_tensor("xaT_h", [E, T], BF16, kind="Internal").ap()
    sbuf_h = nc.dram_tensor("sbuf_h", [SBROWS, ROWL], BF16, kind="Internal").ap()
    bo_h = nc.dram_tensor("bo_h", [T, 132], F32, kind="Internal").ap()
    ccin_h = nc.dram_tensor("ccin_h", [T // 2, 512], BF16, kind="Internal").ap()
    ccout_h = nc.dram_tensor("ccout_h", [T, 512], BF16, kind="Internal").ap()
    attn_h = nc.dram_tensor("attn_h", [NHEADS_CORE, T, DH], BF16, kind="Internal").ap()
    sort_scr1 = nc.dram_tensor("sort_scr1", [G * NB], F32, kind="Internal").ap()
    sort_scr2 = nc.dram_tensor("sort_scr2", [G * NB], F32, kind="Internal").ap()

    st = ExitStack()
    ctx = Ctx(nc, st)

    def sb(name, shape, dt):
        return st.enter_context(nc.sbuf_tensor(name, shape, dt))

    def psum(name, shape, dt=F32):
        return st.enter_context(nc.psum_tensor(name, shape, dt))

    with st:
        build_body(nc, ctx, ap, outT, dbg, x1T_h, xaT_h, sbuf_h, bo_h, ccin_h, ccout_h,
                   attn_h, sort_scr1, sort_scr2, sb, psum, debug, stage)
        block = st.enter_context(nc.Block())
        ctx.finalize(block)
    return nc


def build_body(nc, ctx, ap, outT, dbg, x1T_h, xaT_h, sbuf_h, bo_h, ccin_h, ccout_h, attn_h,
               sort_scr1, sort_scr2, sb, psum, debug, stage):
    from contextlib import ExitStack

    tern = sb("tern", [128, 8, 1024], BF16)
    wqkv = sb("wqkv", [128, 8, 1024], BF16)
    rotb = sb("rotb", [128, 256], BF16)
    cst = {}
    for k in CONST_DTYPES:
        cst[k] = sb("c_" + k, list(ap[k].shape), CONST_DTYPES[k])
    halfbias = sb("halfbias", [128, 8], F32)
    boutc = sb("boutc", [128, 8], F32)
    halfalpha = sb("halfalpha", [128, 1], F32)
    deltas = sb("deltas", [128, 8], F32)

    qL, qA, qG, pe, ve, ac, qS = "qL", "qA", "qG", "pe", "ve", "ac", "qS"
    ctx.setup_parity("gpsimd")

    def barrier(engines=("sync", "scalar", "vector", "tensor", "gpsimd")):
        for sem in (qG, pe, ve, ac):
            v = ctx.cnt(sem) if sem in ctx._semcnt else 0
            if v:
                for e in engines:
                    ctx.wait(e, sem, v)
        for role in (qL, qA, qS, "qDg"):
            for e in engines:
                ctx.wait_role_all(e, role)

    with ExitStack() as g1:
        def sb1(name, shape, dt):
            return g1.enter_context(nc.sbuf_tensor(name, shape, dt))

        def ps1(name, shape, dt=F32):
            return g1.enter_context(nc.psum_tensor(name, shape, dt))

        wf32 = sb1("wf32", [128, 8, 1024], F32)
        tpos = sb1("tpos", [128, 8, 1024], BF16)
        rotf = sb1("rotf", [128, 256], F32)
        xslab = sb1("xslab", [128, 8, 512], F32)
        sq = sb1("sq", [128, 8, 512], F32)
        xln = sb1("xln", [128, 8, 512], BF16)
        x1slab = sb1("x1slab", [128, 8, 512], F32)
        rowstat = sb1("rowstat", [1, 8 * 512], F32)
        evac = sb1("evac", [128, 8, 512], F32)
        PS_STAT = ps1("ps_stat", [1, 1024])
        PS_REP = ps1("ps_rep", [128, 512])
        PS_REP2 = ps1("ps_rep2", [128, 512])
        PS_F = ps1("ps_f", [128, 1024])
        PS_M = ps1("ps_m", [128, 512])

        for k in CONST_DTYPES:
            ctx.dma("sync", cst[k][:], ap[k][:], qL)
        ctx.dma("sync", rotf[:], ap["rotT"][:], qL)
        ctx.dma("sync", halfbias[:], ap["fbiasRows"][:], qL)
        ctx.dma("sync", boutc[:], ap["boutRows"][:], qL)
        ctx.dma("sync", wf32[:], ap["wffnT"].rearrange("(c p) o -> p c o", p=128), qL)

        ctx.wait_role_all("vector", qL)
        ctx.op("vector", lambda e: e.tensor_copy(rotb[:], rotf[:]), inc=ve)
        ctx.op("vector", lambda e: e.tensor_scalar(halfbias[:], halfbias[:], 0.5, None, op0=ALU.mult), inc=ve)

        v_abs = ctx.op("vector", lambda e: e.tensor_reduce(
            out=deltas[:, 0:1].rearrange("p (a o) -> p a o", a=1),
            in_=wf32[:].rearrange("p c o -> p (c o)").rearrange("p (a n) -> p a n", a=1),
            axis=AX.X, op=ALU.add, apply_absolute_value=True), inc=ve)
        ctx.wait("tensor", ve, v_abs)
        p_sum = ctx.op("tensor", lambda e: e.matmul(PS_M[0:1, 0:1], deltas[:, 0:1], cst["ones128f"][:],
                                                    start=True, stop=True), inc=pe)
        ctx.wait("vector", pe, p_sum)
        v_del = ctx.op("vector", lambda e: e.tensor_scalar(deltas[0:1, 1:2], PS_M[0:1, 0:1],
                                                           0.7 / (E * E), None, op0=ALU.mult), inc=ve, psum=True)
        ctx.wait("tensor", ve, v_del)
        p_drep = ctx.op("tensor", lambda e: e.matmul(PS_M[:, 1:2], cst["ones1x128f"][:], deltas[0:1, 1:2],
                                                     start=True, stop=True), inc=pe)
        ctx.wait("vector", pe, p_drep)
        v_dc = ctx.op("vector", lambda e: e.tensor_copy(deltas[:, 0:1], PS_M[:, 1:2]), inc=ve, psum=True)
        ctx.wait("vector", ve, v_dc)
        v_dn = ctx.op("vector", lambda e: e.tensor_scalar(deltas[:, 1:2], deltas[:, 0:1], -1.0, None, op0=ALU.mult), inc=ve)
        ctx.wait("vector", ve, v_dn)
        v_t1 = ctx.op("vector", lambda e: e.tensor_scalar(tpos[:], wf32[:], deltas[:, 0:1], None, op0=ALU.is_gt), inc=ve)
        v_t2 = ctx.op("vector", lambda e: e.tensor_scalar(tern[:], wf32[:], deltas[:, 1:2], None, op0=ALU.is_lt), inc=ve)
        ctx.wait("vector", ve, v_t2)
        v_t3 = ctx.op("vector", lambda e: e.tensor_tensor(tern[:], tpos[:], tern[:], ALU.subtract), inc=ve)

        ctx.wait("sync", ve, v_t3)
        l_wq = ctx.dma("sync", wf32[:], ap["wqkvT"].rearrange("(c p) o -> p c o", p=128), qL)
        ctx.waitt("vector", l_wq)
        v_wq = ctx.op("vector", lambda e: e.tensor_copy(wqkv[:], wf32[:]), inc=ve)
        l_al = ctx.dma("sync", deltas[0:1, 3:4], ap["alpha"][:], qL)
        ctx.waitt("tensor", l_al)
        ctx.wait("tensor", ve, v_wq)
        p_arep = ctx.op("tensor", lambda e: e.matmul(PS_M[:, 2:3], cst["ones1x128f"][:], deltas[0:1, 3:4],
                                                     start=True, stop=True), inc=pe)
        ctx.wait("vector", pe, p_arep)
        ctx.op("vector", lambda e: e.tensor_scalar(halfalpha[:], PS_M[:, 2:3], 0.5, None, op0=ALU.mult), inc=ve, psum=True)

        def ln_rows(src_slab, slab_cols):
            c0 = ctx.cnt(ve)
            ctx.wait("scalar", ve, c0)
            ctx.wait("tensor", ve, c0)
            a_sq = ctx.op("scalar", lambda e: e.activation(out=sq[:, :, 0:slab_cols], in_=src_slab,
                                                           func=AF.Square), inc=ac)
            for fc in range(8):
                ctx.op("tensor", lambda e, fc=fc: e.matmul(
                    PS_STAT[0:1, 0:slab_cols], cst["ones128f"][:], src_slab[:, fc, 0:slab_cols],
                    start=(fc == 0), stop=(fc == 7)), inc=pe if fc == 7 else None)
            ctx.wait("tensor", ac, a_sq)
            for fc in range(8):
                ctx.op("tensor", lambda e, fc=fc: e.matmul(
                    PS_STAT[0:1, 512:512 + slab_cols], cst["ones128f"][:], sq[:, fc, 0:slab_cols],
                    start=(fc == 0), stop=(fc == 7)), inc=pe if fc == 7 else None)
            p_done = ctx.cnt(pe)
            ctx.wait("vector", pe, p_done)
            R = rowstat[:].rearrange("a (k n) -> a k n", n=512)
            v1 = ctx.op("vector", lambda e: e.tensor_scalar(R[:, 0, 0:slab_cols], PS_STAT[0:1, 0:slab_cols],
                                                            1.0 / E, None, op0=ALU.mult), inc=ve, psum=True)
            v2 = ctx.op("vector", lambda e: e.tensor_scalar(R[:, 1, 0:slab_cols], PS_STAT[0:1, 512:512 + slab_cols],
                                                            1.0 / E, None, op0=ALU.mult), inc=ve, psum=True)
            ctx.wait("vector", ve, v2)
            v3 = ctx.op("vector", lambda e: e.tensor_tensor(R[:, 2, 0:slab_cols], R[:, 0, 0:slab_cols],
                                                            R[:, 0, 0:slab_cols], ALU.mult), inc=ve)
            ctx.wait("vector", ve, v3)
            v4 = ctx.op("vector", lambda e: e.tensor_tensor(R[:, 2, 0:slab_cols], R[:, 1, 0:slab_cols],
                                                            R[:, 2, 0:slab_cols], ALU.subtract), inc=ve)
            ctx.wait("scalar", ve, v4)
            a1 = ctx.op("scalar", lambda e: e.activation(out=R[:, 3, 0:slab_cols], in_=R[:, 2, 0:slab_cols],
                                                         func=AF.Ln, bias=cst["epscol"][0:1, 0:1]), inc=ac)
            ctx.wait("scalar", ac, a1)
            a2 = ctx.op("scalar", lambda e: e.activation(out=R[:, 1, 0:slab_cols], in_=R[:, 3, 0:slab_cols],
                                                         func=AF.Exp, scale=-0.5), inc=ac)
            ctx.wait("vector", ac, a2)
            v5 = ctx.op("vector", lambda e: e.tensor_tensor(R[:, 0, 0:slab_cols], R[:, 0, 0:slab_cols],
                                                            R[:, 1, 0:slab_cols], ALU.mult), inc=ve)
            ctx.wait("tensor", ve, v5)
            pa = ctx.op("tensor", lambda e: e.matmul(PS_REP[:, 0:slab_cols], cst["ones1x128f"][:],
                                                     R[0:1, 1, 0:slab_cols], start=True, stop=True), inc=pe)
            pb = ctx.op("tensor", lambda e: e.matmul(PS_REP2[:, 0:slab_cols], cst["ones1x128f"][:],
                                                     R[0:1, 0, 0:slab_cols], start=True, stop=True), inc=pe)
            return pb

        SLAB = 512
        prev_w = []
        for s in range(T // SLAB):
            for tok in prev_w:
                ctx.waitt("vector", tok)
            prev_w = []
            ld = ctx.dma("sync", xslab[:],
                         ap["xT"].rearrange("(c p) t -> p c t", p=128)[:, :, s * SLAB:(s + 1) * SLAB], qL)
            ctx.waitt("scalar", ld)
            ctx.waitt("vector", ld)
            ctx.waitt("tensor", ld)
            pb = ln_rows(xslab[:], SLAB)
            ctx.wait("vector", pe, pb)
            vA = ctx.op("vector", lambda e: e.tensor_tensor(
                sq[:], xslab[:],
                PS_REP[:, 0:SLAB].rearrange("p (c n) -> p c n", c=1).to_broadcast([128, 8, SLAB]),
                ALU.mult), inc=ve, psum=True)
            ctx.wait("vector", ve, vA)
            vB = ctx.op("vector", lambda e: e.tensor_tensor(
                xln[:], sq[:],
                PS_REP2[:, 0:SLAB].rearrange("p (c n) -> p c n", c=1).to_broadcast([128, 8, SLAB]),
                ALU.subtract), inc=ve, psum=True)
            ctx.wait("tensor", ve, vB)
            ev_by_oc = {}
            for oc in range(8):
                slot = oc % 2
                if oc >= 2:
                    ctx.wait("tensor", ve, ev_by_oc[oc - 2])
                for fc in range(8):
                    ctx.op("tensor", lambda e, fc=fc, oc=oc, slot=slot: e.matmul(
                        PS_F[:, slot * 512:slot * 512 + SLAB],
                        tern[:, fc, oc * 128:(oc + 1) * 128], xln[:, fc, :],
                        start=(fc == 0), stop=(fc == 7)), inc=pe if fc == 7 else None)
                pdone = ctx.cnt(pe)
                ctx.wait("vector", pe, pdone)
                ev = ctx.op("vector", lambda e, oc=oc, slot=slot: e.tensor_scalar(
                    evac[:, oc, :], PS_F[:, slot * 512:slot * 512 + SLAB],
                    halfalpha[:, 0:1], halfbias[:, oc:oc + 1], op0=ALU.mult, op1=ALU.add), inc=ve, psum=True)
                ev_by_oc[oc] = ev
                ctx.wait("vector", ve, ev)
                ctx.op("vector", lambda e, oc=oc: e.tensor_tensor(
                    x1slab[:, oc, :], xslab[:, oc, :], evac[:, oc, :], ALU.add), inc=ve)
            x1done = ctx.cnt(ve)
            ctx.wait("sync", ve, x1done)
            prev_w.append(ctx.dma("sync", x1T_h.rearrange("(c p) t -> p c t", p=128)[:, :, s * SLAB:(s + 1) * SLAB], x1slab[:], qS))
            if "dbg_x1" in dbg:
                prev_w.append(ctx.dma("sync", dbg["dbg_x1"].rearrange("(c p) t -> p c t", p=128)[:, :, s * SLAB:(s + 1) * SLAB], x1slab[:], qS))
            pb2 = ln_rows(x1slab[:], SLAB)
            ctx.wait("vector", pe, pb2)
            vA2 = ctx.op("vector", lambda e: e.tensor_tensor(
                sq[:], x1slab[:],
                PS_REP[:, 0:SLAB].rearrange("p (c n) -> p c n", c=1).to_broadcast([128, 8, SLAB]),
                ALU.mult), inc=ve, psum=True)
            ctx.wait("vector", ve, vA2)
            vB2 = ctx.op("vector", lambda e: e.tensor_tensor(
                xln[:], sq[:],
                PS_REP2[:, 0:SLAB].rearrange("p (c n) -> p c n", c=1).to_broadcast([128, 8, SLAB]),
                ALU.subtract), inc=ve, psum=True)
            ctx.wait("sync", ve, vB2)
            prev_w.append(ctx.dma("sync", xaT_h.rearrange("(c p) t -> p c t", p=128)[:, :, s * SLAB:(s + 1) * SLAB], xln[:], qS))
            if "dbg_xa" in dbg:
                prev_w.append(ctx.dma("sync", dbg["dbg_xa"].rearrange("(c p) t -> p c t", p=128)[:, :, s * SLAB:(s + 1) * SLAB], xln[:], qS))
            ctx.wait("sync", ve, ctx.cnt(ve))
        barrier()

    if stage < 2:
        finish_zero(nc, ctx, outT, sb)
        return

    with ExitStack() as g2:
        def sb2(name, shape, dt):
            return g2.enter_context(nc.sbuf_tensor(name, shape, dt))

        def ps2(name, shape, dt=F32):
            return g2.enter_context(nc.psum_tensor(name, shape, dt))

        payload = sb2("payload", [128, G, ROWL], BF16)
        buckets = sb2("buckets", [128, G, NH], F32)
        xat = sb2("xat", [128, 8, 2 * 128], BF16)
        qkTt = sb2("qkTt", [128, 2 * 128], BF16)
        u1 = sb2("u1", [128, G, 264], BF16)
        rn_all = sb2("rn_all", [128, G], F32)
        rn2 = sb2("rn2", [128, G], F32)
        rn3 = sb2("rn3", [128, G], F32)
        amax_all = sb2("amax_all", [128, 5 * G * NH], F32)
        aidx_all = sb2("aidx_all", [128, 2 * G * NH], U32)
        acond_all = sb2("acond_all", [128, G * NH], U32)
        onehot = sb2("onehot", [128, G * NB], BF16)
        ttrs = sb2("ttrs", [128, G * NB], F32)
        crow = sb2("crow", [1, G * NB], F32)
        ctT = sb2("ctT", [32, NB], F32)
        srow = sb2("srow", [1, 4 * NB], F32)
        ccs_row = sb2("ccs_row", [1, G * NB], F32)
        posf = sb2("posf", [128, G], F32)
        posfw = sb2("posfw", [128, G, NH], U32)
        posbo = sb2("posbo", [128, G, NH], U32)
        qkT_s = sb2("qkT_s", [128, 4224], BF16)
        knT_s = sb2("knT_s", [128, 4224], BF16)
        vt = sb2("vt", [128, 33, 132], BF16)
        vt2 = sb2("vt2", [128, 32, 132], BF16)
        PT = sb2("PT", [128, 512], BF16)
        PT2 = sb2("PT2", [128, 512], BF16)
        qidr = sb2("qidr", [1, 128], BF16)
        bmask = sb2("bmask", [128, 64], BF16)
        bm1 = sb2("bm1", [128, 64], F32)
        bm2 = sb2("bm2", [128, 64], F32)
        staging = sb2("staging", [128, G, 132], F32)
        acc = sb2("acc", [128, G, 132], F32)
        rcp = sb2("rcp", [128, G], F32)
        PA = ps2("pa", [128, 2048])
        PB = ps2("pb", [128, 1024])
        PC = ps2("pc", [128, 512])
        PD = ps2("pd", [128, 256], BF16)

        vz = ctx.op("vector", lambda e: e.memset(staging[:, :, 129:132], 0.0), inc=ve)
        ctx.op("vector", lambda e: e.memset(payload[:, :, IDLO + 1:ROWL], 0.0), inc=ve)
        ctx.op("vector", lambda e: e.memset(payload[:, :, ONEOFF:ONEOFF + 1], 1.0), inc=ve)
        ctx.op("vector", lambda e: e.tensor_copy(payload[:, :, IDHI:IDHI + 1],
                                                 cst["idhi"][:].rearrange("p (g o) -> p g o", o=1)), inc=ve)
        vlast = ctx.op("vector", lambda e: e.tensor_copy(payload[:, :, IDLO:IDLO + 1],
                                                         cst["idlo"][:].rearrange("p (g o) -> p g o", o=1)), inc=ve)

        def build_head_payload(hd):
            xaT_t = xaT_h.rearrange("(c p) t -> p c t", p=128)
            war_ve, war_ac, war_tr, war_rot = {}, {}, {}, {}
            for g in range(P1B_TILES):
                dbuf = g % 2
                if g >= 2:
                    ctx.wait("sync", ve, war_rot[g - 2])
                ld = ctx.dma("sync", xat[:, :, dbuf * 128:(dbuf + 1) * 128],
                             xaT_t[:, :, g * 128:(g + 1) * 128], qA)
                ctx.waitt("tensor", ld)
                slot = g % 2
                if g >= 2:
                    ctx.wait("tensor", ve, war_ve[g - 2])
                    ctx.wait("tensor", ac, war_ac[g - 2])
                for fc in range(8):
                    ctx.op("tensor", lambda e, fc=fc, slot=slot, dbuf=dbuf, hd=hd: e.matmul(
                        PB[:, slot * 256:slot * 256 + 128],
                        xat[:, fc, dbuf * 128:(dbuf + 1) * 128],
                        wqkv[:, fc, hd * 128:(hd + 1) * 128],
                        start=(fc == 0), stop=(fc == 7)), inc=None)
                for fc in range(8):
                    ctx.op("tensor", lambda e, fc=fc, slot=slot, dbuf=dbuf, hd=hd: e.matmul(
                        PB[:, slot * 256 + 128:slot * 256 + 256],
                        xat[:, fc, dbuf * 128:(dbuf + 1) * 128],
                        wqkv[:, fc, 512 + hd * 128:512 + (hd + 1) * 128],
                        start=(fc == 0), stop=(fc == 7)), inc=pe if fc == 7 else None)
                pq = ctx.cnt(pe)
                ctx.wait("scalar", pe, pq)
                aq = ctx.op("scalar", lambda e, g=g, slot=slot: e.activation(
                    out=payload[:, g, 0:128], in_=PB[:, slot * 256:slot * 256 + 128],
                    func=AF.Copy), inc=ac, psum=True)
                ctx.wait("vector", pe, pq)
                vv = ctx.op("vector", lambda e, g=g, slot=slot: e.tensor_copy(
                    payload[:, g, VOFF:VOFF + 128], PB[:, slot * 256 + 128:slot * 256 + 256]), inc=ve, psum=True)
                tslot = g % 2
                if g >= 2:
                    ctx.wait("tensor", ve, war_tr[g - 2])
                ctx.wait("tensor", ac, aq)
                p_tr = ctx.op("tensor", lambda e, g=g, tslot=tslot: e.transpose(
                    PD[:, tslot * 128:(tslot + 1) * 128], payload[:, g, 0:128], cst["ident128"][:]), inc=pe)
                ctx.wait("vector", pe, p_tr)
                v_tr = ctx.op("vector", lambda e, tslot=tslot: e.tensor_copy(
                    qkTt[:, tslot * 128:(tslot + 1) * 128], PD[:, tslot * 128:(tslot + 1) * 128]), inc=ve, psum=True)
                war_tr[g] = v_tr
                ctx.wait("tensor", ve, v_tr)
                if g >= 2:
                    ctx.wait("tensor", ve, war_rot[g - 2])
                p_rot = ctx.op("tensor", lambda e, tslot=tslot: e.matmul(
                    PA[:, tslot * 256:(tslot + 1) * 256], qkTt[:, tslot * 128:(tslot + 1) * 128],
                    rotb[:], start=True, stop=True), inc=pe)
                ctx.wait("vector", pe, p_rot)
                v_re = ctx.op("vector", lambda e, g=g, tslot=tslot: e.tensor_copy(
                    u1[:, g, 0:256], PA[:, tslot * 256:(tslot + 1) * 256]), inc=ve, psum=True)
                war_rot[g] = v_re
                war_ve[g] = vv
                war_ac[g] = aq
            cA = ctx.cnt(ve)
            ctx.wait("vector", ve, cA)
            ctx.wait("vector", ac, ctx.cnt(ac))
            for half in range(2):
                hs = half * (G // 2)
                v_sq = ctx.op("vector", lambda e, hs=hs: e.tensor_tensor(
                    ttrs[:].rearrange("p (g d) -> p g d", d=128),
                    payload[:, hs:hs + G // 2, 0:128], payload[:, hs:hs + G // 2, 0:128], ALU.mult), inc=ve)
                ctx.wait("vector", ve, v_sq)
                v_ss = ctx.op("vector", lambda e, hs=hs: e.tensor_reduce(
                    out=rn_all[:, hs:hs + G // 2].rearrange("p (g o) -> p g o", o=1),
                    in_=ttrs[:].rearrange("p (g d) -> p g d", d=128),
                    axis=AX.X, op=ALU.add), inc=ve)
                ctx.wait("vector", ve, v_ss)
            v_cl = ctx.op("vector", lambda e: e.tensor_scalar(rn2[:], rn_all[:], 1e-24, None, op0=ALU.max), inc=ve)
            ctx.wait("scalar", ve, v_cl)
            a_ln = ctx.op("scalar", lambda e: e.activation(out=rn_all[:], in_=rn2[:], func=AF.Ln), inc=ac)
            ctx.wait("scalar", ac, a_ln)
            a_ex = ctx.op("scalar", lambda e: e.activation(out=rn3[:], in_=rn_all[:], func=AF.Exp, scale=-0.5), inc=ac)
            ctx.wait("vector", ac, a_ex)
            v_n1 = ctx.op("vector", lambda e: e.tensor_tensor(rn_all[:], rn3[:], rn3[:], ALU.mult), inc=ve)
            ctx.wait("vector", ve, v_n1)
            v_n2 = ctx.op("vector", lambda e: e.tensor_tensor(rn_all[:], rn_all[:], rn2[:], ALU.mult), inc=ve)
            ctx.wait("vector", ve, v_n2)
            v_n3 = ctx.op("vector", lambda e: e.tensor_scalar(rn_all[:], rn_all[:], -0.5, 1.5, op0=ALU.mult, op1=ALU.add), inc=ve)
            ctx.wait("vector", ve, v_n3)
            v_n4 = ctx.op("vector", lambda e: e.tensor_tensor(rn2[:], rn3[:], rn_all[:], ALU.mult), inc=ve)
            ctx.wait("vector", ve, v_n4)
            v_kn = ctx.op("vector", lambda e: e.tensor_tensor(
                payload[:, :, 128:256], payload[:, :, 0:128],
                rn2[:].rearrange("p (g o) -> p g o", o=1).to_broadcast([128, G, 128]), ALU.mult), inc=ve)
            R4 = u1[:, :, 0:256].rearrange("p g (h r) -> p g h r", r=32)
            v1 = ctx.op("vector", lambda e: e.tensor_reduce(
                out=amax_all[:, 0:G * NH].rearrange("p (g h o) -> p g h o", h=NH, o=1),
                in_=R4, axis=AX.X, op=ALU.max), inc=ve)
            v2 = ctx.op("vector", lambda e: e.tensor_reduce(
                out=amax_all[:, G * NH:2 * G * NH].rearrange("p (g h o) -> p g h o", h=NH, o=1),
                in_=R4, axis=AX.X, op=ALU.min), inc=ve)
            ctx.wait("vector", ve, v2)
            for g in range(P1B_TILES):
                ctx.op("vector", lambda e, g=g: e.max_index(
                    aidx_all[:, g * NH:(g + 1) * NH], amax_all[:, g * NH:(g + 1) * NH],
                    u1[:, g, 0:256]), inc=ve)
                ctx.op("vector", lambda e, g=g: e.max_index(
                    aidx_all[:, G * NH + g * NH:G * NH + (g + 1) * NH],
                    amax_all[:, G * NH + g * NH:G * NH + (g + 1) * NH],
                    u1[:, g, 0:256]), inc=ve)
            mi = ctx.cnt(ve)
            ctx.wait("vector", ve, mi)
            vc1 = ctx.op("vector", lambda e: e.tensor_copy(amax_all[:, 2 * G * NH:3 * G * NH], aidx_all[:, 0:G * NH]), inc=ve)
            vc2 = ctx.op("vector", lambda e: e.tensor_copy(amax_all[:, 3 * G * NH:4 * G * NH], aidx_all[:, G * NH:2 * G * NH]), inc=ve)
            v7 = ctx.op("vector", lambda e: e.tensor_tensor(amax_all[:, 4 * G * NH:5 * G * NH],
                                                            amax_all[:, 0:G * NH], amax_all[:, G * NH:2 * G * NH], ALU.add), inc=ve)
            ctx.wait("vector", ve, v7)
            v8 = ctx.op("vector", lambda e: e.tensor_scalar(acond_all[:], amax_all[:, 4 * G * NH:5 * G * NH],
                                                            0.0, None, op0=ALU.is_ge), inc=ve)
            v9 = ctx.op("vector", lambda e: e.tensor_scalar(amax_all[:, 3 * G * NH:4 * G * NH],
                                                            amax_all[:, 3 * G * NH:4 * G * NH], 32.0, None, op0=ALU.add), inc=ve)
            ctx.wait("vector", ve, v9)
            v10 = ctx.op("vector", lambda e: e.tensor_copy(
                buckets[:].rearrange("p g h -> p (g h)"), amax_all[:, 3 * G * NH:4 * G * NH]), inc=ve)
            ctx.wait("vector", ve, v10)
            v11 = ctx.op("vector", lambda e: e.copy_predicated(
                buckets[:].rearrange("p g h -> p (g h)"), acond_all[:], amax_all[:, 2 * G * NH:3 * G * NH]), inc=ve)
            ctx.wait("vector", ve, v11)
            v12 = ctx.op("vector", lambda e: e.tensor_tensor(
                buckets[:], buckets[:],
                cst["offs8"][:].rearrange("p (o h) -> p o h", o=1).to_broadcast([128, G, NH]), ALU.subtract), inc=ve)
            ctx.wait("vector", ve, v12)
            ctx.op("vector", lambda e: e.tensor_scalar(
                buckets[:].rearrange("p g h -> p (g h)"), buckets[:].rearrange("p g h -> p (g h)"),
                0.0, 63.0, op0=ALU.max, op1=ALU.min), inc=ve)

        def sort_hash(hd, h):
            ctx.wait("vector", pe, ctx.cnt(pe))
            ctx.wait("vector", ve, ctx.cnt(ve))
            ctx.wait("vector", ac, ctx.cnt(ac))
            buck = buckets[:, :, h:h + 1].rearrange("p g o -> p (g o)")
            v_oh = ctx.op("vector", lambda e: e.tensor_tensor(
                onehot[:].rearrange("p (g k) -> p g k", k=NB), cst["iota64"][:].rearrange("p (g k) -> p g k", k=NB),
                buckets[:, :, h:h + 1].to_broadcast([128, G, NB]), ALU.is_equal), inc=ve)
            ctx.wait("tensor", ve, v_oh)
            for i in range(4):
                mm = ctx.op("tensor", lambda e, i=i: e.matmul(
                    PA[0:1, i * 512:(i + 1) * 512], cst["ones128b"][:], onehot[:, i * 512:(i + 1) * 512],
                    start=True, stop=True), inc=pe if i == 3 else None)
            ctx.wait("vector", pe, ctx.cnt(pe))
            v_cr = ctx.op("vector", lambda e: e.tensor_copy(crow[:], PA[0:1, :]), inc=ve, psum=True)
            ctx.wait("sync", ve, v_cr)
            d1 = ctx.dma("sync", sort_scr1.rearrange("(o n) -> o n", o=1), crow[:], "qDg")
            ctx.waitt("sync", d1)
            d2 = ctx.dma("sync", ctT[:], sort_scr1.rearrange("(g k) -> g k", g=G), "qDg")
            ctx.waitt("tensor", d2)
            p_cc = ctx.op("tensor", lambda e: e.matmul(PB[0:32, 0:NB], cst["slt32"][:], ctT[:],
                                                       start=True, stop=False), inc=pe)
            p_tot = ctx.op("tensor", lambda e: e.matmul(PB[0:1, 512:512 + NB], cst["ones32f"][:], ctT[:],
                                                        start=True, stop=True), inc=pe)
            ctx.wait("vector", pe, p_tot)
            v_tr = ctx.op("vector", lambda e: e.tensor_copy(srow[:, 0:NB], PB[0:1, 512:512 + NB]), inc=ve, psum=True)
            ctx.wait("vector", ve, v_tr)
            v_sc = ctx.op("vector", lambda e: e.tensor_tensor_scan(
                srow[:, NB:2 * NB], srow[:, 0:NB], cst["zeros1x64"][:], 0.0, ALU.add, ALU.add), inc=ve)
            ctx.wait("vector", ve, v_sc)
            v_st = ctx.op("vector", lambda e: e.tensor_tensor(
                srow[:, 2 * NB:3 * NB], srow[:, NB:2 * NB], srow[:, 0:NB], ALU.subtract), inc=ve)
            ctx.wait("tensor", ve, v_st)
            p_add = ctx.op("tensor", lambda e: e.matmul(PB[0:32, 0:NB], cst["ones1x128f"][:, 0:32],
                                                        srow[0:1, 2 * NB:3 * NB], start=False, stop=True), inc=pe)
            ctx.wait("vector", pe, p_add)
            v_ccs = ctx.op("vector", lambda e: e.tensor_copy(ctT[:], PB[0:32, 0:NB]), inc=ve, psum=True)
            ctx.wait("sync", ve, v_ccs)
            d3 = ctx.dma("sync", sort_scr2.rearrange("(g k) -> g k", g=G), ctT[:], "qDg")
            ctx.waitt("sync", d3)
            d4 = ctx.dma("sync", ccs_row[:], sort_scr2.rearrange("(o n) -> o n", o=1), "qDg")
            ctx.waitt("tensor", d4)
            for i in range(4):
                ctx.op("tensor", lambda e, i=i: e.matmul(
                    PA[:, i * 512:(i + 1) * 512], cst["ones1x128f"][:], ccs_row[:, i * 512:(i + 1) * 512],
                    start=True, stop=False), inc=None)
            for i in range(4):
                ctx.op("tensor", lambda e, i=i: e.matmul(
                    PA[:, i * 512:(i + 1) * 512], cst["slt128"][:], onehot[:, i * 512:(i + 1) * 512],
                    start=False, stop=True), inc=pe if i == 3 else None)
            p_r = ctx.cnt(pe)
            ctx.wait("vector", pe, p_r)
            v_mu = ctx.op("vector", lambda e: e.tensor_tensor(ttrs[:], onehot[:], PA[:], ALU.mult), inc=ve, psum=True)
            ctx.wait("vector", ve, v_mu)
            v_ps = ctx.op("vector", lambda e: e.tensor_reduce(
                out=posf[:].rearrange("p (g o) -> p g o", o=1),
                in_=ttrs[:].rearrange("p (g k) -> p g k", k=NB), axis=AX.X, op=ALU.add), inc=ve)
            ctx.wait("vector", ve, v_ps)
            v_pf = ctx.op("vector", lambda e, h=h: e.tensor_scalar(
                posfw[:, :, h], posf[:], float(64 + h * T), None, op0=ALU.add), inc=ve)
            v_pb = ctx.op("vector", lambda e, h=h: e.tensor_scalar(
                posbo[:, :, h], posf[:], 0.0, None, op0=ALU.add), inc=ve)
            return ctx.cnt(ve)

        def scatter_hash(hd, h):
            for g in range(G):
                ctx.op("gpsimd", lambda e, g=g, h=h: e.indirect_dma_start(
                    out=sbuf_h[:],
                    out_offset=bass.IndirectOffsetOnAxis(posfw[:, g, h:h + 1], 0),
                    in_=payload[:, g, :], in_offset=None), inc=qG, n=16)

        beq = sb2("beq", [128, 128], F32)
        kid = sb2("kid", [128, 2], F32)
        attend_state = {"acc_ve": 0, "bo_q": None, "bo_q1": None, "wrap_q": None, "scat_q": 0,
                        "pend": None, "prev_gq": 0}

        def flush_acc():
            if attend_state["pend"] is not None:
                pgq, pbuf = attend_state["pend"]
                ctx.wait("vector", qG, pgq)
                v_acc = ctx.op("vector", lambda e, pbuf=pbuf: e.tensor_tensor(
                    acc[:], acc[:], u1[:, :, pbuf:pbuf + 132], ALU.add), inc=ve)
                attend_state["acc_ve"] = v_acc
                attend_state["pend"] = None

        def emit_attend(hd, h):
            base = h * T
            ctx.wait("sync", pe, ctx.cnt(pe))
            ctx.wait("scalar", pe, ctx.cnt(pe))
            ctx.wait("scalar", qG, attend_state["scat_q"])
            if h == 0:
                ctx.waitt("sync", attend_state["wrap_q"])
                ctx.waitt("scalar", attend_state["wrap_q"])
            r1 = ctx.dma("sync", qkT_s[:], sbuf_h[base:base + 4224, 0:128], qL, transpose=True)
            r2 = ctx.dma("sync", knT_s[:], sbuf_h[base:base + 4224, 128:256], qL, transpose=True)
            r3 = ctx.dma("scalar", vt[:],
                         sbuf_h[base:base + 4224, :].rearrange("(b p) r -> p b r", p=128)[:, :, VOFF:VOFF + 132], qA)
            r3b = ctx.dma("scalar", vt2[:],
                          sbuf_h[base + 64:base + 64 + 4096, :].rearrange("(b p) r -> p b r", p=128)[:, :, VOFF:VOFF + 132], qA)
            r4 = ctx.dma("sync", qidr[0:1, 0:64],
                         sbuf_h[base + 64:base + 128, IDHI:IDHI + 1].rearrange("t c -> c t"), qL,
                         allow_slow_non_contiguous=True)
            r5 = ctx.dma("sync", qidr[0:1, 64:128],
                         sbuf_h[base + 64:base + 128, IDLO:IDLO + 1].rearrange("t c -> c t"), qL,
                         allow_slow_non_contiguous=True)
            if ATT_PARTS < 2:
                return
            ctx.waitt("tensor", r4)
            ctx.waitt("tensor", r5)
            p_qh = ctx.op("tensor", lambda e: e.matmul(PB[:, 896:960], cst["ones1x128b"][:],
                                                       qidr[0:1, 0:64], start=True, stop=True), inc=pe)
            p_ql = ctx.op("tensor", lambda e: e.matmul(PB[:, 960:1024], cst["ones1x128b"][:],
                                                       qidr[0:1, 64:128], start=True, stop=True), inc=pe)
            ctx.waitt("vector", r3)
            vk1 = ctx.op("vector", lambda e: e.tensor_copy(kid[:, 0:1], vt[:, 0, 129:130]), inc=ve)
            vk2 = ctx.op("vector", lambda e: e.tensor_copy(kid[:, 1:2], vt[:, 0, 130:131]), inc=ve)
            ctx.wait("vector", pe, p_ql)
            ctx.wait("vector", ve, vk2)
            b1 = ctx.op("vector", lambda e: e.tensor_scalar(beq[:, 0:64], PB[:, 896:960],
                                                            kid[:, 0:1], None, op0=ALU.is_equal), inc=ve, psum=True)
            b2 = ctx.op("vector", lambda e: e.tensor_scalar(beq[:, 64:128], PB[:, 960:1024],
                                                            kid[:, 1:2], None, op0=ALU.is_equal), inc=ve, psum=True)
            ctx.wait("vector", ve, b2)
            b3 = ctx.op("vector", lambda e: e.tensor_tensor(beq[:, 0:64], beq[:, 0:64],
                                                            beq[:, 64:128], ALU.mult), inc=ve)
            ctx.wait("vector", ve, b3)
            b4 = ctx.op("vector", lambda e: e.tensor_scalar(bmask[:], beq[:, 0:64],
                                                            -1.0, 1.0, op0=ALU.mult, op1=ALU.add), inc=ve)
            if ATT_PARTS < 3:
                return
            evac_of = {}
            exp_of = {}
            mask_of = {}

            def dots_batch(j8):
                bank = PC if j8 % 2 == 0 else PB
                if j8 >= 2:
                    ctx.wait("tensor", ac, exp_of[j8 - 2])
                for cc in range(8):
                    c = 8 * j8 + cc
                    ctx.op("tensor", lambda e, c=c, cc=cc, bank=bank: e.matmul(
                        bank[:, cc * 64:(cc + 1) * 64],
                        knT_s[:, 64 * c:64 * c + 128],
                        qkT_s[:, 64 * c + 64:64 * c + 128],
                        start=True, stop=True), inc=pe if cc == 7 else None)
                return ctx.cnt(pe)

            def exp_mask(j8, last_dots):
                bank = PC if j8 % 2 == 0 else PB
                pt = PT if j8 % 2 == 0 else PT2
                ctx.wait("scalar", pe, last_dots)
                if j8 >= 2:
                    ctx.wait("scalar", ve, mask_of[j8 - 2])
                a_exp = ctx.op("scalar", lambda e, bank=bank, pt=pt: e.activation(
                    out=pt[:], in_=bank[:, 0:512], func=AF.Exp, scale=RSQD), inc=ac, psum=True)
                exp_of[j8] = a_exp
                ctx.wait("vector", ac, a_exp)
                v_m = ctx.op("vector", lambda e, pt=pt: e.tensor_tensor(pt[:], pt[:], cst["pmask8"][:], ALU.mult), inc=ve)
                if j8 == 0:
                    ctx.wait("vector", ve, max(v_m, b4))
                    v_m = ctx.op("vector", lambda e, pt=pt: e.tensor_tensor(pt[:, 0:64], pt[:, 0:64],
                                                                            bmask[:], ALU.mult), inc=ve)
                mask_of[j8] = v_m
                return v_m

            def pv_batch(j8):
                pt = PT if j8 % 2 == 0 else PT2
                ctx.wait("tensor", ve, mask_of[j8])
                if j8 == 0:
                    ctx.waitt("tensor", r3)
                    ctx.waitt("tensor", r3b)
                    if attend_state["bo_q"] is not None:
                        ctx.waitt("vector", attend_state["bo_q"])
                        ctx.waitt("vector", attend_state["bo_q1"])
                for cc in range(8):
                    c = 8 * j8 + cc
                    m = c // 2
                    slot = [0, 512, 1024, 1536][m % 4]
                    if c % 2 == 0:
                        if m >= 4 and (m - 4) in evac_of:
                            ctx.wait("tensor", ve, evac_of[m - 4])
                        ctx.op("tensor", lambda e, c=c, cc=cc, slot=slot, pt=pt: e.matmul(
                            PA[64:128, slot:slot + 129],
                            pt[:, cc * 64:(cc + 1) * 64],
                            vt[:, c // 2, 0:129],
                            start=True, stop=True, tile_position=(0, 64)), inc=None)
                    else:
                        pv = ctx.op("tensor", lambda e, c=c, cc=cc, slot=slot, pt=pt: e.matmul(
                            PA[0:64, slot:slot + 129],
                            pt[:, cc * 64:(cc + 1) * 64],
                            vt2[:, (c - 1) // 2, 0:129],
                            start=True, stop=True, tile_position=(0, 0)), inc=pe)
                        ctx.wait("vector", pe, pv)
                        evac_of[m] = ctx.op("vector", lambda e, m=m, slot=slot: e.tensor_copy(
                            staging[:, m, 0:129], PA[:, slot:slot + 129]), inc=ve, psum=True)

            ctx.waitt("tensor", r1)
            ctx.waitt("tensor", r2)
            ld_prev = dots_batch(0)
            for j8 in range(8):
                ld_cur = ld_prev
                if j8 < 7:
                    ld_prev = dots_batch(j8 + 1)
                exp_mask(j8, ld_cur)
                pv_batch(j8)
            flush_acc()
            stg_ve = ctx.cnt(ve)
            ctx.wait("sync", ve, stg_ve)
            ctx.wait("sync", qG, attend_state["prev_gq"])
            boT_ap = bo_h.rearrange("(g p) r -> p g r", p=128)
            w_bo1 = ctx.dma("sync", boT_ap[0:64, :, :], staging[64:128, :, :], qS)
            w_bo = ctx.dma("sync", boT_ap[64:128, :, :], staging[0:64, :, :], qS)
            attend_state["bo_q"] = w_bo
            attend_state["bo_q1"] = w_bo1
            if ATT_PARTS < 6:
                return
            ctx.waitt("gpsimd", w_bo1)
            ctx.waitt("gpsimd", w_bo)
            ctx.wait("gpsimd", ve, attend_state["acc_ve"])
            gbuf = 0 if h % 2 == 0 else 132
            for g in range(G):
                ctx.op("gpsimd", lambda e, g=g, h=h, gbuf=gbuf: e.indirect_dma_start(
                    out=u1[:, g, gbuf:gbuf + 132], out_offset=None,
                    in_=bo_h[:],
                    in_offset=bass.IndirectOffsetOnAxis(posbo[:, g, h:h + 1], 0)), inc=qG, n=16)
            gq = ctx.cnt(qG)
            attend_state["prev_gq"] = gq
            attend_state["pend"] = (gq, gbuf)

        NHD = NHEADS_RUN if stage >= 5 else min(NHEADS_RUN, 1) if stage < 4 else NHEADS_RUN
        for hd in range(NHD):
            barrier(("vector", "scalar", "tensor", "gpsimd", "sync"))
            build_head_payload(hd)
            if hd == 0 and "dbg_payload" in dbg:
                pd = ctx.cnt(ve)
                ctx.wait("sync", ve, pd)
                ctx.wait("sync", ac, ctx.cnt(ac))
                ctx.dma("sync", dbg["dbg_payload"][:], payload[:].rearrange("p g r -> p (g r)"), qS)
            if hd == 0 and "dbg_buckets" in dbg:
                pd = ctx.cnt(ve)
                ctx.wait("sync", ve, pd)
                ctx.dma("sync", dbg["dbg_buckets"][:], buckets[:].rearrange("p g h -> p (g h)"), qS)
            if stage < 3:
                continue
            pay_done_ve = ctx.cnt(ve)
            pay_done_ac = ctx.cnt(ac)
            ctx.wait("gpsimd", ve, pay_done_ve)
            ctx.wait("gpsimd", ac, pay_done_ac)
            for h in range(NH):
                sv = sort_hash(hd, h)
                ctx.wait("gpsimd", ve, sv)
                scatter_hash(hd, h)
            if hd == 0 and "dbg_pos" in dbg:
                pd = ctx.cnt(ve)
                ctx.wait("sync", ve, pd)
                ctx.dma("sync", dbg["dbg_pos"][:], posbo[:].rearrange("p g h -> p (g h)"), qS)
            sc_done = ctx.cnt(qG)
            attend_state["scat_q"] = sc_done
            ctx.wait("sync", qG, sc_done)
            wr = ctx.dma("sync", sbuf_h[0:64, :], sbuf_h[7 * T + 4096:7 * T + 4160, :], qS)
            attend_state["wrap_q"] = wr
            if stage < 4:
                continue
            vz2 = ctx.op("vector", lambda e: e.memset(acc[:].rearrange("p g r -> p (g r)"), 0.0), inc=ve)
            attend_state["acc_ve"] = vz2
            for h in range(ATT_HASHES):
                emit_attend(hd, h)
            flush_acc()
            ad = ctx.cnt(ve)
            ctx.wait("vector", ve, ad)
            v_rc = ctx.op("vector", lambda e: e.reciprocal(rcp[:], acc[:, :, 128].rearrange("p g -> p g")), inc=ve)
            ctx.wait("vector", ve, v_rc)
            v_at = ctx.op("vector", lambda e: e.tensor_tensor(
                u1[:, :, 132:260], acc[:, :, 0:128],
                rcp[:].rearrange("p (g o) -> p g o", o=1).to_broadcast([128, G, DH]), ALU.mult), inc=ve)
            ctx.wait("sync", ve, v_at)
            ctx.dma("sync", attn_h[hd, :, :].rearrange("(g p) d -> p g d", p=128), u1[:, :, 132:260], qS)
            if hd == 0 and "dbg_attn" in dbg:
                ctx.dma("sync", dbg["dbg_attn"][:, 0:G * DH].rearrange("p (g d) -> p g d", d=DH), u1[:, :, 132:260], qS)


    barrier()
    if stage < 5:
        finish_zero(nc, ctx, outT, sb)
        return

    qL, qA, qG, pe, ve, ac, qS = "qL", "qA", "qG", "pe", "ve", "ac", "qS"
    with ExitStack() as g3:
        def sb3(name, shape, dt):
            return g3.enter_context(nc.sbuf_tensor(name, shape, dt))

        def ps3(name, shape, dt=F32):
            return g3.enter_context(nc.psum_tensor(name, shape, dt))

        wout_own = sb3("wout_own", [128, 4, 1024], BF16)
        wout_sib = sb3("wout_sib", [128, 4, 1024], BF16)
        wp3f = sb3("wp3f", [128, 4, 1024], F32)
        att_all = sb3("att_all", [128, 16, 8, 128], BF16)
        attnT = sb3("attnT", [128, 8, 512], BF16)
        x2slab = sb3("x2slab", [128, 8, 512], F32)
        sq3 = sb3("sq3", [128, 8, 512], F32)
        xln3 = sb3("xln3", [128, 8, 512], BF16)
        evac3 = sb3("evac3", [128, 8, 512], F32)
        rowstat3 = sb3("rowstat3", [1, 4 * 512], F32)
        outslab = sb3("outslab", [128, 8, 512], F32)
        PS_STAT3 = ps3("ps_stat3", [1, 1024])
        PS_REPa = ps3("ps_repa", [128, 512])
        PS_REPb = ps3("ps_repb", [128, 512])
        PS_F3 = ps3("ps_f3", [128, 1024])
        PS_T3 = ps3("ps_t3", [128, 512], BF16)

        def dma_branchy(eng, out_even, in_even, out_odd, in_odd, role):
            idx = ctx._roles.get(role, 0)
            ctx._roles[role] = (idx + 1) % ctx.NROT
            name = "%s_%d" % (role, idx)
            h = ctx.sem(name)
            prev = ctx._semcnt[name]
            if prev:
                ctx.emit(eng, lambda e, h=h, v=prev: e.wait_ge(h, v))
            v = ctx.bump(name, 16)

            def go(e, oe=out_even, ie=in_even, oo=out_odd, io_=in_odd, h=h):
                pr = ctx.parity_reg(eng)["reg"]
                with e.If_eq(pr, 0):
                    e.dma_start(out=oe, in_=ie).then_inc(h, 16)
                with e.Else():
                    e.dma_start(out=oo, in_=io_).then_inc(h, 16)
            ctx.emit(eng, go)
            return (name, v)

        if "dbg_x1h" in dbg:
            ctx.dma("sync", dbg["dbg_x1h"][:].rearrange("(c p) t -> p c t", p=128),
                    x1T_h.rearrange("(c p) t -> p c t", p=128)[:, :, 0:512], qS)
        lw1 = ctx.dma("sync", wp3f[:], ap["woutT_own"].rearrange("(c p) o -> p c o", p=128), qL)
        ctx.waitt("vector", lw1)
        cw1 = ctx.op("vector", lambda e: e.tensor_copy(wout_own[:], wp3f[:]), inc=ve)
        ctx.wait("sync", ve, cw1)
        lw2 = dma_branchy("gpsimd",
                          wp3f[:], ap["woutT_sibB"].rearrange("(c p) o -> p c o", p=128),
                          wp3f[:], ap["woutT_sibA"].rearrange("(c p) o -> p c o", p=128), qL)
        ctx.waitt("vector", lw2)
        cw2 = ctx.op("vector", lambda e: e.tensor_copy(wout_sib[:], wp3f[:]), inc=ve)

        HT = T // 2
        if P3_PARTS < 1:
            barrier()
            finish_zero(nc, ctx, outT, sb3)
            return
        cc_toks = []
        for hd in range(NHEADS_CORE):
            cc_toks.append(dma_branchy(
                "gpsimd",
                ccin_h[:, hd * 128:(hd + 1) * 128], attn_h[hd, HT:T, :],
                ccin_h[:, hd * 128:(hd + 1) * 128], attn_h[hd, 0:HT, :], qG + "c"))
        for t_ in cc_toks:
            ctx.waitt("gpsimd", t_)
        ccsem = ctx.sem("cc")
        groups = [[0, 1], [2, 3], [4, 5], [6, 7]]
        ctx.emit("gpsimd", lambda e: e.collective_compute(
            "AllGather", ALU.bypass, groups, [ccin_h.opt()], [ccout_h.opt()]).then_inc(ccsem, 1))
        ctx.bump("cc", 1)
        ctx.wait("gpsimd", "cc", 1)

        if P3_PARTS < 2:
            barrier()
            finish_zero(nc, ctx, outT, sb3)
            return
        at_toks = []
        for hd in range(NHEADS_CORE):
            at_toks.append(dma_branchy(
                "gpsimd",
                att_all[:, :, hd, :], attn_h[hd, 0:HT, :].rearrange("(tg p) d -> p tg d", p=128),
                att_all[:, :, hd, :], attn_h[hd, HT:T, :].rearrange("(tg p) d -> p tg d", p=128), qL))
        for hd in range(NHEADS_CORE):
            at_toks.append(dma_branchy(
                "gpsimd",
                att_all[:, :, 4 + hd, :],
                ccout_h[HT:T, hd * 128:(hd + 1) * 128].rearrange("(tg p) d -> p tg d", p=128),
                att_all[:, :, 4 + hd, :],
                ccout_h[0:HT, hd * 128:(hd + 1) * 128].rearrange("(tg p) d -> p tg d", p=128), qL))
        for t_ in at_toks:
            ctx.waitt("tensor", t_)

        if P3_PARTS < 3:
            barrier()
            finish_zero(nc, ctx, outT, sb3)
            return
        x1v = x1T_h.rearrange("(c p) t -> p c t", p=128)

        def ln_rows3(src_slab, slab_cols):
            c0 = ctx.cnt(ve)
            ctx.wait("scalar", ve, c0)
            ctx.wait("tensor", ve, c0)
            a_sq = ctx.op("scalar", lambda e: e.activation(out=sq3[:, :, 0:slab_cols], in_=src_slab,
                                                           func=AF.Square), inc=ac)
            for fc in range(8):
                ctx.op("tensor", lambda e, fc=fc: e.matmul(
                    PS_STAT3[0:1, 0:slab_cols], cst["ones128f"][:], src_slab[:, fc, 0:slab_cols],
                    start=(fc == 0), stop=(fc == 7)), inc=pe if fc == 7 else None)
            ctx.wait("tensor", ac, a_sq)
            for fc in range(8):
                ctx.op("tensor", lambda e, fc=fc: e.matmul(
                    PS_STAT3[0:1, 512:512 + slab_cols], cst["ones128f"][:], sq3[:, fc, 0:slab_cols],
                    start=(fc == 0), stop=(fc == 7)), inc=pe if fc == 7 else None)
            p_done = ctx.cnt(pe)
            ctx.wait("vector", pe, p_done)
            R = rowstat3[:].rearrange("a (k n) -> a k n", n=512)
            v1 = ctx.op("vector", lambda e: e.tensor_scalar(R[:, 0, 0:slab_cols], PS_STAT3[0:1, 0:slab_cols],
                                                            1.0 / E, None, op0=ALU.mult), inc=ve, psum=True)
            v2 = ctx.op("vector", lambda e: e.tensor_scalar(R[:, 1, 0:slab_cols], PS_STAT3[0:1, 512:512 + slab_cols],
                                                            1.0 / E, None, op0=ALU.mult), inc=ve, psum=True)
            ctx.wait("vector", ve, v2)
            v3 = ctx.op("vector", lambda e: e.tensor_tensor(R[:, 2, 0:slab_cols], R[:, 0, 0:slab_cols],
                                                            R[:, 0, 0:slab_cols], ALU.mult), inc=ve)
            ctx.wait("vector", ve, v3)
            v4 = ctx.op("vector", lambda e: e.tensor_tensor(R[:, 2, 0:slab_cols], R[:, 1, 0:slab_cols],
                                                            R[:, 2, 0:slab_cols], ALU.subtract), inc=ve)
            ctx.wait("scalar", ve, v4)
            a1 = ctx.op("scalar", lambda e: e.activation(out=R[:, 3, 0:slab_cols], in_=R[:, 2, 0:slab_cols],
                                                         func=AF.Ln, bias=cst["epscol"][0:1, 0:1]), inc=ac)
            ctx.wait("scalar", ac, a1)
            a2 = ctx.op("scalar", lambda e: e.activation(out=R[:, 1, 0:slab_cols], in_=R[:, 3, 0:slab_cols],
                                                         func=AF.Exp, scale=-0.5), inc=ac)
            ctx.wait("vector", ac, a2)
            v5 = ctx.op("vector", lambda e: e.tensor_tensor(R[:, 0, 0:slab_cols], R[:, 0, 0:slab_cols],
                                                            R[:, 1, 0:slab_cols], ALU.mult), inc=ve)
            ctx.wait("tensor", ve, v5)
            ctx.op("tensor", lambda e: e.matmul(PS_REPa[:, 0:slab_cols], cst["ones1x128f"][:],
                                                R[0:1, 1, 0:slab_cols], start=True, stop=True), inc=pe)
            pb = ctx.op("tensor", lambda e: e.matmul(PS_REPb[:, 0:slab_cols], cst["ones1x128f"][:],
                                                     R[0:1, 0, 0:slab_cols], start=True, stop=True), inc=pe)
            return pb

        SLAB3 = 512
        prev_wr = []
        for sidx in range(HT // SLAB3):
            c0 = ctx.cnt(ve)
            ctx.wait("tensor", ve, c0)
            ctx.wait("tensor", ac, ctx.cnt(ac))
            tr_evac = {}
            for idx3 in range(32):
                hdc, tt = idx3 // 4, idx3 % 4
                tg = sidx * 4 + tt
                tslot3 = idx3 % 4
                if idx3 >= 1:
                    ctx.wait("tensor", ve, tr_evac[idx3 - 1])
                ctx.op("tensor", lambda e, hdc=hdc, tg=tg, tslot3=tslot3: e.transpose(
                    PS_T3[:, tslot3 * 128:(tslot3 + 1) * 128],
                    att_all[:, tg, hdc, :], cst["ident128"][:]), inc=pe)
                p_tr = ctx.cnt(pe)
                if P3_PARTS == 31:
                    tr_evac[idx3] = ctx.op("vector", lambda e: e.memset(attnT[:, 0, 0:4], 0.0), inc=ve)
                    continue
                ctx.wait("vector", pe, p_tr)
                tr_evac[idx3] = ctx.op("vector", lambda e, hdc=hdc, tt=tt, tslot3=tslot3: e.tensor_copy(
                    attnT[:, hdc, tt * 128:(tt + 1) * 128],
                    PS_T3[:, tslot3 * 128:(tslot3 + 1) * 128]), inc=ve, psum=True)
            trd = ctx.cnt(ve)
            ctx.wait("tensor", ve, trd)
            if P3_PARTS < 4:
                continue
            ctx.wait("gpsimd", ve, ctx.cnt(ve))
            ctx.wait("gpsimd", ac, ctx.cnt(ac))
            lx1 = dma_branchy("gpsimd",
                              sq3[:], x1v[:, :, sidx * SLAB3:(sidx + 1) * SLAB3],
                              sq3[:], x1v[:, :, HT + sidx * SLAB3:HT + (sidx + 1) * SLAB3], qA)
            ctx.waitt("vector", lx1)
            if sidx == 0 and "dbg_x1r" in dbg:
                ctx.waitt("sync", lx1)
                ctx.dma("sync", dbg["dbg_x1r"][:].rearrange("p (c t) -> p c t", t=512), sq3[:], qS)
            ev_oc = {}
            for oc in range(8):
                slot = oc % 2
                if oc >= 2:
                    ctx.wait("tensor", ve, ev_oc[oc - 2])
                for hdc in range(8):
                    w = wout_own if hdc < 4 else wout_sib
                    ctx.op("tensor", lambda e, oc=oc, hdc=hdc, slot=slot, w=w: e.matmul(
                        PS_F3[:, slot * 512:slot * 512 + SLAB3],
                        w[:, hdc % 4, oc * 128:(oc + 1) * 128],
                        attnT[:, hdc, :],
                        start=(hdc == 0), stop=(hdc == 7)), inc=pe if hdc == 7 else None)
                pdone = ctx.cnt(pe)
                ctx.wait("vector", pe, pdone)
                if oc == 0:
                    ctx.wait("vector", ac, ctx.cnt(ac))
                e1 = ctx.op("vector", lambda e, oc=oc, slot=slot: e.tensor_scalar(
                    evac3[:, oc, :], PS_F3[:, slot * 512:slot * 512 + SLAB3],
                    boutc[:, oc:oc + 1], None, op0=ALU.add), inc=ve, psum=True)
                ev_oc[oc] = e1
                ctx.wait("vector", ve, e1)
                ctx.op("vector", lambda e, oc=oc: e.tensor_tensor(
                    x2slab[:, oc, :], sq3[:, oc, :], evac3[:, oc, :], ALU.add), inc=ve)
            if P3_PARTS < 5:
                continue
            if sidx == 0 and "dbg_x2" in dbg:
                dd = ctx.cnt(ve)
                ctx.wait("sync", ve, dd)
                ctx.dma("sync", dbg["dbg_x2"][:].rearrange("p (c t) -> p c t", t=512), x2slab[:], qS)
                ctx.dma("sync", dbg["dbg_attT"][:].rearrange("p (c t) -> p c t", t=512), attnT[:], qS)
            pb3 = ln_rows3(x2slab[:], SLAB3)
            ctx.wait("vector", pe, pb3)
            vA3 = ctx.op("vector", lambda e: e.tensor_tensor(
                sq3[:], x2slab[:],
                PS_REPa[:, 0:SLAB3].rearrange("p (c n) -> p c n", c=1).to_broadcast([128, 8, SLAB3]),
                ALU.mult), inc=ve, psum=True)
            ctx.wait("vector", ve, vA3)
            vB3 = ctx.op("vector", lambda e: e.tensor_tensor(
                xln3[:], sq3[:],
                PS_REPb[:, 0:SLAB3].rearrange("p (c n) -> p c n", c=1).to_broadcast([128, 8, SLAB3]),
                ALU.subtract), inc=ve, psum=True)
            ctx.wait("tensor", ve, vB3)
            ev2 = {}
            for oc in range(8):
                slot = oc % 2
                if oc >= 2:
                    ctx.wait("tensor", ve, ev2[oc - 2])
                for fc in range(8):
                    ctx.op("tensor", lambda e, fc=fc, oc=oc, slot=slot: e.matmul(
                        PS_F3[:, slot * 512:slot * 512 + SLAB3],
                        tern[:, fc, oc * 128:(oc + 1) * 128], xln3[:, fc, :],
                        start=(fc == 0), stop=(fc == 7)), inc=pe if fc == 7 else None)
                pdone = ctx.cnt(pe)
                ctx.wait("vector", pe, pdone)
                e2 = ctx.op("vector", lambda e, oc=oc, slot=slot: e.tensor_scalar(
                    evac3[:, oc, :], PS_F3[:, slot * 512:slot * 512 + SLAB3],
                    halfalpha[:, 0:1], halfbias[:, oc:oc + 1], op0=ALU.mult, op1=ALU.add), inc=ve, psum=True)
                ev2[oc] = e2
                ctx.wait("vector", ve, e2)
                ctx.op("vector", lambda e, oc=oc: e.tensor_tensor(
                    outslab[:, oc, :], x2slab[:, oc, :], evac3[:, oc, :], ALU.add), inc=ve)
            odone = ctx.cnt(ve)
            ctx.wait("sync", ve, odone)
            for tok in prev_wr:
                ctx.waitt("vector", tok)
            prev_wr = [ctx.dma("sync", outT.rearrange("(c p) t -> p c t", p=128)[:, :, sidx * SLAB3:(sidx + 1) * SLAB3],
                               outslab[:], qS)]
        for tok in prev_wr:
            ctx.waitt("sync", tok)
        if P3_PARTS < 5:
            barrier()
            finish_zero(nc, ctx, outT, sb3)
            return
        barrier(("sync",))


def finish_zero(nc, ctx, outT, sb):
    zrow = sb("zrow", [128, 8, 64], F32)
    vz = ctx.op("vector", lambda e: e.memset(zrow[:], 0.0), inc="ve")
    ctx.wait("sync", "ve", vz)
    toks = []
    for i in range(8):
        toks.append(ctx.dma("sync", outT.rearrange("(c p) t -> p c t", p=128)[:, :, i * 64:(i + 1) * 64], zrow[:], "qOut"))
    for t_ in toks:
        ctx.waitt("sync", t_)



def make_core_inputs(inputs, consts, c):
    b = c // 2
    hh = c % 2
    heads = list(range(4 * hh, 4 * hh + 4))
    x = np.asarray(inputs["L0_attn_x"], np.float32)
    w_qk = np.asarray(inputs["L0_attn_w_qk"], np.float32)
    w_v = np.asarray(inputs["L0_attn_w_v"], np.float32)
    w_out = np.asarray(inputs["L0_attn_w_out"], np.float32)
    b_out = np.asarray(inputs["L0_attn_b_out"], np.float32)
    rotations = np.asarray(inputs["L0_attn_rotations"], np.float32)
    fw = np.asarray(inputs["L0_ffn_weight"], np.float32)
    fa = np.asarray(inputs["L0_ffn_alpha"], np.float32)
    fb = np.asarray(inputs["L0_ffn_bias"], np.float32)

    d = dict(consts)
    d["xT"] = np.ascontiguousarray(x[b].T)
    d["wffnT"] = np.ascontiguousarray(fw.T)
    colsel = np.concatenate([np.arange(h * DH, (h + 1) * DH) for h in heads])
    d["wqkvT"] = np.ascontiguousarray(np.concatenate([w_qk.T[:, colsel], w_v.T[:, colsel]], axis=1))
    woutT = np.ascontiguousarray(w_out.T)
    d["woutT_own"] = np.ascontiguousarray(woutT[colsel, :])
    d["woutT_sibA"] = np.ascontiguousarray(woutT[0:512, :])
    d["woutT_sibB"] = np.ascontiguousarray(woutT[512:1024, :])
    d["boutRows"] = np.ascontiguousarray(b_out.reshape(8, 128).T)
    d["fbiasRows"] = np.ascontiguousarray(fb.reshape(8, 128).T)
    d["alpha"] = fa.reshape(1, 1)
    d["rotT"] = np.ascontiguousarray(rotations.reshape(DH, NH * 32))
    return d


def kernel(**inputs):
    consts = host_consts()
    nc = build_program()
    in_maps = [make_core_inputs(inputs, consts, c) for c in range(8)]
    res = run_bass_kernel_spmd(nc, in_maps, core_ids=list(range(8)))
    out = np.zeros((B, T, E), dtype=np.float32)
    for c in range(8):
        b = c // 2
        hh = c % 2
        o = res.results[c]["outT"]
        out[b, hh * (T // 2):(hh + 1) * (T // 2), :] = o.T
    return out


# version 06 (final)
# speedup vs baseline: 1.3482x; 1.0518x over previous
import numpy as np
import ml_dtypes
from contextlib import ExitStack

import concourse.bass as bass
import concourse.mybir as mybir
from concourse.bass_utils import run_bass_kernel_spmd

F32 = mybir.dt.float32
BF16 = mybir.dt.bfloat16
U32 = mybir.dt.uint32
I16 = mybir.dt.int16
AX = mybir.AxisListType
ALU = mybir.AluOpType
AF = mybir.ActivationFunctionType

E = 1024
T = 4096
B = 4
NH = 8
DH = 128
NB = 64
G = 32
NHEADS_CORE = 4
ROWL = 400
VOFF = 256
ONEOFF = 384
IDHI = 385
IDLO = 386
SBROWS = 7 * T + 4224
RSQD = float(DH) ** -0.5
P1B_TILES = G
P1B_HASH = True
P1B_PARTS = 9
ATT_HASHES = NH
ATT_PARTS = 9
NHEADS_RUN = NHEADS_CORE
P3_PARTS = 9


class Ctx:

    ENGS = ("sync", "scalar", "vector", "tensor", "gpsimd")

    def __init__(self, nc, st):
        self.nc = nc
        self.st = st
        self.steps = {e: [] for e in self.ENGS}
        self._semcnt = {}
        self._sems = {}
        self._roles = {}
        self.last_act_psum = 0
        self.last_dve_psum = 0
        self._parity_reg = {}

    def setup_parity(self, eng):
        cell = {}
        self._parity_reg[eng] = cell

        def go(e, cell=cell):
            pid = e.partition_id()
            pr = e.alloc_register()
            e.reg_mod(pr, pid, 2)
            cell["reg"] = pr
        self.emit(eng, go)

    def parity_reg(self, eng):
        return self._parity_reg[eng]

    def sem(self, name):
        if name not in self._sems:
            self._sems[name] = self.st.enter_context(self.nc.semaphore(name))
            self._semcnt[name] = 0
        return self._sems[name]

    def cnt(self, name):
        return self._semcnt[name]

    def bump(self, name, n):
        self._semcnt[name] += n
        return self._semcnt[name]

    def emit(self, eng, fn):
        self.steps[eng].append(fn)

    def wait(self, eng, sem_name, val):
        h = self.sem(sem_name)
        self.emit(eng, lambda e, h=h, v=val: e.wait_ge(h, v))

    def op(self, eng, fn, inc=None, n=1, psum=False):
        if psum:
            assert inc is not None
            if eng == "scalar" and self.last_dve_psum:
                self.wait(eng, "ve", self.last_dve_psum)
            if eng == "vector" and self.last_act_psum:
                self.wait(eng, "ac", self.last_act_psum)
        if inc is None:
            self.emit(eng, fn)
            return None
        h = self.sem(inc)
        v = self.bump(inc, n)
        self.emit(eng, lambda e, fn=fn, h=h, n=n: fn(e).then_inc(h, n))
        if psum:
            if eng == "scalar":
                self.last_act_psum = v
            elif eng == "vector":
                self.last_dve_psum = v
        return v

    NROT = 6

    def dma(self, eng, out_ap, in_ap, role, **kw):
        idx = self._roles.get(role, 0)
        self._roles[role] = (idx + 1) % self.NROT
        name = "%s_%d" % (role, idx)
        h = self.sem(name)
        prev = self._semcnt[name]
        if prev:
            self.emit(eng, lambda e, h=h, v=prev: e.wait_ge(h, v))
        v = self.bump(name, 16)
        self.emit(eng, lambda e, o=out_ap, i=in_ap, kw=kw, h=h: e.dma_start(out=o, in_=i, **kw).then_inc(h, 16))
        return (name, v)

    def waitt(self, eng, token):
        name, v = token
        self.wait(eng, name, v)

    def wait_role_all(self, eng, role):
        for idx in range(self.NROT):
            name = "%s_%d" % (role, idx)
            if name in self._semcnt and self._semcnt[name]:
                self.wait(eng, name, self._semcnt[name])

    def finalize(self, block):
        for name in self.ENGS:
            steps = self.steps[name]
            deco = getattr(block, name)

            def make(steps):
                def body(e):
                    for f in steps:
                        f(e)
                return body
            deco(make(steps))



def host_consts():
    c = {}
    c["slt128"] = (np.arange(128)[:, None] < np.arange(128)[None, :]).astype(ml_dtypes.bfloat16)
    c["slt32"] = (np.arange(32)[:, None] < np.arange(32)[None, :]).astype(np.float32)
    c["iota64"] = np.tile(np.arange(NB, dtype=np.float32)[None, :], (128, G))
    c["ones128b"] = np.ones((128, 1), dtype=ml_dtypes.bfloat16)
    c["ones128f"] = np.ones((128, 1), dtype=np.float32)
    c["ones32f"] = np.ones((32, 1), dtype=np.float32)
    c["ones1x128f"] = np.ones((1, 128), dtype=np.float32)
    c["ones1x128b"] = np.ones((1, 128), dtype=ml_dtypes.bfloat16)
    c["zeros1x64"] = np.zeros((1, 64), dtype=np.float32)
    c["epscol"] = np.full((128, 1), 1e-5, dtype=np.float32)
    c["ident128"] = np.eye(128, dtype=ml_dtypes.bfloat16)
    c["offs8"] = np.tile((np.arange(NH, dtype=np.float32) * 32)[None, :], (128, 1))
    pm = np.ones((128, 8, 64), dtype=ml_dtypes.bfloat16)
    for j in range(8):
        for i in range(64):
            pm[64 + i, j, i] = 0
    c["pmask8"] = pm.reshape(128, 512)
    tt = np.arange(T).reshape(G, 128).T
    c["idhi"] = (tt // 64).astype(np.float32)
    c["idlo"] = (tt % 64).astype(np.float32)
    return c


CONST_DTYPES = {
    "slt128": BF16, "slt32": F32, "iota64": F32, "ones128b": BF16,
    "ones128f": F32, "ones32f": F32, "ones1x128f": F32, "ones1x128b": BF16,
    "zeros1x64": F32, "epscol": F32, "ident128": BF16, "offs8": F32, "pmask8": BF16, "idhi": F32, "idlo": F32,
}




def build_program(debug=(), stage=5):
    nc = bass.Bass("TRN2", target_bir_lowering=False)

    ap = {}
    ap["xT"] = nc.dram_tensor("xT", [E, T], F32, kind="ExternalInput").ap()
    ap["wffnT"] = nc.dram_tensor("wffnT", [E, E], F32, kind="ExternalInput").ap()
    ap["wqkvT"] = nc.dram_tensor("wqkvT", [E, E], F32, kind="ExternalInput").ap()
    ap["woutT_own"] = nc.dram_tensor("woutT_own", [512, E], F32, kind="ExternalInput").ap()
    ap["woutT_sibA"] = nc.dram_tensor("woutT_sibA", [512, E], F32, kind="ExternalInput").ap()
    ap["woutT_sibB"] = nc.dram_tensor("woutT_sibB", [512, E], F32, kind="ExternalInput").ap()
    ap["boutRows"] = nc.dram_tensor("boutRows", [128, 8], F32, kind="ExternalInput").ap()
    ap["fbiasRows"] = nc.dram_tensor("fbiasRows", [128, 8], F32, kind="ExternalInput").ap()
    ap["alpha"] = nc.dram_tensor("alpha", [1, 1], F32, kind="ExternalInput").ap()
    ap["rotT"] = nc.dram_tensor("rotT", [128, 256], F32, kind="ExternalInput").ap()
    shapes = {"slt128": [128, 128], "slt32": [32, 32], "iota64": [128, G * NB],
              "ones128b": [128, 1], "ones128f": [128, 1], "ones32f": [32, 1],
              "ones1x128f": [1, 128], "ones1x128b": [1, 128], "zeros1x64": [1, 64],
              "epscol": [128, 1], "ident128": [128, 128],
              "offs8": [128, 8], "pmask8": [128, 512], "idhi": [128, G], "idlo": [128, G]}
    for k, dt in CONST_DTYPES.items():
        ap[k] = nc.dram_tensor(k, shapes[k], dt, kind="ExternalInput").ap()

    outT = nc.dram_tensor("outT", [E, T // 2], F32, kind="ExternalOutput").ap()
    dbg = {}
    if "x1" in debug:
        dbg["dbg_x1"] = nc.dram_tensor("dbg_x1", [E, T], F32, kind="ExternalOutput").ap()
    if "xa" in debug:
        dbg["dbg_xa"] = nc.dram_tensor("dbg_xa", [E, T], BF16, kind="ExternalOutput").ap()
    if "payload" in debug:
        dbg["dbg_payload"] = nc.dram_tensor("dbg_payload", [128, G * ROWL], BF16, kind="ExternalOutput").ap()
    if "buckets" in debug:
        dbg["dbg_buckets"] = nc.dram_tensor("dbg_buckets", [128, G * NH], F32, kind="ExternalOutput").ap()
    if "pos" in debug:
        dbg["dbg_pos"] = nc.dram_tensor("dbg_pos", [128, G * NH], U32, kind="ExternalOutput").ap()
    if "attn" in debug:
        dbg["dbg_attn"] = nc.dram_tensor("dbg_attn", [128, G * DH], BF16, kind="ExternalOutput").ap()
    if "p3" in debug:
        dbg["dbg_x1r"] = nc.dram_tensor("dbg_x1r", [128, 8 * 512], F32, kind="ExternalOutput").ap()
        dbg["dbg_x2"] = nc.dram_tensor("dbg_x2", [128, 8 * 512], F32, kind="ExternalOutput").ap()
        dbg["dbg_attT"] = nc.dram_tensor("dbg_attT", [128, 8 * 512], BF16, kind="ExternalOutput").ap()
        dbg["dbg_x1h"] = nc.dram_tensor("dbg_x1h", [E, 512], F32, kind="ExternalOutput").ap()

    x1T_h = nc.dram_tensor("x1T_h", [E, T], F32, kind="Internal").ap()
    xaT_h = nc.dram_tensor("xaT_h", [E, T], BF16, kind="Internal").ap()
    sbuf_h = nc.dram_tensor("sbuf_h", [SBROWS, ROWL], BF16, kind="Internal").ap()
    bo_h = nc.dram_tensor("bo_h", [T, 132], F32, kind="Internal").ap()
    ccin_h = nc.dram_tensor("ccin_h", [T // 2, 512], BF16, kind="Internal").ap()
    ccout_h = nc.dram_tensor("ccout_h", [T, 512], BF16, kind="Internal").ap()
    attn_h = nc.dram_tensor("attn_h", [NHEADS_CORE, T, DH], BF16, kind="Internal").ap()
    sort_scr1 = nc.dram_tensor("sort_scr1", [G * NB], F32, kind="Internal").ap()
    sort_scr2 = nc.dram_tensor("sort_scr2", [G * NB], F32, kind="Internal").ap()

    st = ExitStack()
    ctx = Ctx(nc, st)

    def sb(name, shape, dt):
        return st.enter_context(nc.sbuf_tensor(name, shape, dt))

    def psum(name, shape, dt=F32):
        return st.enter_context(nc.psum_tensor(name, shape, dt))

    with st:
        build_body(nc, ctx, ap, outT, dbg, x1T_h, xaT_h, sbuf_h, bo_h, ccin_h, ccout_h,
                   attn_h, sort_scr1, sort_scr2, sb, psum, debug, stage)
        block = st.enter_context(nc.Block())
        ctx.finalize(block)
    return nc


def build_body(nc, ctx, ap, outT, dbg, x1T_h, xaT_h, sbuf_h, bo_h, ccin_h, ccout_h, attn_h,
               sort_scr1, sort_scr2, sb, psum, debug, stage):
    from contextlib import ExitStack

    tern = sb("tern", [128, 8, 1024], BF16)
    wqkv = sb("wqkv", [128, 8, 1024], BF16)
    rotb = sb("rotb", [128, 256], BF16)
    cst = {}
    for k in CONST_DTYPES:
        cst[k] = sb("c_" + k, list(ap[k].shape), CONST_DTYPES[k])
    halfbias = sb("halfbias", [128, 8], F32)
    boutc = sb("boutc", [128, 8], F32)
    halfalpha = sb("halfalpha", [128, 1], F32)
    deltas = sb("deltas", [128, 8], F32)

    qL, qA, qG, pe, ve, ac, qS = "qL", "qA", "qG", "pe", "ve", "ac", "qS"
    ctx.setup_parity("gpsimd")

    def barrier(engines=("sync", "scalar", "vector", "tensor", "gpsimd")):
        for sem in (qG, pe, ve, ac):
            v = ctx.cnt(sem) if sem in ctx._semcnt else 0
            if v:
                for e in engines:
                    ctx.wait(e, sem, v)
        for role in (qL, qA, qS, "qDg"):
            for e in engines:
                ctx.wait_role_all(e, role)

    with ExitStack() as g1:
        def sb1(name, shape, dt):
            return g1.enter_context(nc.sbuf_tensor(name, shape, dt))

        def ps1(name, shape, dt=F32):
            return g1.enter_context(nc.psum_tensor(name, shape, dt))

        wf32 = sb1("wf32", [128, 8, 1024], F32)
        tpos = sb1("tpos", [128, 8, 1024], BF16)
        rotf = sb1("rotf", [128, 256], F32)
        xslab = sb1("xslab", [128, 8, 512], F32)
        sq = sb1("sq", [128, 8, 512], F32)
        xln = sb1("xln", [128, 8, 512], BF16)
        x1slab = sb1("x1slab", [128, 8, 512], F32)
        rowstat = sb1("rowstat", [1, 8 * 512], F32)
        evac = sb1("evac", [128, 8, 512], F32)
        PS_STAT = ps1("ps_stat", [1, 1024])
        PS_REP = ps1("ps_rep", [128, 512])
        PS_REP2 = ps1("ps_rep2", [128, 512])
        PS_F = ps1("ps_f", [128, 1024])
        PS_M = ps1("ps_m", [128, 512])

        for k in CONST_DTYPES:
            ctx.dma("sync", cst[k][:], ap[k][:], qL)
        ctx.dma("sync", rotf[:], ap["rotT"][:], qL)
        ctx.dma("sync", halfbias[:], ap["fbiasRows"][:], qL)
        ctx.dma("sync", boutc[:], ap["boutRows"][:], qL)
        ctx.dma("sync", wf32[:], ap["wffnT"].rearrange("(c p) o -> p c o", p=128), qL)

        ctx.wait_role_all("vector", qL)
        ctx.op("vector", lambda e: e.tensor_copy(rotb[:], rotf[:]), inc=ve)
        ctx.op("vector", lambda e: e.tensor_scalar(halfbias[:], halfbias[:], 0.5, None, op0=ALU.mult), inc=ve)

        v_abs = ctx.op("vector", lambda e: e.tensor_reduce(
            out=deltas[:, 0:1].rearrange("p (a o) -> p a o", a=1),
            in_=wf32[:].rearrange("p c o -> p (c o)").rearrange("p (a n) -> p a n", a=1),
            axis=AX.X, op=ALU.add, apply_absolute_value=True), inc=ve)
        ctx.wait("tensor", ve, v_abs)
        p_sum = ctx.op("tensor", lambda e: e.matmul(PS_M[0:1, 0:1], deltas[:, 0:1], cst["ones128f"][:],
                                                    start=True, stop=True), inc=pe)
        ctx.wait("vector", pe, p_sum)
        v_del = ctx.op("vector", lambda e: e.tensor_scalar(deltas[0:1, 1:2], PS_M[0:1, 0:1],
                                                           0.7 / (E * E), None, op0=ALU.mult), inc=ve, psum=True)
        ctx.wait("tensor", ve, v_del)
        p_drep = ctx.op("tensor", lambda e: e.matmul(PS_M[:, 1:2], cst["ones1x128f"][:], deltas[0:1, 1:2],
                                                     start=True, stop=True), inc=pe)
        ctx.wait("vector", pe, p_drep)
        v_dc = ctx.op("vector", lambda e: e.tensor_copy(deltas[:, 0:1], PS_M[:, 1:2]), inc=ve, psum=True)
        ctx.wait("vector", ve, v_dc)
        v_dn = ctx.op("vector", lambda e: e.tensor_scalar(deltas[:, 1:2], deltas[:, 0:1], -1.0, None, op0=ALU.mult), inc=ve)
        ctx.wait("vector", ve, v_dn)
        v_t1 = ctx.op("vector", lambda e: e.tensor_scalar(tpos[:], wf32[:], deltas[:, 0:1], None, op0=ALU.is_gt), inc=ve)
        v_t2 = ctx.op("vector", lambda e: e.tensor_scalar(tern[:], wf32[:], deltas[:, 1:2], None, op0=ALU.is_lt), inc=ve)
        ctx.wait("vector", ve, v_t2)
        v_t3 = ctx.op("vector", lambda e: e.tensor_tensor(tern[:], tpos[:], tern[:], ALU.subtract), inc=ve)

        ctx.wait("sync", ve, v_t3)
        l_wq = ctx.dma("sync", wf32[:], ap["wqkvT"].rearrange("(c p) o -> p c o", p=128), qL)
        ctx.waitt("vector", l_wq)
        v_wq = ctx.op("vector", lambda e: e.tensor_copy(wqkv[:], wf32[:]), inc=ve)
        l_al = ctx.dma("sync", deltas[0:1, 3:4], ap["alpha"][:], qL)
        ctx.waitt("tensor", l_al)
        ctx.wait("tensor", ve, v_wq)
        p_arep = ctx.op("tensor", lambda e: e.matmul(PS_M[:, 2:3], cst["ones1x128f"][:], deltas[0:1, 3:4],
                                                     start=True, stop=True), inc=pe)
        ctx.wait("vector", pe, p_arep)
        ctx.op("vector", lambda e: e.tensor_scalar(halfalpha[:], PS_M[:, 2:3], 0.5, None, op0=ALU.mult), inc=ve, psum=True)

        def ln_rows(src_slab, slab_cols):
            c0 = ctx.cnt(ve)
            ctx.wait("scalar", ve, c0)
            ctx.wait("tensor", ve, c0)
            a_sq = ctx.op("scalar", lambda e: e.activation(out=sq[:, :, 0:slab_cols], in_=src_slab,
                                                           func=AF.Square), inc=ac)
            for fc in range(8):
                ctx.op("tensor", lambda e, fc=fc: e.matmul(
                    PS_STAT[0:1, 0:slab_cols], cst["ones128f"][:], src_slab[:, fc, 0:slab_cols],
                    start=(fc == 0), stop=(fc == 7)), inc=pe if fc == 7 else None)
            ctx.wait("tensor", ac, a_sq)
            for fc in range(8):
                ctx.op("tensor", lambda e, fc=fc: e.matmul(
                    PS_STAT[0:1, 512:512 + slab_cols], cst["ones128f"][:], sq[:, fc, 0:slab_cols],
                    start=(fc == 0), stop=(fc == 7)), inc=pe if fc == 7 else None)
            p_done = ctx.cnt(pe)
            ctx.wait("vector", pe, p_done)
            R = rowstat[:].rearrange("a (k n) -> a k n", n=512)
            v1 = ctx.op("vector", lambda e: e.tensor_scalar(R[:, 0, 0:slab_cols], PS_STAT[0:1, 0:slab_cols],
                                                            1.0 / E, None, op0=ALU.mult), inc=ve, psum=True)
            v2 = ctx.op("vector", lambda e: e.tensor_scalar(R[:, 1, 0:slab_cols], PS_STAT[0:1, 512:512 + slab_cols],
                                                            1.0 / E, None, op0=ALU.mult), inc=ve, psum=True)
            ctx.wait("vector", ve, v2)
            v3 = ctx.op("vector", lambda e: e.tensor_tensor(R[:, 2, 0:slab_cols], R[:, 0, 0:slab_cols],
                                                            R[:, 0, 0:slab_cols], ALU.mult), inc=ve)
            ctx.wait("vector", ve, v3)
            v4 = ctx.op("vector", lambda e: e.tensor_tensor(R[:, 2, 0:slab_cols], R[:, 1, 0:slab_cols],
                                                            R[:, 2, 0:slab_cols], ALU.subtract), inc=ve)
            ctx.wait("scalar", ve, v4)
            a1 = ctx.op("scalar", lambda e: e.activation(out=R[:, 3, 0:slab_cols], in_=R[:, 2, 0:slab_cols],
                                                         func=AF.Ln, bias=cst["epscol"][0:1, 0:1]), inc=ac)
            ctx.wait("scalar", ac, a1)
            a2 = ctx.op("scalar", lambda e: e.activation(out=R[:, 1, 0:slab_cols], in_=R[:, 3, 0:slab_cols],
                                                         func=AF.Exp, scale=-0.5), inc=ac)
            ctx.wait("vector", ac, a2)
            v5 = ctx.op("vector", lambda e: e.tensor_tensor(R[:, 0, 0:slab_cols], R[:, 0, 0:slab_cols],
                                                            R[:, 1, 0:slab_cols], ALU.mult), inc=ve)
            ctx.wait("tensor", ve, v5)
            pa = ctx.op("tensor", lambda e: e.matmul(PS_REP[:, 0:slab_cols], cst["ones1x128f"][:],
                                                     R[0:1, 1, 0:slab_cols], start=True, stop=True), inc=pe)
            pb = ctx.op("tensor", lambda e: e.matmul(PS_REP2[:, 0:slab_cols], cst["ones1x128f"][:],
                                                     R[0:1, 0, 0:slab_cols], start=True, stop=True), inc=pe)
            return pb

        SLAB = 512
        prev_w = []
        for s in range(T // SLAB):
            for tok in prev_w:
                ctx.waitt("vector", tok)
            prev_w = []
            ld = ctx.dma("sync", xslab[:],
                         ap["xT"].rearrange("(c p) t -> p c t", p=128)[:, :, s * SLAB:(s + 1) * SLAB], qL)
            ctx.waitt("scalar", ld)
            ctx.waitt("vector", ld)
            ctx.waitt("tensor", ld)
            pb = ln_rows(xslab[:], SLAB)
            ctx.wait("vector", pe, pb)
            vA = ctx.op("vector", lambda e: e.tensor_tensor(
                sq[:], xslab[:],
                PS_REP[:, 0:SLAB].rearrange("p (c n) -> p c n", c=1).to_broadcast([128, 8, SLAB]),
                ALU.mult), inc=ve, psum=True)
            ctx.wait("vector", ve, vA)
            vB = ctx.op("vector", lambda e: e.tensor_tensor(
                xln[:], sq[:],
                PS_REP2[:, 0:SLAB].rearrange("p (c n) -> p c n", c=1).to_broadcast([128, 8, SLAB]),
                ALU.subtract), inc=ve, psum=True)
            ctx.wait("tensor", ve, vB)
            ev_by_oc = {}
            for oc in range(8):
                slot = oc % 2
                if oc >= 2:
                    ctx.wait("tensor", ve, ev_by_oc[oc - 2])
                for fc in range(8):
                    ctx.op("tensor", lambda e, fc=fc, oc=oc, slot=slot: e.matmul(
                        PS_F[:, slot * 512:slot * 512 + SLAB],
                        tern[:, fc, oc * 128:(oc + 1) * 128], xln[:, fc, :],
                        start=(fc == 0), stop=(fc == 7)), inc=pe if fc == 7 else None)
                pdone = ctx.cnt(pe)
                ctx.wait("vector", pe, pdone)
                ev = ctx.op("vector", lambda e, oc=oc, slot=slot: e.tensor_scalar(
                    evac[:, oc, :], PS_F[:, slot * 512:slot * 512 + SLAB],
                    halfalpha[:, 0:1], halfbias[:, oc:oc + 1], op0=ALU.mult, op1=ALU.add), inc=ve, psum=True)
                ev_by_oc[oc] = ev
                ctx.wait("vector", ve, ev)
                ctx.op("vector", lambda e, oc=oc: e.tensor_tensor(
                    x1slab[:, oc, :], xslab[:, oc, :], evac[:, oc, :], ALU.add), inc=ve)
            x1done = ctx.cnt(ve)
            ctx.wait("sync", ve, x1done)
            prev_w.append(ctx.dma("sync", x1T_h.rearrange("(c p) t -> p c t", p=128)[:, :, s * SLAB:(s + 1) * SLAB], x1slab[:], qS))
            if "dbg_x1" in dbg:
                prev_w.append(ctx.dma("sync", dbg["dbg_x1"].rearrange("(c p) t -> p c t", p=128)[:, :, s * SLAB:(s + 1) * SLAB], x1slab[:], qS))
            pb2 = ln_rows(x1slab[:], SLAB)
            ctx.wait("vector", pe, pb2)
            vA2 = ctx.op("vector", lambda e: e.tensor_tensor(
                sq[:], x1slab[:],
                PS_REP[:, 0:SLAB].rearrange("p (c n) -> p c n", c=1).to_broadcast([128, 8, SLAB]),
                ALU.mult), inc=ve, psum=True)
            ctx.wait("vector", ve, vA2)
            vB2 = ctx.op("vector", lambda e: e.tensor_tensor(
                xln[:], sq[:],
                PS_REP2[:, 0:SLAB].rearrange("p (c n) -> p c n", c=1).to_broadcast([128, 8, SLAB]),
                ALU.subtract), inc=ve, psum=True)
            ctx.wait("sync", ve, vB2)
            prev_w.append(ctx.dma("sync", xaT_h.rearrange("(c p) t -> p c t", p=128)[:, :, s * SLAB:(s + 1) * SLAB], xln[:], qS))
            if "dbg_xa" in dbg:
                prev_w.append(ctx.dma("sync", dbg["dbg_xa"].rearrange("(c p) t -> p c t", p=128)[:, :, s * SLAB:(s + 1) * SLAB], xln[:], qS))
            ctx.wait("sync", ve, ctx.cnt(ve))
        barrier()

    if stage < 2:
        finish_zero(nc, ctx, outT, sb)
        return

    with ExitStack() as g2:
        def sb2(name, shape, dt):
            return g2.enter_context(nc.sbuf_tensor(name, shape, dt))

        def ps2(name, shape, dt=F32):
            return g2.enter_context(nc.psum_tensor(name, shape, dt))

        payload = sb2("payload", [128, G, ROWL], BF16)
        buckets = sb2("buckets", [128, G, NH], F32)
        xat = sb2("xat", [128, 8, 2 * 128], BF16)
        qkTt = sb2("qkTt", [128, 2 * 128], BF16)
        u1 = sb2("u1", [128, G, 264], BF16)
        rn_all = sb2("rn_all", [128, G], F32)
        rn2 = sb2("rn2", [128, G], F32)
        rn3 = sb2("rn3", [128, G], F32)
        amax_all = sb2("amax_all", [128, 5 * G * NH], F32)
        aidx_all = sb2("aidx_all", [128, 2 * G * NH], U32)
        acond_all = sb2("acond_all", [128, G * NH], U32)
        onehot = sb2("onehot", [128, G * NB], BF16)
        ttrs = sb2("ttrs", [128, G * NB], F32)
        crow = sb2("crow", [1, G * NB], F32)
        ctT = sb2("ctT", [32, NB], F32)
        srow = sb2("srow", [1, 4 * NB], F32)
        ccs_row = sb2("ccs_row", [1, G * NB], F32)
        posf = sb2("posf", [128, G], F32)
        posfw = sb2("posfw", [128, G, NH], U32)
        posbo = sb2("posbo", [128, G, NH], U32)
        qkT_s = sb2("qkT_s", [128, 4224], BF16)
        knT_s = sb2("knT_s", [128, 4224], BF16)
        vt = sb2("vt", [128, 33, 132], BF16)
        vt2 = sb2("vt2", [128, 32, 132], BF16)
        PT = sb2("PT", [128, 512], BF16)
        PT2 = sb2("PT2", [128, 512], BF16)
        qidr = sb2("qidr", [1, 128], BF16)
        bmask = sb2("bmask", [128, 64], BF16)
        bm1 = sb2("bm1", [128, 64], F32)
        bm2 = sb2("bm2", [128, 64], F32)
        staging = sb2("staging", [128, G, 132], F32)
        acc = sb2("acc", [128, G, 132], F32)
        rcp = sb2("rcp", [128, G], F32)
        PA = ps2("pa", [128, 2048])
        PB = ps2("pb", [128, 1024])
        PC = ps2("pc", [128, 512])
        PD = ps2("pd", [128, 256], BF16)

        vz = ctx.op("vector", lambda e: e.memset(staging[:, :, 129:132], 0.0), inc=ve)
        ctx.op("vector", lambda e: e.memset(payload[:, :, IDLO + 1:ROWL], 0.0), inc=ve)
        ctx.op("vector", lambda e: e.memset(payload[:, :, ONEOFF:ONEOFF + 1], 1.0), inc=ve)
        ctx.op("vector", lambda e: e.tensor_copy(payload[:, :, IDHI:IDHI + 1],
                                                 cst["idhi"][:].rearrange("p (g o) -> p g o", o=1)), inc=ve)
        vlast = ctx.op("vector", lambda e: e.tensor_copy(payload[:, :, IDLO:IDLO + 1],
                                                         cst["idlo"][:].rearrange("p (g o) -> p g o", o=1)), inc=ve)

        def build_head_payload(hd):
            xaT_t = xaT_h.rearrange("(c p) t -> p c t", p=128)
            war_ve, war_ac, war_tr, war_rot = {}, {}, {}, {}
            for g in range(P1B_TILES):
                dbuf = g % 2
                if g >= 2:
                    ctx.wait("sync", ve, war_rot[g - 2])
                ld = ctx.dma("sync", xat[:, :, dbuf * 128:(dbuf + 1) * 128],
                             xaT_t[:, :, g * 128:(g + 1) * 128], qA)
                ctx.waitt("tensor", ld)
                slot = g % 2
                if g >= 2:
                    ctx.wait("tensor", ve, war_ve[g - 2])
                    ctx.wait("tensor", ac, war_ac[g - 2])
                for fc in range(8):
                    ctx.op("tensor", lambda e, fc=fc, slot=slot, dbuf=dbuf, hd=hd: e.matmul(
                        PB[:, slot * 256:slot * 256 + 128],
                        xat[:, fc, dbuf * 128:(dbuf + 1) * 128],
                        wqkv[:, fc, hd * 128:(hd + 1) * 128],
                        start=(fc == 0), stop=(fc == 7)), inc=None)
                for fc in range(8):
                    ctx.op("tensor", lambda e, fc=fc, slot=slot, dbuf=dbuf, hd=hd: e.matmul(
                        PB[:, slot * 256 + 128:slot * 256 + 256],
                        xat[:, fc, dbuf * 128:(dbuf + 1) * 128],
                        wqkv[:, fc, 512 + hd * 128:512 + (hd + 1) * 128],
                        start=(fc == 0), stop=(fc == 7)), inc=pe if fc == 7 else None)
                pq = ctx.cnt(pe)
                ctx.wait("scalar", pe, pq)
                aq = ctx.op("scalar", lambda e, g=g, slot=slot: e.activation(
                    out=payload[:, g, 0:128], in_=PB[:, slot * 256:slot * 256 + 128],
                    func=AF.Copy), inc=ac, psum=True)
                ctx.wait("vector", pe, pq)
                vv = ctx.op("vector", lambda e, g=g, slot=slot: e.tensor_copy(
                    payload[:, g, VOFF:VOFF + 128], PB[:, slot * 256 + 128:slot * 256 + 256]), inc=ve, psum=True)
                tslot = g % 2
                if g >= 2:
                    ctx.wait("tensor", ve, war_tr[g - 2])
                ctx.wait("tensor", ac, aq)
                p_tr = ctx.op("tensor", lambda e, g=g, tslot=tslot: e.transpose(
                    PD[:, tslot * 128:(tslot + 1) * 128], payload[:, g, 0:128], cst["ident128"][:]), inc=pe)
                ctx.wait("vector", pe, p_tr)
                v_tr = ctx.op("vector", lambda e, tslot=tslot: e.tensor_copy(
                    qkTt[:, tslot * 128:(tslot + 1) * 128], PD[:, tslot * 128:(tslot + 1) * 128]), inc=ve, psum=True)
                war_tr[g] = v_tr
                ctx.wait("tensor", ve, v_tr)
                if g >= 2:
                    ctx.wait("tensor", ve, war_rot[g - 2])
                p_rot = ctx.op("tensor", lambda e, tslot=tslot: e.matmul(
                    PA[:, tslot * 256:(tslot + 1) * 256], qkTt[:, tslot * 128:(tslot + 1) * 128],
                    rotb[:], start=True, stop=True), inc=pe)
                ctx.wait("vector", pe, p_rot)
                v_re = ctx.op("vector", lambda e, g=g, tslot=tslot: e.tensor_copy(
                    u1[:, g, 0:256], PA[:, tslot * 256:(tslot + 1) * 256]), inc=ve, psum=True)
                war_rot[g] = v_re
                war_ve[g] = vv
                war_ac[g] = aq
            cA = ctx.cnt(ve)
            ctx.wait("vector", ve, cA)
            ctx.wait("vector", ac, ctx.cnt(ac))
            for half in range(2):
                hs = half * (G // 2)
                v_sq = ctx.op("vector", lambda e, hs=hs: e.tensor_tensor(
                    ttrs[:].rearrange("p (g d) -> p g d", d=128),
                    payload[:, hs:hs + G // 2, 0:128], payload[:, hs:hs + G // 2, 0:128], ALU.mult), inc=ve)
                ctx.wait("vector", ve, v_sq)
                v_ss = ctx.op("vector", lambda e, hs=hs: e.tensor_reduce(
                    out=rn_all[:, hs:hs + G // 2].rearrange("p (g o) -> p g o", o=1),
                    in_=ttrs[:].rearrange("p (g d) -> p g d", d=128),
                    axis=AX.X, op=ALU.add), inc=ve)
                ctx.wait("vector", ve, v_ss)
            v_cl = ctx.op("vector", lambda e: e.tensor_scalar(rn2[:], rn_all[:], 1e-24, None, op0=ALU.max), inc=ve)
            ctx.wait("scalar", ve, v_cl)
            a_ln = ctx.op("scalar", lambda e: e.activation(out=rn_all[:], in_=rn2[:], func=AF.Ln), inc=ac)
            ctx.wait("scalar", ac, a_ln)
            a_ex = ctx.op("scalar", lambda e: e.activation(out=rn3[:], in_=rn_all[:], func=AF.Exp, scale=-0.5), inc=ac)
            ctx.wait("vector", ac, a_ex)
            v_n1 = ctx.op("vector", lambda e: e.tensor_tensor(rn_all[:], rn3[:], rn3[:], ALU.mult), inc=ve)
            ctx.wait("vector", ve, v_n1)
            v_n2 = ctx.op("vector", lambda e: e.tensor_tensor(rn_all[:], rn_all[:], rn2[:], ALU.mult), inc=ve)
            ctx.wait("vector", ve, v_n2)
            v_n3 = ctx.op("vector", lambda e: e.tensor_scalar(rn_all[:], rn_all[:], -0.5, 1.5, op0=ALU.mult, op1=ALU.add), inc=ve)
            ctx.wait("vector", ve, v_n3)
            v_n4 = ctx.op("vector", lambda e: e.tensor_tensor(rn2[:], rn3[:], rn_all[:], ALU.mult), inc=ve)
            ctx.wait("vector", ve, v_n4)
            v_kn = ctx.op("vector", lambda e: e.tensor_tensor(
                payload[:, :, 128:256], payload[:, :, 0:128],
                rn2[:].rearrange("p (g o) -> p g o", o=1).to_broadcast([128, G, 128]), ALU.mult), inc=ve)
            R4 = u1[:, :, 0:256].rearrange("p g (h r) -> p g h r", r=32)
            v1 = ctx.op("vector", lambda e: e.tensor_reduce(
                out=amax_all[:, 0:G * NH].rearrange("p (g h o) -> p g h o", h=NH, o=1),
                in_=R4, axis=AX.X, op=ALU.max), inc=ve)
            v2 = ctx.op("vector", lambda e: e.tensor_reduce(
                out=amax_all[:, G * NH:2 * G * NH].rearrange("p (g h o) -> p g h o", h=NH, o=1),
                in_=R4, axis=AX.X, op=ALU.min), inc=ve)
            ctx.wait("vector", ve, v2)
            for g in range(P1B_TILES):
                ctx.op("vector", lambda e, g=g: e.max_index(
                    aidx_all[:, g * NH:(g + 1) * NH], amax_all[:, g * NH:(g + 1) * NH],
                    u1[:, g, 0:256]), inc=ve)
                ctx.op("vector", lambda e, g=g: e.max_index(
                    aidx_all[:, G * NH + g * NH:G * NH + (g + 1) * NH],
                    amax_all[:, G * NH + g * NH:G * NH + (g + 1) * NH],
                    u1[:, g, 0:256]), inc=ve)
            mi = ctx.cnt(ve)
            ctx.wait("vector", ve, mi)
            vc1 = ctx.op("vector", lambda e: e.tensor_copy(amax_all[:, 2 * G * NH:3 * G * NH], aidx_all[:, 0:G * NH]), inc=ve)
            vc2 = ctx.op("vector", lambda e: e.tensor_copy(amax_all[:, 3 * G * NH:4 * G * NH], aidx_all[:, G * NH:2 * G * NH]), inc=ve)
            v7 = ctx.op("vector", lambda e: e.tensor_tensor(amax_all[:, 4 * G * NH:5 * G * NH],
                                                            amax_all[:, 0:G * NH], amax_all[:, G * NH:2 * G * NH], ALU.add), inc=ve)
            ctx.wait("vector", ve, v7)
            v8 = ctx.op("vector", lambda e: e.tensor_scalar(acond_all[:], amax_all[:, 4 * G * NH:5 * G * NH],
                                                            0.0, None, op0=ALU.is_ge), inc=ve)
            v9 = ctx.op("vector", lambda e: e.tensor_scalar(amax_all[:, 3 * G * NH:4 * G * NH],
                                                            amax_all[:, 3 * G * NH:4 * G * NH], 32.0, None, op0=ALU.add), inc=ve)
            ctx.wait("vector", ve, v9)
            v10 = ctx.op("vector", lambda e: e.tensor_copy(
                buckets[:].rearrange("p g h -> p (g h)"), amax_all[:, 3 * G * NH:4 * G * NH]), inc=ve)
            ctx.wait("vector", ve, v10)
            v11 = ctx.op("vector", lambda e: e.copy_predicated(
                buckets[:].rearrange("p g h -> p (g h)"), acond_all[:], amax_all[:, 2 * G * NH:3 * G * NH]), inc=ve)
            ctx.wait("vector", ve, v11)
            v12 = ctx.op("vector", lambda e: e.tensor_tensor(
                buckets[:], buckets[:],
                cst["offs8"][:].rearrange("p (o h) -> p o h", o=1).to_broadcast([128, G, NH]), ALU.subtract), inc=ve)
            ctx.wait("vector", ve, v12)
            ctx.op("vector", lambda e: e.tensor_scalar(
                buckets[:].rearrange("p g h -> p (g h)"), buckets[:].rearrange("p g h -> p (g h)"),
                0.0, 63.0, op0=ALU.max, op1=ALU.min), inc=ve)

        def sort_hash(hd, h):
            ctx.wait("vector", pe, ctx.cnt(pe))
            ctx.wait("vector", ve, ctx.cnt(ve))
            ctx.wait("vector", ac, ctx.cnt(ac))
            buck = buckets[:, :, h:h + 1].rearrange("p g o -> p (g o)")
            v_oh = ctx.op("vector", lambda e: e.tensor_tensor(
                onehot[:].rearrange("p (g k) -> p g k", k=NB), cst["iota64"][:].rearrange("p (g k) -> p g k", k=NB),
                buckets[:, :, h:h + 1].to_broadcast([128, G, NB]), ALU.is_equal), inc=ve)
            ctx.wait("tensor", ve, v_oh)
            for i in range(4):
                mm = ctx.op("tensor", lambda e, i=i: e.matmul(
                    PA[0:1, i * 512:(i + 1) * 512], cst["ones128b"][:], onehot[:, i * 512:(i + 1) * 512],
                    start=True, stop=True), inc=pe if i == 3 else None)
            ctx.wait("vector", pe, ctx.cnt(pe))
            v_cr = ctx.op("vector", lambda e: e.tensor_copy(crow[:], PA[0:1, :]), inc=ve, psum=True)
            ctx.wait("sync", ve, v_cr)
            d1 = ctx.dma("sync", sort_scr1.rearrange("(o n) -> o n", o=1), crow[:], "qDg")
            ctx.waitt("sync", d1)
            d2 = ctx.dma("sync", ctT[:], sort_scr1.rearrange("(g k) -> g k", g=G), "qDg")
            ctx.waitt("tensor", d2)
            p_cc = ctx.op("tensor", lambda e: e.matmul(PB[0:32, 0:NB], cst["slt32"][:], ctT[:],
                                                       start=True, stop=False), inc=pe)
            p_tot = ctx.op("tensor", lambda e: e.matmul(PB[0:1, 512:512 + NB], cst["ones32f"][:], ctT[:],
                                                        start=True, stop=True), inc=pe)
            ctx.wait("vector", pe, p_tot)
            v_tr = ctx.op("vector", lambda e: e.tensor_copy(srow[:, 0:NB], PB[0:1, 512:512 + NB]), inc=ve, psum=True)
            ctx.wait("vector", ve, v_tr)
            v_sc = ctx.op("vector", lambda e: e.tensor_tensor_scan(
                srow[:, NB:2 * NB], srow[:, 0:NB], cst["zeros1x64"][:], 0.0, ALU.add, ALU.add), inc=ve)
            ctx.wait("vector", ve, v_sc)
            v_st = ctx.op("vector", lambda e: e.tensor_tensor(
                srow[:, 2 * NB:3 * NB], srow[:, NB:2 * NB], srow[:, 0:NB], ALU.subtract), inc=ve)
            ctx.wait("tensor", ve, v_st)
            p_add = ctx.op("tensor", lambda e: e.matmul(PB[0:32, 0:NB], cst["ones1x128f"][:, 0:32],
                                                        srow[0:1, 2 * NB:3 * NB], start=False, stop=True), inc=pe)
            ctx.wait("vector", pe, p_add)
            v_ccs = ctx.op("vector", lambda e: e.tensor_copy(ctT[:], PB[0:32, 0:NB]), inc=ve, psum=True)
            ctx.wait("sync", ve, v_ccs)
            d3 = ctx.dma("sync", sort_scr2.rearrange("(g k) -> g k", g=G), ctT[:], "qDg")
            ctx.waitt("sync", d3)
            d4 = ctx.dma("sync", ccs_row[:], sort_scr2.rearrange("(o n) -> o n", o=1), "qDg")
            ctx.waitt("tensor", d4)
            for i in range(4):
                ctx.op("tensor", lambda e, i=i: e.matmul(
                    PA[:, i * 512:(i + 1) * 512], cst["ones1x128f"][:], ccs_row[:, i * 512:(i + 1) * 512],
                    start=True, stop=False), inc=None)
            for i in range(4):
                ctx.op("tensor", lambda e, i=i: e.matmul(
                    PA[:, i * 512:(i + 1) * 512], cst["slt128"][:], onehot[:, i * 512:(i + 1) * 512],
                    start=False, stop=True), inc=pe if i == 3 else None)
            p_r = ctx.cnt(pe)
            ctx.wait("vector", pe, p_r)
            v_mu = ctx.op("vector", lambda e: e.tensor_tensor(ttrs[:], onehot[:], PA[:], ALU.mult), inc=ve, psum=True)
            ctx.wait("vector", ve, v_mu)
            v_ps = ctx.op("vector", lambda e: e.tensor_reduce(
                out=posf[:].rearrange("p (g o) -> p g o", o=1),
                in_=ttrs[:].rearrange("p (g k) -> p g k", k=NB), axis=AX.X, op=ALU.add), inc=ve)
            ctx.wait("vector", ve, v_ps)
            v_pf = ctx.op("vector", lambda e, h=h: e.tensor_scalar(
                posfw[:, :, h], posf[:], float(64 + h * T), None, op0=ALU.add), inc=ve)
            v_pb = ctx.op("vector", lambda e, h=h: e.tensor_scalar(
                posbo[:, :, h], posf[:], 0.0, None, op0=ALU.add), inc=ve)
            return ctx.cnt(ve)

        def scatter_hash(hd, h):
            for g in range(G):
                ctx.op("gpsimd", lambda e, g=g, h=h: e.indirect_dma_start(
                    out=sbuf_h[:],
                    out_offset=bass.IndirectOffsetOnAxis(posfw[:, g, h:h + 1], 0),
                    in_=payload[:, g, :], in_offset=None), inc=qG, n=16)

        beq = sb2("beq", [128, 128], F32)
        kid = sb2("kid", [128, 2], F32)
        attend_state = {"acc_ve": 0, "bo_q": None, "bo_q1": None, "wrap_q": None,
                        "pend": None, "prev_gq": 0, "pe_mid": None}

        def flush_acc():
            if attend_state["pend"] is not None:
                pgq, pbuf = attend_state["pend"]
                ctx.wait("vector", qG, pgq)
                v_acc = ctx.op("vector", lambda e, pbuf=pbuf: e.tensor_tensor(
                    acc[:], acc[:], u1[:, :, pbuf:pbuf + 132], ALU.add), inc=ve)
                attend_state["acc_ve"] = v_acc
                attend_state["pend"] = None

        def emit_attend(hd, h, scq):
            base = h * T
            pe_mid = attend_state["pe_mid"]
            pe_full = ctx.cnt(pe)
            if pe_mid is None:
                pe_mid = pe_full
            ctx.wait("sync", pe, pe_mid)
            ctx.wait("sync", qG, scq)
            ctx.wait("scalar", pe, pe_mid)
            ctx.wait("scalar", qG, scq)
            if h == 0:
                ctx.waitt("sync", attend_state["wrap_q"])
                ctx.waitt("scalar", attend_state["wrap_q"])
            r1 = ctx.dma("sync", qkT_s[:, 0:2176], sbuf_h[base:base + 2176, 0:128], qL, transpose=True)
            r2 = ctx.dma("sync", knT_s[:, 0:2176], sbuf_h[base:base + 2176, 128:256], qL, transpose=True)
            r4 = ctx.dma("sync", qidr[0:1, 0:64],
                         sbuf_h[base + 64:base + 128, IDHI:IDHI + 1].rearrange("t c -> c t"), qL,
                         allow_slow_non_contiguous=True)
            r5 = ctx.dma("sync", qidr[0:1, 64:128],
                         sbuf_h[base + 64:base + 128, IDLO:IDLO + 1].rearrange("t c -> c t"), qL,
                         allow_slow_non_contiguous=True)
            r3 = ctx.dma("scalar", vt[:, 0:17, :],
                         sbuf_h[base:base + 4224, :].rearrange("(b p) r -> p b r", p=128)[:, 0:17, VOFF:VOFF + 132], qA)
            r3b = ctx.dma("scalar", vt2[:, 0:16, :],
                          sbuf_h[base + 64:base + 64 + 4096, :].rearrange("(b p) r -> p b r", p=128)[:, 0:16, VOFF:VOFF + 132], qA)
            ctx.wait("sync", pe, pe_full)
            ctx.wait("scalar", pe, pe_full)
            r1b = ctx.dma("sync", qkT_s[:, 2176:4224], sbuf_h[base + 2176:base + 4224, 0:128], qL, transpose=True)
            r2b = ctx.dma("sync", knT_s[:, 2176:4224], sbuf_h[base + 2176:base + 4224, 128:256], qL, transpose=True)
            r3c = ctx.dma("scalar", vt[:, 17:33, :],
                          sbuf_h[base:base + 4224, :].rearrange("(b p) r -> p b r", p=128)[:, 17:33, VOFF:VOFF + 132], qA)
            r3d = ctx.dma("scalar", vt2[:, 16:32, :],
                          sbuf_h[base + 64:base + 64 + 4096, :].rearrange("(b p) r -> p b r", p=128)[:, 16:32, VOFF:VOFF + 132], qA)
            if ATT_PARTS < 2:
                return
            ctx.waitt("tensor", r4)
            ctx.waitt("tensor", r5)
            p_qh = ctx.op("tensor", lambda e: e.matmul(PB[:, 896:960], cst["ones1x128b"][:],
                                                       qidr[0:1, 0:64], start=True, stop=True), inc=pe)
            p_ql = ctx.op("tensor", lambda e: e.matmul(PB[:, 960:1024], cst["ones1x128b"][:],
                                                       qidr[0:1, 64:128], start=True, stop=True), inc=pe)
            ctx.waitt("vector", r3)
            vk1 = ctx.op("vector", lambda e: e.tensor_copy(kid[:, 0:1], vt[:, 0, 129:130]), inc=ve)
            vk2 = ctx.op("vector", lambda e: e.tensor_copy(kid[:, 1:2], vt[:, 0, 130:131]), inc=ve)
            ctx.wait("vector", pe, p_ql)
            ctx.wait("vector", ve, vk2)
            b1 = ctx.op("vector", lambda e: e.tensor_scalar(beq[:, 0:64], PB[:, 896:960],
                                                            kid[:, 0:1], None, op0=ALU.is_equal), inc=ve, psum=True)
            b2 = ctx.op("vector", lambda e: e.tensor_scalar(beq[:, 64:128], PB[:, 960:1024],
                                                            kid[:, 1:2], None, op0=ALU.is_equal), inc=ve, psum=True)
            ctx.wait("vector", ve, b2)
            b3 = ctx.op("vector", lambda e: e.tensor_tensor(beq[:, 0:64], beq[:, 0:64],
                                                            beq[:, 64:128], ALU.mult), inc=ve)
            ctx.wait("vector", ve, b3)
            b4 = ctx.op("vector", lambda e: e.tensor_scalar(bmask[:], beq[:, 0:64],
                                                            -1.0, 1.0, op0=ALU.mult, op1=ALU.add), inc=ve)
            if ATT_PARTS < 3:
                return
            evac_of = {}
            exp_of = {}
            mask_of = {}

            def dots_batch(j8):
                bank = PC if j8 % 2 == 0 else PB
                if j8 >= 2:
                    ctx.wait("tensor", ac, exp_of[j8 - 2])
                for cc in range(8):
                    c = 8 * j8 + cc
                    ctx.op("tensor", lambda e, c=c, cc=cc, bank=bank: e.matmul(
                        bank[:, cc * 64:(cc + 1) * 64],
                        knT_s[:, 64 * c:64 * c + 128],
                        qkT_s[:, 64 * c + 64:64 * c + 128],
                        start=True, stop=True), inc=pe if cc == 7 else None)
                return ctx.cnt(pe)

            def exp_mask(j8, last_dots):
                bank = PC if j8 % 2 == 0 else PB
                pt = PT if j8 % 2 == 0 else PT2
                ctx.wait("scalar", pe, last_dots)
                if j8 >= 2:
                    ctx.wait("scalar", ve, mask_of[j8 - 2])
                a_exp = ctx.op("scalar", lambda e, bank=bank, pt=pt: e.activation(
                    out=pt[:], in_=bank[:, 0:512], func=AF.Exp, scale=RSQD), inc=ac, psum=True)
                exp_of[j8] = a_exp
                ctx.wait("vector", ac, a_exp)
                v_m = ctx.op("vector", lambda e, pt=pt: e.tensor_tensor(pt[:], pt[:], cst["pmask8"][:], ALU.mult), inc=ve)
                if j8 == 0:
                    ctx.wait("vector", ve, max(v_m, b4))
                    v_m = ctx.op("vector", lambda e, pt=pt: e.tensor_tensor(pt[:, 0:64], pt[:, 0:64],
                                                                            bmask[:], ALU.mult), inc=ve)
                mask_of[j8] = v_m
                return v_m

            def pv_batch(j8):
                pt = PT if j8 % 2 == 0 else PT2
                ctx.wait("tensor", ve, mask_of[j8])
                if j8 == 0:
                    ctx.waitt("tensor", r3)
                    ctx.waitt("tensor", r3b)
                if j8 == 4:
                    ctx.waitt("tensor", r3c)
                    ctx.waitt("tensor", r3d)
                    if attend_state["bo_q"] is not None:
                        ctx.waitt("vector", attend_state["bo_q"])
                        ctx.waitt("vector", attend_state["bo_q1"])
                for cc in range(8):
                    c = 8 * j8 + cc
                    m = c // 2
                    slot = [0, 512, 1024, 1536][m % 4]
                    if c % 2 == 0:
                        if m >= 4 and (m - 4) in evac_of:
                            ctx.wait("tensor", ve, evac_of[m - 4])
                        ctx.op("tensor", lambda e, c=c, cc=cc, slot=slot, pt=pt: e.matmul(
                            PA[64:128, slot:slot + 129],
                            pt[:, cc * 64:(cc + 1) * 64],
                            vt[:, c // 2, 0:129],
                            start=True, stop=True, tile_position=(0, 64)), inc=None)
                    else:
                        pv = ctx.op("tensor", lambda e, c=c, cc=cc, slot=slot, pt=pt: e.matmul(
                            PA[0:64, slot:slot + 129],
                            pt[:, cc * 64:(cc + 1) * 64],
                            vt2[:, (c - 1) // 2, 0:129],
                            start=True, stop=True, tile_position=(0, 0)), inc=pe)
                        ctx.wait("vector", pe, pv)
                        evac_of[m] = ctx.op("vector", lambda e, m=m, slot=slot: e.tensor_copy(
                            staging[:, m, 0:129], PA[:, slot:slot + 129]), inc=ve, psum=True)

            ctx.waitt("tensor", r1)
            ctx.waitt("tensor", r2)
            ld_prev = dots_batch(0)
            for j8 in range(8):
                ld_cur = ld_prev
                if j8 < 7:
                    if j8 + 1 == 4:
                        ctx.waitt("tensor", r1b)
                        ctx.waitt("tensor", r2b)
                    ld_prev = dots_batch(j8 + 1)
                exp_mask(j8, ld_cur)
                pv_batch(j8)
                if j8 == 4:
                    attend_state["pe_mid"] = ctx.cnt(pe)
            flush_acc()
            stg_ve = ctx.cnt(ve)
            ctx.wait("sync", ve, stg_ve)
            ctx.wait("sync", qG, attend_state["prev_gq"])
            boT_ap = bo_h.rearrange("(g p) r -> p g r", p=128)
            w_bo1 = ctx.dma("sync", boT_ap[0:64, :, :], staging[64:128, :, :], qS)
            w_bo = ctx.dma("sync", boT_ap[64:128, :, :], staging[0:64, :, :], qS)
            attend_state["bo_q"] = w_bo
            attend_state["bo_q1"] = w_bo1
            if ATT_PARTS < 6:
                return
            ctx.waitt("gpsimd", w_bo1)
            ctx.waitt("gpsimd", w_bo)
            ctx.wait("gpsimd", ve, attend_state["acc_ve"])
            gbuf = 0 if h % 2 == 0 else 132
            for g in range(G):
                ctx.op("gpsimd", lambda e, g=g, h=h, gbuf=gbuf: e.indirect_dma_start(
                    out=u1[:, g, gbuf:gbuf + 132], out_offset=None,
                    in_=bo_h[:],
                    in_offset=bass.IndirectOffsetOnAxis(posbo[:, g, h:h + 1], 0)), inc=qG, n=16)
            gq = ctx.cnt(qG)
            attend_state["prev_gq"] = gq
            attend_state["pend"] = (gq, gbuf)

        NHD = NHEADS_RUN if stage >= 5 else min(NHEADS_RUN, 1) if stage < 4 else NHEADS_RUN
        for hd in range(NHD):
            barrier(("vector", "scalar", "tensor", "gpsimd", "sync"))
            build_head_payload(hd)
            if hd == 0 and "dbg_payload" in dbg:
                pd = ctx.cnt(ve)
                ctx.wait("sync", ve, pd)
                ctx.wait("sync", ac, ctx.cnt(ac))
                ctx.dma("sync", dbg["dbg_payload"][:], payload[:].rearrange("p g r -> p (g r)"), qS)
            if hd == 0 and "dbg_buckets" in dbg:
                pd = ctx.cnt(ve)
                ctx.wait("sync", ve, pd)
                ctx.dma("sync", dbg["dbg_buckets"][:], buckets[:].rearrange("p g h -> p (g h)"), qS)
            if stage < 3:
                continue
            pay_done_ve = ctx.cnt(ve)
            pay_done_ac = ctx.cnt(ac)
            ctx.wait("gpsimd", ve, pay_done_ve)
            ctx.wait("gpsimd", ac, pay_done_ac)
            scq = []
            for h in range(NH):
                sv = sort_hash(hd, h)
                ctx.wait("gpsimd", ve, sv)
                scatter_hash(hd, h)
                scq.append(ctx.cnt(qG))
            if hd == 0 and "dbg_pos" in dbg:
                pd = ctx.cnt(ve)
                ctx.wait("sync", ve, pd)
                ctx.dma("sync", dbg["dbg_pos"][:], posbo[:].rearrange("p g h -> p (g h)"), qS)
            sc_done = ctx.cnt(qG)
            if stage < 4:
                continue
            ctx.wait("tensor", ve, ctx.cnt(ve))
            ctx.wait("tensor", ac, ctx.cnt(ac))
            vz2 = ctx.op("vector", lambda e: e.memset(acc[:].rearrange("p g r -> p (g r)"), 0.0), inc=ve)
            attend_state["acc_ve"] = vz2
            attend_state["pe_mid"] = None
            order = list(range(1, ATT_HASHES)) + [0]
            for h in order:
                if h == 0:
                    ctx.wait("sync", qG, sc_done)
                    wr = ctx.dma("sync", sbuf_h[0:64, :], sbuf_h[7 * T + 4096:7 * T + 4160, :], qS)
                    attend_state["wrap_q"] = wr
                emit_attend(hd, h, sc_done if h == 0 else scq[h])
            flush_acc()
            ad = ctx.cnt(ve)
            ctx.wait("vector", ve, ad)
            v_rc = ctx.op("vector", lambda e: e.reciprocal(rcp[:], acc[:, :, 128].rearrange("p g -> p g")), inc=ve)
            ctx.wait("vector", ve, v_rc)
            v_at = ctx.op("vector", lambda e: e.tensor_tensor(
                u1[:, :, 132:260], acc[:, :, 0:128],
                rcp[:].rearrange("p (g o) -> p g o", o=1).to_broadcast([128, G, DH]), ALU.mult), inc=ve)
            ctx.wait("sync", ve, v_at)
            ctx.dma("sync", attn_h[hd, :, :].rearrange("(g p) d -> p g d", p=128), u1[:, :, 132:260], qS)
            if hd == 0 and "dbg_attn" in dbg:
                ctx.dma("sync", dbg["dbg_attn"][:, 0:G * DH].rearrange("p (g d) -> p g d", d=DH), u1[:, :, 132:260], qS)


    barrier()
    if stage < 5:
        finish_zero(nc, ctx, outT, sb)
        return

    qL, qA, qG, pe, ve, ac, qS = "qL", "qA", "qG", "pe", "ve", "ac", "qS"
    with ExitStack() as g3:
        def sb3(name, shape, dt):
            return g3.enter_context(nc.sbuf_tensor(name, shape, dt))

        def ps3(name, shape, dt=F32):
            return g3.enter_context(nc.psum_tensor(name, shape, dt))

        wout_own = sb3("wout_own", [128, 4, 1024], BF16)
        wout_sib = sb3("wout_sib", [128, 4, 1024], BF16)
        wp3f = sb3("wp3f", [128, 4, 1024], F32)
        att_all = sb3("att_all", [128, 16, 8, 128], BF16)
        attnT = sb3("attnT", [128, 8, 512], BF16)
        x2slab = sb3("x2slab", [128, 8, 512], F32)
        sq3 = sb3("sq3", [128, 8, 512], F32)
        xln3 = sb3("xln3", [128, 8, 512], BF16)
        evac3 = sb3("evac3", [128, 8, 512], F32)
        rowstat3 = sb3("rowstat3", [1, 4 * 512], F32)
        outslab = sb3("outslab", [128, 8, 512], F32)
        PS_STAT3 = ps3("ps_stat3", [1, 1024])
        PS_REPa = ps3("ps_repa", [128, 512])
        PS_REPb = ps3("ps_repb", [128, 512])
        PS_F3 = ps3("ps_f3", [128, 1024])
        PS_T3 = ps3("ps_t3", [128, 512], BF16)

        def dma_branchy(eng, out_even, in_even, out_odd, in_odd, role):
            idx = ctx._roles.get(role, 0)
            ctx._roles[role] = (idx + 1) % ctx.NROT
            name = "%s_%d" % (role, idx)
            h = ctx.sem(name)
            prev = ctx._semcnt[name]
            if prev:
                ctx.emit(eng, lambda e, h=h, v=prev: e.wait_ge(h, v))
            v = ctx.bump(name, 16)

            def go(e, oe=out_even, ie=in_even, oo=out_odd, io_=in_odd, h=h):
                pr = ctx.parity_reg(eng)["reg"]
                with e.If_eq(pr, 0):
                    e.dma_start(out=oe, in_=ie).then_inc(h, 16)
                with e.Else():
                    e.dma_start(out=oo, in_=io_).then_inc(h, 16)
            ctx.emit(eng, go)
            return (name, v)

        if "dbg_x1h" in dbg:
            ctx.dma("sync", dbg["dbg_x1h"][:].rearrange("(c p) t -> p c t", p=128),
                    x1T_h.rearrange("(c p) t -> p c t", p=128)[:, :, 0:512], qS)
        lw1 = ctx.dma("sync", wp3f[:], ap["woutT_own"].rearrange("(c p) o -> p c o", p=128), qL)
        ctx.waitt("vector", lw1)
        cw1 = ctx.op("vector", lambda e: e.tensor_copy(wout_own[:], wp3f[:]), inc=ve)
        ctx.wait("sync", ve, cw1)
        lw2 = dma_branchy("gpsimd",
                          wp3f[:], ap["woutT_sibB"].rearrange("(c p) o -> p c o", p=128),
                          wp3f[:], ap["woutT_sibA"].rearrange("(c p) o -> p c o", p=128), qL)
        ctx.waitt("vector", lw2)
        cw2 = ctx.op("vector", lambda e: e.tensor_copy(wout_sib[:], wp3f[:]), inc=ve)

        HT = T // 2
        if P3_PARTS < 1:
            barrier()
            finish_zero(nc, ctx, outT, sb3)
            return
        cc_toks = []
        for hd in range(NHEADS_CORE):
            cc_toks.append(dma_branchy(
                "gpsimd",
                ccin_h[:, hd * 128:(hd + 1) * 128], attn_h[hd, HT:T, :],
                ccin_h[:, hd * 128:(hd + 1) * 128], attn_h[hd, 0:HT, :], qG + "c"))
        for t_ in cc_toks:
            ctx.waitt("gpsimd", t_)
        ccsem = ctx.sem("cc")
        groups = [[0, 1], [2, 3], [4, 5], [6, 7]]
        ctx.emit("gpsimd", lambda e: e.collective_compute(
            "AllGather", ALU.bypass, groups, [ccin_h.opt()], [ccout_h.opt()]).then_inc(ccsem, 1))
        ctx.bump("cc", 1)
        ctx.wait("gpsimd", "cc", 1)

        if P3_PARTS < 2:
            barrier()
            finish_zero(nc, ctx, outT, sb3)
            return
        at_toks = []
        for hd in range(NHEADS_CORE):
            at_toks.append(dma_branchy(
                "gpsimd",
                att_all[:, :, hd, :], attn_h[hd, 0:HT, :].rearrange("(tg p) d -> p tg d", p=128),
                att_all[:, :, hd, :], attn_h[hd, HT:T, :].rearrange("(tg p) d -> p tg d", p=128), qL))
        for hd in range(NHEADS_CORE):
            at_toks.append(dma_branchy(
                "gpsimd",
                att_all[:, :, 4 + hd, :],
                ccout_h[HT:T, hd * 128:(hd + 1) * 128].rearrange("(tg p) d -> p tg d", p=128),
                att_all[:, :, 4 + hd, :],
                ccout_h[0:HT, hd * 128:(hd + 1) * 128].rearrange("(tg p) d -> p tg d", p=128), qL))
        for t_ in at_toks:
            ctx.waitt("tensor", t_)

        if P3_PARTS < 3:
            barrier()
            finish_zero(nc, ctx, outT, sb3)
            return
        x1v = x1T_h.rearrange("(c p) t -> p c t", p=128)

        def ln_rows3(src_slab, slab_cols):
            c0 = ctx.cnt(ve)
            ctx.wait("scalar", ve, c0)
            ctx.wait("tensor", ve, c0)
            a_sq = ctx.op("scalar", lambda e: e.activation(out=sq3[:, :, 0:slab_cols], in_=src_slab,
                                                           func=AF.Square), inc=ac)
            for fc in range(8):
                ctx.op("tensor", lambda e, fc=fc: e.matmul(
                    PS_STAT3[0:1, 0:slab_cols], cst["ones128f"][:], src_slab[:, fc, 0:slab_cols],
                    start=(fc == 0), stop=(fc == 7)), inc=pe if fc == 7 else None)
            ctx.wait("tensor", ac, a_sq)
            for fc in range(8):
                ctx.op("tensor", lambda e, fc=fc: e.matmul(
                    PS_STAT3[0:1, 512:512 + slab_cols], cst["ones128f"][:], sq3[:, fc, 0:slab_cols],
                    start=(fc == 0), stop=(fc == 7)), inc=pe if fc == 7 else None)
            p_done = ctx.cnt(pe)
            ctx.wait("vector", pe, p_done)
            R = rowstat3[:].rearrange("a (k n) -> a k n", n=512)
            v1 = ctx.op("vector", lambda e: e.tensor_scalar(R[:, 0, 0:slab_cols], PS_STAT3[0:1, 0:slab_cols],
                                                            1.0 / E, None, op0=ALU.mult), inc=ve, psum=True)
            v2 = ctx.op("vector", lambda e: e.tensor_scalar(R[:, 1, 0:slab_cols], PS_STAT3[0:1, 512:512 + slab_cols],
                                                            1.0 / E, None, op0=ALU.mult), inc=ve, psum=True)
            ctx.wait("vector", ve, v2)
            v3 = ctx.op("vector", lambda e: e.tensor_tensor(R[:, 2, 0:slab_cols], R[:, 0, 0:slab_cols],
                                                            R[:, 0, 0:slab_cols], ALU.mult), inc=ve)
            ctx.wait("vector", ve, v3)
            v4 = ctx.op("vector", lambda e: e.tensor_tensor(R[:, 2, 0:slab_cols], R[:, 1, 0:slab_cols],
                                                            R[:, 2, 0:slab_cols], ALU.subtract), inc=ve)
            ctx.wait("scalar", ve, v4)
            a1 = ctx.op("scalar", lambda e: e.activation(out=R[:, 3, 0:slab_cols], in_=R[:, 2, 0:slab_cols],
                                                         func=AF.Ln, bias=cst["epscol"][0:1, 0:1]), inc=ac)
            ctx.wait("scalar", ac, a1)
            a2 = ctx.op("scalar", lambda e: e.activation(out=R[:, 1, 0:slab_cols], in_=R[:, 3, 0:slab_cols],
                                                         func=AF.Exp, scale=-0.5), inc=ac)
            ctx.wait("vector", ac, a2)
            v5 = ctx.op("vector", lambda e: e.tensor_tensor(R[:, 0, 0:slab_cols], R[:, 0, 0:slab_cols],
                                                            R[:, 1, 0:slab_cols], ALU.mult), inc=ve)
            ctx.wait("tensor", ve, v5)
            ctx.op("tensor", lambda e: e.matmul(PS_REPa[:, 0:slab_cols], cst["ones1x128f"][:],
                                                R[0:1, 1, 0:slab_cols], start=True, stop=True), inc=pe)
            pb = ctx.op("tensor", lambda e: e.matmul(PS_REPb[:, 0:slab_cols], cst["ones1x128f"][:],
                                                     R[0:1, 0, 0:slab_cols], start=True, stop=True), inc=pe)
            return pb

        SLAB3 = 512
        prev_wr = []
        for sidx in range(HT // SLAB3):
            c0 = ctx.cnt(ve)
            ctx.wait("tensor", ve, c0)
            ctx.wait("tensor", ac, ctx.cnt(ac))
            tr_evac = {}
            for idx3 in range(32):
                hdc, tt = idx3 // 4, idx3 % 4
                tg = sidx * 4 + tt
                tslot3 = idx3 % 4
                if idx3 >= 1:
                    ctx.wait("tensor", ve, tr_evac[idx3 - 1])
                ctx.op("tensor", lambda e, hdc=hdc, tg=tg, tslot3=tslot3: e.transpose(
                    PS_T3[:, tslot3 * 128:(tslot3 + 1) * 128],
                    att_all[:, tg, hdc, :], cst["ident128"][:]), inc=pe)
                p_tr = ctx.cnt(pe)
                if P3_PARTS == 31:
                    tr_evac[idx3] = ctx.op("vector", lambda e: e.memset(attnT[:, 0, 0:4], 0.0), inc=ve)
                    continue
                ctx.wait("vector", pe, p_tr)
                tr_evac[idx3] = ctx.op("vector", lambda e, hdc=hdc, tt=tt, tslot3=tslot3: e.tensor_copy(
                    attnT[:, hdc, tt * 128:(tt + 1) * 128],
                    PS_T3[:, tslot3 * 128:(tslot3 + 1) * 128]), inc=ve, psum=True)
            trd = ctx.cnt(ve)
            ctx.wait("tensor", ve, trd)
            if P3_PARTS < 4:
                continue
            ctx.wait("gpsimd", ve, ctx.cnt(ve))
            ctx.wait("gpsimd", ac, ctx.cnt(ac))
            lx1 = dma_branchy("gpsimd",
                              sq3[:], x1v[:, :, sidx * SLAB3:(sidx + 1) * SLAB3],
                              sq3[:], x1v[:, :, HT + sidx * SLAB3:HT + (sidx + 1) * SLAB3], qA)
            ctx.waitt("vector", lx1)
            if sidx == 0 and "dbg_x1r" in dbg:
                ctx.waitt("sync", lx1)
                ctx.dma("sync", dbg["dbg_x1r"][:].rearrange("p (c t) -> p c t", t=512), sq3[:], qS)
            ev_oc = {}
            for oc in range(8):
                slot = oc % 2
                if oc >= 2:
                    ctx.wait("tensor", ve, ev_oc[oc - 2])
                for hdc in range(8):
                    w = wout_own if hdc < 4 else wout_sib
                    ctx.op("tensor", lambda e, oc=oc, hdc=hdc, slot=slot, w=w: e.matmul(
                        PS_F3[:, slot * 512:slot * 512 + SLAB3],
                        w[:, hdc % 4, oc * 128:(oc + 1) * 128],
                        attnT[:, hdc, :],
                        start=(hdc == 0), stop=(hdc == 7)), inc=pe if hdc == 7 else None)
                pdone = ctx.cnt(pe)
                ctx.wait("vector", pe, pdone)
                if oc == 0:
                    ctx.wait("vector", ac, ctx.cnt(ac))
                e1 = ctx.op("vector", lambda e, oc=oc, slot=slot: e.tensor_scalar(
                    evac3[:, oc, :], PS_F3[:, slot * 512:slot * 512 + SLAB3],
                    boutc[:, oc:oc + 1], None, op0=ALU.add), inc=ve, psum=True)
                ev_oc[oc] = e1
                ctx.wait("vector", ve, e1)
                ctx.op("vector", lambda e, oc=oc: e.tensor_tensor(
                    x2slab[:, oc, :], sq3[:, oc, :], evac3[:, oc, :], ALU.add), inc=ve)
            if P3_PARTS < 5:
                continue
            if sidx == 0 and "dbg_x2" in dbg:
                dd = ctx.cnt(ve)
                ctx.wait("sync", ve, dd)
                ctx.dma("sync", dbg["dbg_x2"][:].rearrange("p (c t) -> p c t", t=512), x2slab[:], qS)
                ctx.dma("sync", dbg["dbg_attT"][:].rearrange("p (c t) -> p c t", t=512), attnT[:], qS)
            pb3 = ln_rows3(x2slab[:], SLAB3)
            ctx.wait("vector", pe, pb3)
            vA3 = ctx.op("vector", lambda e: e.tensor_tensor(
                sq3[:], x2slab[:],
                PS_REPa[:, 0:SLAB3].rearrange("p (c n) -> p c n", c=1).to_broadcast([128, 8, SLAB3]),
                ALU.mult), inc=ve, psum=True)
            ctx.wait("vector", ve, vA3)
            vB3 = ctx.op("vector", lambda e: e.tensor_tensor(
                xln3[:], sq3[:],
                PS_REPb[:, 0:SLAB3].rearrange("p (c n) -> p c n", c=1).to_broadcast([128, 8, SLAB3]),
                ALU.subtract), inc=ve, psum=True)
            ctx.wait("tensor", ve, vB3)
            ev2 = {}
            for oc in range(8):
                slot = oc % 2
                if oc >= 2:
                    ctx.wait("tensor", ve, ev2[oc - 2])
                for fc in range(8):
                    ctx.op("tensor", lambda e, fc=fc, oc=oc, slot=slot: e.matmul(
                        PS_F3[:, slot * 512:slot * 512 + SLAB3],
                        tern[:, fc, oc * 128:(oc + 1) * 128], xln3[:, fc, :],
                        start=(fc == 0), stop=(fc == 7)), inc=pe if fc == 7 else None)
                pdone = ctx.cnt(pe)
                ctx.wait("vector", pe, pdone)
                e2 = ctx.op("vector", lambda e, oc=oc, slot=slot: e.tensor_scalar(
                    evac3[:, oc, :], PS_F3[:, slot * 512:slot * 512 + SLAB3],
                    halfalpha[:, 0:1], halfbias[:, oc:oc + 1], op0=ALU.mult, op1=ALU.add), inc=ve, psum=True)
                ev2[oc] = e2
                ctx.wait("vector", ve, e2)
                ctx.op("vector", lambda e, oc=oc: e.tensor_tensor(
                    outslab[:, oc, :], x2slab[:, oc, :], evac3[:, oc, :], ALU.add), inc=ve)
            odone = ctx.cnt(ve)
            ctx.wait("sync", ve, odone)
            for tok in prev_wr:
                ctx.waitt("vector", tok)
            prev_wr = [ctx.dma("sync", outT.rearrange("(c p) t -> p c t", p=128)[:, :, sidx * SLAB3:(sidx + 1) * SLAB3],
                               outslab[:], qS)]
        for tok in prev_wr:
            ctx.waitt("sync", tok)
        if P3_PARTS < 5:
            barrier()
            finish_zero(nc, ctx, outT, sb3)
            return
        barrier(("sync",))


def finish_zero(nc, ctx, outT, sb):
    zrow = sb("zrow", [128, 8, 64], F32)
    vz = ctx.op("vector", lambda e: e.memset(zrow[:], 0.0), inc="ve")
    ctx.wait("sync", "ve", vz)
    toks = []
    for i in range(8):
        toks.append(ctx.dma("sync", outT.rearrange("(c p) t -> p c t", p=128)[:, :, i * 64:(i + 1) * 64], zrow[:], "qOut"))
    for t_ in toks:
        ctx.waitt("sync", t_)



def make_core_inputs(inputs, consts, c):
    b = c // 2
    hh = c % 2
    heads = list(range(4 * hh, 4 * hh + 4))
    x = np.asarray(inputs["L0_attn_x"], np.float32)
    w_qk = np.asarray(inputs["L0_attn_w_qk"], np.float32)
    w_v = np.asarray(inputs["L0_attn_w_v"], np.float32)
    w_out = np.asarray(inputs["L0_attn_w_out"], np.float32)
    b_out = np.asarray(inputs["L0_attn_b_out"], np.float32)
    rotations = np.asarray(inputs["L0_attn_rotations"], np.float32)
    fw = np.asarray(inputs["L0_ffn_weight"], np.float32)
    fa = np.asarray(inputs["L0_ffn_alpha"], np.float32)
    fb = np.asarray(inputs["L0_ffn_bias"], np.float32)

    d = dict(consts)
    d["xT"] = np.ascontiguousarray(x[b].T)
    d["wffnT"] = np.ascontiguousarray(fw.T)
    colsel = np.concatenate([np.arange(h * DH, (h + 1) * DH) for h in heads])
    d["wqkvT"] = np.ascontiguousarray(np.concatenate([w_qk.T[:, colsel], w_v.T[:, colsel]], axis=1))
    woutT = np.ascontiguousarray(w_out.T)
    d["woutT_own"] = np.ascontiguousarray(woutT[colsel, :])
    d["woutT_sibA"] = np.ascontiguousarray(woutT[0:512, :])
    d["woutT_sibB"] = np.ascontiguousarray(woutT[512:1024, :])
    d["boutRows"] = np.ascontiguousarray(b_out.reshape(8, 128).T)
    d["fbiasRows"] = np.ascontiguousarray(fb.reshape(8, 128).T)
    d["alpha"] = fa.reshape(1, 1)
    d["rotT"] = np.ascontiguousarray(rotations.reshape(DH, NH * 32))
    return d


def kernel(**inputs):
    consts = host_consts()
    nc = build_program()
    in_maps = [make_core_inputs(inputs, consts, c) for c in range(8)]
    res = run_bass_kernel_spmd(nc, in_maps, core_ids=list(range(8)))
    out = np.zeros((B, T, E), dtype=np.float32)
    for c in range(8):
        b = c // 2
        hh = c % 2
        o = res.results[c]["outT"]
        out[b, hh * (T // 2):(hh + 1) * (T // 2), :] = o.T
    return out
